# version 10
# baseline (speedup 1.0000x reference)
import numpy as np
import ml_dtypes
import concourse.bass as bass
import concourse.mybir as mybir
from concourse.bass_utils import run_bass_kernel_spmd
from contextlib import ExitStack

F32 = mybir.dt.float32
BF16 = mybir.dt.bfloat16
AF = mybir.ActivationFunctionType
ALU = mybir.AluOpType
AX = mybir.AxisListType

S, D, H, Dh = 16384, 2048, 16, 128
NCORE = 8
TC = S // NCORE
BF = ml_dtypes.bfloat16

f32 = np.float32


def _rope_tables():
    pos = np.arange(S, dtype=np.float64)
    inv_freq = 1.0 / 10000.0 ** (np.arange(0, Dh, 2, dtype=np.float64) / Dh)
    freqs = np.repeat(pos[:, None] * inv_freq[None, :], 2, axis=-1)
    cos, sin = np.cos(freqs), np.sin(freqs)
    sbase = (np.arange(0, Dh, 2, dtype=np.float64) + 0.4 * Dh) / (1.4 * Dh)
    power = (pos - S // 2) / 512.0
    scale = np.repeat(sbase[None, :] ** power[:, None], 2, axis=-1)
    sgn = np.tile(np.array([-1.0, 1.0]), Dh // 2)[None, :]
    cosq = (cos * scale).T.astype(f32)
    sinq = (sin * scale * sgn).T.astype(f32)
    cosk = (cos / scale).T.astype(f32)
    sink = (sin / scale * sgn).T.astype(f32)
    return cosq, sinq, cosk, sink


_GAMMA = (1.0 - 2.0 ** (-5.0 - np.arange(H, dtype=np.float64)))


def _decay_tiles(h):
    g = _GAMMA[h]
    qd = np.tile((g ** np.arange(512, dtype=np.float64))[None, :], (128, 1)).astype(f32)
    kd1 = (g ** (-np.arange(128, dtype=np.float64)))[None, :]
    kd = np.tile(np.concatenate([kd1] * 4, axis=1), (128, 1)).astype(f32)
    m = np.arange(-3, 128, dtype=np.float64)
    with np.errstate(under="ignore"):
        ps = np.tile((g ** (128.0 * m))[None, :], (128, 1)).astype(f32)
    return qd, kd, ps


def _diag_masks():
    j = np.arange(128)[:, None]
    i = np.arange(512)[None, :]
    tiles = [(128 * d + j <= i).astype(f32) for d in range(4)]
    return np.concatenate(tiles, axis=1).astype(BF)


def _causal128():
    j = np.arange(128)[:, None]
    i = np.arange(128)[None, :]
    return (j <= i).astype(f32).astype(BF)


def _pack_bias(b):
    return np.ascontiguousarray(b.reshape(-1, 128).T.astype(f32))


def _tile_lhsT(wT, kdim, odim):
    nk, no = kdim // 128, odim // 128
    t = wT.reshape(nk, 128, no, 128).transpose(2, 0, 1, 3)
    return np.ascontiguousarray(t.reshape(no * nk * 128, 128))


class Plan:

    ENGINES = ("tensor", "scalar", "vector", "gpsimd", "sync")
    AUTO = {"scalar": "evS", "vector": "dvS", "gpsimd": "plS"}

    def __init__(self):
        self.ops = {e: [] for e in self.ENGINES}
        self.counts = {}
        self.ns = ""

    def set_ns(self, ns):
        self.ns = ns

    def mark(self, name):
        return (self.ns + name, self.counts[self.ns + name])

    def mark0(self, name):
        n = self.ns + name
        return (n, self.counts.get(n, 0))

    def has(self, name):
        return (self.ns + name) in self.counts

    def cnt(self, name):
        return self.counts[self.ns + name]

    def snapshot(self):
        return {eng: len(self.ops[eng]) for eng in self.ENGINES}

    def merge_from(self, other, since=None, defer_self=None):
        for eng in self.ENGINES:
            s0 = since[eng] if since else 0
            if defer_self and eng in defer_self:
                s0 += defer_self[eng]
            head, a = self.ops[eng][:s0], self.ops[eng][s0:]
            b = other.ops[eng]
            if not b:
                continue
            if not a:
                self.ops[eng] = head + list(b)
                continue
            out = []
            ratio = len(a) / (len(a) + len(b))
            ia = ib = 0
            for k in range(len(a) + len(b)):
                if (ia + 1) / (k + 2) <= ratio and ia < len(a) or ib >= len(b):
                    out.append(a[ia]); ia += 1
                else:
                    out.append(b[ib]); ib += 1
            self.ops[eng] = head + out
        for k, v in other.counts.items():
            assert k not in self.counts, k
            self.counts[k] = v

    def barrier(self, extra=()):
        waits = [(s, c) for s, c in self.counts.items()] + list(extra)
        for eng in self.ENGINES:
            self.ops[eng].append((lambda e: e.nop(), list(waits), None, 1))

    def emit(self, eng, fn, waits=(), inc=None, inc_k=1, _dma=False):
        waits = [w for w in waits if w is not None]
        if inc is not None:
            inc = self.ns + inc
        auto = (self.ns + self.AUTO[eng]) if (eng in self.AUTO and not _dma) \
            else None
        if auto is not None:
            prev = self.counts.get(auto, 0)
            if prev:
                waits = [(auto, prev)] + waits
            if inc is None:
                inc, inc_k = auto, 1
            assert inc == auto and inc_k == 1, (eng, inc)
        self.ops[eng].append((fn, waits, inc, inc_k))
        if inc is not None:
            self.counts[inc] = self.counts.get(inc, 0) + inc_k
            return (inc, self.counts[inc])
        return None

    def dma(self, eng, out, in_, waits=(), inc=None):
        return self.emit(eng, lambda e, o=out, i=in_: e.dma_start(out=o, in_=i),
                         waits=waits, inc=inc, inc_k=16, _dma=True)

    def realize(self, nc, sems):
        for name in self.counts:
            assert name in sems, f"semaphore {name} not allocated"
        with nc.Block() as block:
            for eng in self.ENGINES:
                if not self.ops[eng]:
                    continue

                def make(body, eng=eng):
                    def _(e):
                        for fn, waits, inc, inc_k in body:
                            for sname, cnt in waits:
                                e.wait_ge(sems[sname], cnt)
                            inst = fn(e)
                            if inc is not None:
                                inst.then_inc(sems[inc], inc_k)
                    return _

                getattr(block, eng)(make(self.ops[eng]))


def _fresh(ctx, nc, names):
    sems = {}
    for n in names:
        sems[n] = ctx.enter_context(nc.semaphore(n))
    return sems



def _dram(nc, io, ns, name, shape, dtype, kind):
    if io is not None and name in io:
        return io[name]
    return nc.dram_tensor(ns + name, shape, dtype, kind=kind).ap()


def build_launch1():
    nc = bass.Bass("TRN2", target_bir_lowering=False)
    xT = nc.dram_tensor("xT", [D, TC], F32, kind="ExternalInput").ap()
    Ws, Bs, tabs, outs = {}, {}, {}, {}
    for w in "qkvg":
        Ws[w] = nc.dram_tensor(f"W{w}", [D, D], F32, kind="ExternalInput").ap()
        Bs[w] = nc.dram_tensor(f"b{w}", [128, 16], F32, kind="ExternalInput").ap()
    for t in ("cosq", "sinq", "cosk", "sink"):
        tabs[t] = nc.dram_tensor(t, [128, TC], F32, kind="ExternalInput").ap()
    for o in ("qr", "kr", "v", "g"):
        outs[o] = nc.dram_tensor(f"o{o}", [D, TC], BF16, kind="ExternalOutput").ap()

    NT = 4
    NB = 512
    p = Plan()
    with ExitStack() as ctx:
        e = ctx.enter_context
        xb = e(nc.sbuf_tensor("xb", [128, 16 * TC], BF16))
        wb = e(nc.sbuf_tensor("wb", [128, 16 * D], BF16))
        tb_ = {t: e(nc.sbuf_tensor(f"sb_{t}", [128, TC], F32)) for t in tabs}
        bb = {w: e(nc.sbuf_tensor(f"bb{w}", [128, 16], F32)) for w in "qkvg"}
        qa = e(nc.sbuf_tensor("qa", [128, NB], F32))
        qs = e(nc.sbuf_tensor("qs", [128, NB], F32))
        t1 = e(nc.sbuf_tensor("t1", [128, NB], F32))
        t2 = e(nc.sbuf_tensor("t2", [128, NB], F32))
        ob = [e(nc.sbuf_tensor(f"ob{i}", [128, NB], BF16)) for i in range(4)]
        ps = [e(nc.psum_tensor(f"ps{i}", [128, NB], F32)) for i in range(2)]
        sems = _fresh(ctx, nc, ["ldx", "ldw", "ldt", "mmS", "dvS", "evS",
                                "st0", "st1", "st2", "st3"])

        for dm in range(16):
            p.dma("gpsimd", xb[:, dm * TC:(dm + 1) * TC], xT[dm * 128:(dm + 1) * 128, :], inc="ldx")
        for t in tabs:
            p.dma("sync", tb_[t][:, :], tabs[t][:, :], inc="ldt")
        for w in "qkvg":
            p.dma("sync", bb[w][:, :], Bs[w][:, :], inc="ldt")
        x_ready = p.mark("ldx")
        tabs_ready = p.mark("ldt")

        pairswap = []
        for i in range(0, 32, 2):
            pairswap += [i + 1, i]

        last_bank_free = [None, None]
        last_ob_dma = [None, None, None, None]
        wload_done = None
        n_mm_groups = 0
        for wi, w in enumerate("qkvg"):
            wconsumed = (p.ns + "mmS", n_mm_groups) if wi > 0 else None
            wld = None
            for dm in range(16):
                wld = p.dma("gpsimd", wb[:, dm * D:(dm + 1) * D],
                            Ws[w][dm * 128:(dm + 1) * 128, :],
                            waits=(wconsumed,) if dm == 0 and wconsumed else (),
                            inc="ldw")
            for tbi in range(NT):
                for do in range(16):
                    bank = do % 2
                    idx = tbi * 16 + do
                    for dm in range(16):
                        waits = []
                        if dm == 0:
                            waits = [x_ready, wld, last_bank_free[bank]]
                        p.emit("tensor",
                               lambda e, b=bank, dmm=dm, doo=do, tt=tbi, ww=w: e.matmul(
                                   ps[b][:, :],
                                   wb[:, dmm * D + doo * 128: dmm * D + doo * 128 + 128],
                                   xb[:, dmm * TC + tt * NB: dmm * TC + tt * NB + NB],
                                   start=(dmm == 0), stop=(dmm == 15)),
                               waits=waits,
                               inc="mmS" if dm == 15 else None)
                    mm_done = p.mark("mmS")
                    n_mm_groups += 1
                    obi = idx % 4
                    if w in ("q", "k"):
                        ct = "cosq" if w == "q" else "cosk"
                        st = "sinq" if w == "q" else "sink"
                        bf_ = p.emit("vector",
                                     lambda e, b=bank, ww=w, doo=do: e.tensor_scalar(
                                         out=qa[:, :], in0=ps[b][:, :],
                                         scalar1=bb[ww][:, doo:doo + 1], scalar2=None,
                                         op0=ALU.add),
                                     waits=[mm_done, tabs_ready], inc="dvS")
                        last_bank_free[bank] = bf_
                        p.emit("vector", lambda e: e.stream_shuffle(qs[:, :], qa[:, :], pairswap))
                        p.emit("vector",
                               lambda e, tt=tbi, c=ct: e.tensor_tensor(
                                   out=t1[:, :], in0=qa[:, :],
                                   in1=tb_[c][:, tt * NB:(tt + 1) * NB], op=ALU.mult))
                        p.emit("vector",
                               lambda e, tt=tbi, s_=st: e.tensor_tensor(
                                   out=t2[:, :], in0=qs[:, :],
                                   in1=tb_[s_][:, tt * NB:(tt + 1) * NB], op=ALU.mult))
                        dv = p.emit("vector",
                                    lambda e, o=obi: e.tensor_tensor(
                                        out=ob[o][:, :], in0=t1[:, :], in1=t2[:, :],
                                        op=ALU.add),
                                    waits=[last_ob_dma[obi]], inc="dvS")
                        ready = dv
                    elif w == "v":
                        dv = p.emit("vector",
                                    lambda e, b=bank, o=obi, doo=do: e.tensor_scalar(
                                        out=ob[o][:, :], in0=ps[b][:, :],
                                        scalar1=bb["v"][:, doo:doo + 1], scalar2=None,
                                        op0=ALU.add),
                                    waits=[mm_done, last_ob_dma[obi]], inc="dvS")
                        last_bank_free[bank] = dv
                        ready = dv
                    else:
                        ev = p.emit("scalar",
                                    lambda e, b=bank, o=obi, doo=do: e.activation(
                                        out=ob[o][:, :], in_=ps[b][:, :], func=AF.Silu,
                                        bias=bb["g"][:, doo:doo + 1]),
                                    waits=[mm_done, last_ob_dma[obi]], inc="evS")
                        last_bank_free[bank] = ev
                        ready = ev
                    oname = {"q": "qr", "k": "kr", "v": "v", "g": "g"}[w]
                    st_ = p.dma("sync", outs[oname][do * 128:(do + 1) * 128,
                                                    tbi * NB:(tbi + 1) * NB],
                                ob[obi][:, :], waits=[ready], inc=f"st{obi}")
                    last_ob_dma[obi] = st_
        p.emit("sync", lambda e: e.nop(),
               waits=[p.mark(f"st{i}") for i in range(4) if p.has(f"st{i}")])
        p.realize(nc, sems)
    return nc


_ROPE_CACHE = None


def _launch1_inputs(inputs):
    global _ROPE_CACHE
    if _ROPE_CACHE is None:
        _ROPE_CACHE = _rope_tables()
    cosq, sinq, cosk, sink = _ROPE_CACHE
    x = np.asarray(inputs["L0_attn_x"])[0]
    xTfull = np.ascontiguousarray(x.T)
    maps = []
    for c in range(NCORE):
        sl = slice(c * TC, (c + 1) * TC)
        m = {
            "xT": np.ascontiguousarray(xTfull[:, sl]),
            "cosq": np.ascontiguousarray(cosq[:, sl]),
            "sinq": np.ascontiguousarray(sinq[:, sl]),
            "cosk": np.ascontiguousarray(cosk[:, sl]),
            "sink": np.ascontiguousarray(sink[:, sl]),
        }
        for w, nm in (("q", "Wq"), ("k", "Wk"), ("v", "Wv"), ("g", "Wg")):
            m[f"W{w}"] = np.ascontiguousarray(inputs[f"L0_attn_{nm}"].astype(f32))
            m[f"b{w}"] = _pack_bias(np.asarray(inputs[f"L0_attn_b{w}"]))
        maps.append(m)
    return maps


def run_launch1(inputs):
    nc = build_launch1()
    maps = _launch1_inputs(inputs)
    res = run_bass_kernel_spmd(nc, maps, core_ids=list(range(NCORE)))
    full = {}
    for o in ("qr", "kr", "v", "g"):
        full[o] = np.concatenate([res.results[c][f"o{o}"] for c in range(NCORE)], axis=1)
    return full


_NC_CACHE = {}


def _nc(name, builder):
    if name not in _NC_CACHE:
        _NC_CACHE[name] = builder()
    return _NC_CACHE[name]


def kernel(**inputs):
    return kernel_mega(**inputs)


def kernel_multilaunch(**inputs):
    cores = list(range(NCORE))
    res = run_bass_kernel_spmd(_nc("l1", build_launch1),
                               _launch1_inputs(inputs), core_ids=cores)
    l1out = {o: np.concatenate([res.results[c][f"o{o}"] for c in cores], axis=1)
             for o in ("qr", "kr", "v", "g")}
    res = run_bass_kernel_spmd(_nc("l2", build_launch2),
                               _launch2_inputs(l1out), core_ids=cores)
    y_full = np.concatenate([res.results[c]["yh"] for c in cores], axis=0)
    res = run_bass_kernel_spmd(_nc("l3", build_launch3),
                               _launch3_inputs(inputs, y_full), core_ids=cores)
    x2 = np.concatenate([res.results[c]["x2o"] for c in cores], axis=1)
    x2b = np.concatenate([res.results[c]["x2bo"] for c in cores], axis=1)
    res = run_bass_kernel_spmd(_nc("l4", build_launch4),
                               _launch4_inputs(inputs, x2b), core_ids=cores)
    olocal = np.concatenate([res.results[c]["olocal"] for c in cores], axis=1)
    qhat_all = np.stack([res.results[c]["qhato"] for c in cores])
    summ_all = np.stack([res.results[c]["summo"] for c in cores])
    res = run_bass_kernel_spmd(
        _nc("l5", build_launch5),
        _launch5_inputs(inputs, x2, olocal, qhat_all, summ_all),
        core_ids=cores)
    outx = np.concatenate([res.results[c]["outx"] for c in cores], axis=1)
    return np.ascontiguousarray(outx.T).reshape(1, S, D)


def build_launch2(nqb=32, nheads=2, mega=None):
    import os
    stop = os.environ.get("L2STOP", "full")
    if mega is None:
        nc, p, io, ns = bass.Bass("TRN2", target_bir_lowering=False), Plan(), None, ""
    else:
        nc, p, io, ns = mega["nc"], mega["p"], mega["io"], mega["ns"]
    p.set_ns(ns)
    qrh = _dram(nc, io, ns, "qrh", [256, S], BF16, "ExternalInput")
    krh = _dram(nc, io, ns, "krh", [256, S], BF16, "ExternalInput")
    vTh = _dram(nc, io, ns, "vTh", [256, S], BF16, "ExternalInput")
    gh = _dram(nc, io, ns, "gh", [256, S], BF16, "ExternalInput")
    qdec = _dram(nc, io, ns, "qdec", [256, 512], F32, "ExternalInput")
    kdec = _dram(nc, io, ns, "kdec", [256, 512], F32, "ExternalInput")
    pairsc = _dram(nc, io, ns, "pairsc", [256, 131], F32, "ExternalInput")
    masks = _dram(nc, io, ns, "masks", [128, 2048], BF16, "ExternalInput")
    yh = _dram(nc, io, ns, "yh", [256, S], BF16, "ExternalOutput")

    NB = 512
    with ExitStack() as ctx:
        e = ctx.enter_context
        ksb = e(nc.sbuf_tensor(ns + "ksb", [128, S], BF16))
        vsb = e(nc.sbuf_tensor(ns + "vsb", [128, S], BF16))
        qdec_sb = e(nc.sbuf_tensor(ns + "qdec_sb", [128, 1024], F32))
        kdec_sb = e(nc.sbuf_tensor(ns + "kdec_sb", [128, 1024], F32))
        prs_sb = e(nc.sbuf_tensor(ns + "prs_sb", [128, 262], F32))
        msk_sb = e(nc.sbuf_tensor(ns + "msk_sb", [128, 2048], BF16))
        qt_raw = [e(nc.sbuf_tensor(ns + f"qt_raw{i}", [128, NB], BF16)) for i in range(2)]
        gt = [e(nc.sbuf_tensor(ns + f"gt{i}", [128, NB], BF16)) for i in range(2)]
        qtil = e(nc.sbuf_tensor(ns + "qtil", [128, NB], BF16))
        ssb = [e(nc.sbuf_tensor(ns + f"ssb{i}", [128, NB], BF16)) for i in range(3)]
        rsb = e(nc.sbuf_tensor(ns + "rsb", [128, NB], F32))
        rbf = e(nc.sbuf_tensor(ns + "rbf", [128, NB], BF16))
        sqb = e(nc.sbuf_tensor(ns + "sqb", [128, NB], BF16))
        mu = e(nc.sbuf_tensor(ns + "mu", [1, NB], F32))
        msq = e(nc.sbuf_tensor(ns + "msq", [1, NB], F32))
        varr = e(nc.sbuf_tensor(ns + "varr", [1, NB], F32))
        rstd = e(nc.sbuf_tensor(ns + "rstd", [1, NB], F32))
        A_bf = e(nc.sbuf_tensor(ns + "A_bf", [1, NB], BF16))
        B_bf = e(nc.sbuf_tensor(ns + "B_bf", [1, NB], BF16))
        t1 = e(nc.sbuf_tensor(ns + "t1", [128, NB], F32))
        t2 = e(nc.sbuf_tensor(ns + "t2", [128, NB], F32))
        obuf = [e(nc.sbuf_tensor(ns + f"obuf{i}", [128, NB], BF16)) for i in range(2)]
        onescol = e(nc.sbuf_tensor(ns + "onescol", [128, 1], BF16))
        onesrow = e(nc.sbuf_tensor(ns + "onesrow", [1, 128], BF16))
        psA = e(nc.psum_tensor(ns + "psA", [128, NB], F32))
        psB = e(nc.psum_tensor(ns + "psB", [128, NB], F32))
        psC = e(nc.psum_tensor(ns + "psC", [128, NB], F32))
        ps_sc = [psA, psB, psC]
        ps_ret = e(nc.psum_tensor(ns + "ps_ret", [128, NB], F32))
        ps_st1 = e(nc.psum_tensor(ns + "ps_st1", [1, NB], F32))
        ps_st2 = e(nc.psum_tensor(ns + "ps_st2", [1, NB], F32))
        ps_bc1 = ps_ret
        ps_bc2 = e(nc.psum_tensor(ns + "ps_bc2", [128, NB], F32))
        sems = _fresh(mega["sem_ctx"] if mega else ctx, nc,
                      [ns + s_ for s_ in ("ldt", "ldk", "ldv", "ldq0", "ldq1",
                                          "ldg0", "ldg1", "mmS", "dvS", "evS",
                                          "plS", "st0", "st1")])

        p.emit("gpsimd", lambda e: e.memset(onescol[:, :], 1.0), inc="plS")
        p.emit("gpsimd", lambda e: e.memset(onesrow[:, :], 1.0), inc="plS")
        ones_ready = p.mark("plS")
        p.dma("sync", qdec_sb[:, 0:512], qdec[0:128, :], inc="ldt")
        p.dma("sync", qdec_sb[:, 512:1024], qdec[128:256, :], inc="ldt")
        p.dma("sync", kdec_sb[:, 0:512], kdec[0:128, :], inc="ldt")
        p.dma("sync", kdec_sb[:, 512:1024], kdec[128:256, :], inc="ldt")
        p.dma("sync", prs_sb[:, 0:131], pairsc[0:128, :], inc="ldt")
        p.dma("sync", prs_sb[:, 131:262], pairsc[128:256, :], inc="ldt")
        p.dma("sync", msk_sb[:, :], masks[:, :], inc="ldt")
        tabs_ready = p.mark("ldt")

        last_sc_free = [None, None, None]
        last_ssb_freed = [None, None, None]
        last_q_use = [None, None]
        last_g_use = [None, None]
        last_ob_dma = [None, None]
        ret_free = None
        st_free = None
        bc_free = None

        for h in range(nheads):
            kv_war = p.mark0("mmS") if h > 0 else None
            for i in range(4):
                w = [kv_war] if (kv_war and i == 0) else []
                kld = p.dma("sync", ksb[:, i * 4096:(i + 1) * 4096],
                            krh[h * 128:(h + 1) * 128, i * 4096:(i + 1) * 4096],
                            waits=w, inc="ldk")
            for i in range(4):
                vld = p.dma("sync", vsb[:, i * 4096:(i + 1) * 4096],
                            vTh[h * 128:(h + 1) * 128, i * 4096:(i + 1) * 4096],
                            waits=[kv_war] if (kv_war and i == 0) else [],
                            inc="ldv")
            pre = []
            for ci in range(32):
                pre.append(p.emit("vector",
                                  lambda e, c=ci, hh=h: e.tensor_tensor(
                                      out=ksb[:, c * NB:(c + 1) * NB],
                                      in0=ksb[:, c * NB:(c + 1) * NB],
                                      in1=kdec_sb[:, hh * NB:(hh + 1) * NB],
                                      op=ALU.mult),
                                  waits=[kld, tabs_ready] if ci == 0 else [],
                                  inc="dvS"))
            k_ready = pre[-1]
            v_ready = vld

            for qb in range(nqb):
                par = qb % 2
                qdma = p.dma("sync", qt_raw[par][:, :],
                             qrh[h * 128:(h + 1) * 128, qb * NB:(qb + 1) * NB],
                             waits=[last_q_use[par]], inc=f"ldq{par}")
                gdma = p.dma("sync", gt[par][:, :],
                             gh[h * 128:(h + 1) * 128, qb * NB:(qb + 1) * NB],
                             waits=[last_g_use[par]], inc=f"ldg{par}")
                qprev_mm = p.mark0("mmS") if (qb > 0 or h > 0) else None
                qrdy = p.emit("vector",
                              lambda e, pr=par, hh=h: e.tensor_tensor(
                                  out=qtil[:, :], in0=qt_raw[pr][:, :],
                                  in1=qdec_sb[:, hh * NB:(hh + 1) * NB], op=ALU.mult),
                              waits=[qdma, tabs_ready, qprev_mm], inc="dvS")
                last_q_use[par] = qrdy

                nkt = 4 * qb + 4
                evac_done = [None] * nkt

                def emit_mm2(kt):
                    w2 = [evac_done[kt]]
                    if kt == 0:
                        w2 += [v_ready, bc_free]
                        w2 += list(ret_free or ())
                    p.emit("tensor",
                           lambda e, k_=kt, last=(kt == nkt - 1): e.matmul(
                               ps_ret[:, :], vsb[:, k_ * 128:(k_ + 1) * 128],
                               ssb[k_ % 3][:, :], start=(k_ == 0), stop=last),
                           waits=w2, inc="mmS")
                    last_ssb_freed[kt % 3] = p.mark("mmS")

                for kt in range(nkt):
                    b = kt % 3
                    w1 = [qrdy] if kt == 0 else []
                    if kt == 1:
                        w1 = [qrdy]
                    if last_sc_free[b] is not None:
                        w1.append(last_sc_free[b])
                    if kt == 0:
                        w1.append(k_ready)
                    p.emit("tensor",
                           lambda e, bb=b, k_=kt: e.matmul(
                               ps_sc[bb][:, :], ksb[:, k_ * 128:(k_ + 1) * 128],
                               qtil[:, :], start=True, stop=True),
                           waits=w1, inc="mmS")
                    mm1_done = p.mark("mmS")

                    m = 4 * qb - kt
                    col = h * 131 + (m + 3)
                    diag = kt > 4 * qb - 1 - 0
                    diag = kt >= 4 * qb
                    ssb_w = [mm1_done]
                    if last_ssb_freed[b] is not None:
                        ssb_w.append(last_ssb_freed[b])
                    if diag:
                        d = kt - 4 * qb
                        ev = p.emit("vector",
                                    lambda e, bb=b, c=col: e.tensor_scalar(
                                        out=ssb[bb][:, :], in0=ps_sc[bb][:, :],
                                        scalar1=prs_sb[:, c:c + 1], scalar2=None,
                                        op0=ALU.mult),
                                    waits=ssb_w, inc="dvS")
                        ev = p.emit("vector",
                                    lambda e, bb=b, dd=d: e.tensor_tensor(
                                        out=ssb[bb][:, :], in0=ssb[bb][:, :],
                                        in1=msk_sb[:, dd * NB:(dd + 1) * NB],
                                        op=ALU.mult),
                                    inc="dvS")
                    elif kt % 2 == 0:
                        ev = p.emit("scalar",
                                    lambda e, bb=b, c=col: e.activation(
                                        out=ssb[bb][:, :], in_=ps_sc[bb][:, :],
                                        func=AF.Copy, scale=prs_sb[:, c:c + 1]),
                                    waits=ssb_w, inc="evS")
                    else:
                        ev = p.emit("vector",
                                    lambda e, bb=b, c=col: e.tensor_scalar(
                                        out=ssb[bb][:, :], in0=ps_sc[bb][:, :],
                                        scalar1=prs_sb[:, c:c + 1], scalar2=None,
                                        op0=ALU.mult),
                                    waits=ssb_w, inc="dvS")
                    last_sc_free[b] = ev
                    evac_done[kt] = ev
                    if stop == "full" and kt >= 2:
                        emit_mm2(kt - 2)

                    if stop == "evac":
                        if kt == nkt - 1:
                            od = p.dma("sync",
                                       yh[h * 128:(h + 1) * 128, qb * NB:(qb + 1) * NB],
                                       ssb[kt % 2][:, :], waits=[ev], inc=f"st{par}")
                            last_ob_dma[par] = od
                            last_ssb_freed[b] = od
                        else:
                            last_ssb_freed[b] = ev
                        continue

                    if stop != "full":
                        w2 = [ev]
                        if kt == 0:
                            w2 += [v_ready, bc_free]
                            w2 += list(ret_free or ())
                        p.emit("tensor",
                               lambda e, k_=kt, last=(kt == nkt - 1): e.matmul(
                                   ps_ret[:, :],
                                   vsb[:, k_ * 128:(k_ + 1) * 128],
                                   ssb[k_ % 3][:, :], start=(k_ == 0),
                                   stop=last),
                               waits=w2, inc="mmS")
                        last_ssb_freed[b] = p.mark("mmS")
                if stop == "full":
                    for kt in range(max(nkt - 2, 0), nkt):
                        emit_mm2(kt)
                ret_done = p.mark("mmS")
                if stop == "evac":
                    continue
                if stop == "ret":
                    r1 = p.emit("scalar", lambda e: e.activation(
                        out=rsb[:, :], in_=ps_ret[:, :], func=AF.Copy, scale=1.0),
                        waits=[ret_done], inc="evS")
                    yv = p.emit("vector", lambda e, pr=par: e.tensor_copy(
                        obuf[pr][:, :], rsb[:, :]), waits=[r1, last_ob_dma[par]],
                        inc="dvS")
                    ret_free = (yv,)
                    od = p.dma("sync", yh[h * 128:(h + 1) * 128, qb * NB:(qb + 1) * NB],
                               obuf[par][:, :], waits=[yv], inc=f"st{par}")
                    last_ob_dma[par] = od
                    last_g_use[par] = gdma
                    continue

                r1 = p.emit("scalar", lambda e: e.activation(
                    out=rsb[:, :], in_=ps_ret[:, :], func=AF.Copy, scale=1.0),
                    waits=[ret_done, bc_free], inc="evS")
                r3 = p.emit("scalar", lambda e: e.activation(
                    out=sqb[:, :], in_=ps_ret[:, :], func=AF.Square),
                    waits=[ret_done], inc="evS")
                r2 = p.emit("vector", lambda e: e.tensor_copy(rbf[:, :], ps_ret[:, :]),
                            waits=[ret_done, r3], inc="dvS")
                ret_free = (r2, r3)
                p.emit("tensor", lambda e: e.matmul(
                    ps_st1[:, :], onescol[:, :], rbf[:, :], start=True, stop=True),
                    waits=[r2, ones_ready, st_free], inc="mmS")
                s1 = p.mark("mmS")
                p.emit("tensor", lambda e: e.matmul(
                    ps_st2[:, :], onescol[:, :], sqb[:, :], start=True, stop=True),
                    waits=[r3], inc="mmS")
                s2 = p.mark("mmS")
                m1 = p.emit("vector", lambda e: e.tensor_scalar(
                    out=mu[:, :], in0=ps_st1[:, :], scalar1=float(1.0 / 128.0),
                    scalar2=None, op0=ALU.mult), waits=[s1], inc="dvS")
                m2 = p.emit("scalar", lambda e: e.activation(
                    out=msq[:, :], in_=mu[:, :], func=AF.Square), waits=[m1], inc="evS")
                v1 = p.emit("vector", lambda e: e.tensor_scalar(
                    out=varr[:, :], in0=ps_st2[:, :], scalar1=float(1.0 / 128.0),
                    scalar2=None, op0=ALU.mult), waits=[s2], inc="dvS")
                st_free = v1
                v2 = p.emit("vector", lambda e: e.tensor_tensor(
                    out=varr[:, :], in0=varr[:, :], in1=msq[:, :], op=ALU.subtract),
                    waits=[m2], inc="dvS")
                l1 = p.emit("scalar", lambda e: e.activation(
                    out=rstd[:, :], in_=varr[:, :], func=AF.Ln, bias=np.float32(1e-5)),
                    waits=[v2], inc="evS")
                l2 = p.emit("scalar", lambda e: e.activation(
                    out=rstd[:, :], in_=rstd[:, :], func=AF.Exp, scale=np.float32(-0.5)),
                    inc="evS")
                if stop == "stats":
                    yv = p.emit("vector", lambda e, pr=par: e.tensor_copy(
                        obuf[pr][:, :], rsb[:, :]), waits=[r1, l2, last_ob_dma[par]],
                        inc="dvS")
                    ret_free = (r2, r3)
                    st_free = v1
                    od = p.dma("sync", yh[h * 128:(h + 1) * 128, qb * NB:(qb + 1) * NB],
                               obuf[par][:, :], waits=[yv], inc=f"st{par}")
                    last_ob_dma[par] = od
                    last_g_use[par] = gdma
                    continue
                a1 = p.emit("vector", lambda e: e.tensor_copy(A_bf[:, :], rstd[:, :]),
                            waits=[l2], inc="dvS")
                b1 = p.emit("vector", lambda e: e.scalar_tensor_tensor(
                    out=B_bf[:, :], in0=mu[:, :], scalar=float(-1.0),
                    in1=rstd[:, :], op0=ALU.mult, op1=ALU.mult),
                    waits=[l2], inc="dvS")
                p.emit("tensor", lambda e: e.matmul(
                    ps_bc1[:, :], onesrow[:, :], A_bf[:, :], start=True, stop=True),
                    waits=[a1, bc_free], inc="mmS")
                c1 = p.mark("mmS")
                p.emit("tensor", lambda e: e.matmul(
                    ps_bc2[:, :], onesrow[:, :], B_bf[:, :], start=True, stop=True),
                    waits=[b1], inc="mmS")
                c2 = p.mark("mmS")
                u1 = p.emit("vector", lambda e: e.tensor_tensor(
                    out=t1[:, :], in0=rsb[:, :], in1=ps_bc1[:, :], op=ALU.mult),
                    waits=[c1, r1], inc="dvS")
                u2 = p.emit("vector", lambda e: e.tensor_tensor(
                    out=t2[:, :], in0=t1[:, :], in1=ps_bc2[:, :], op=ALU.add),
                    waits=[c2], inc="dvS")
                bc_free = u2
                if stop == "bc":
                    yv = p.emit("vector", lambda e, pr=par: e.tensor_copy(
                        obuf[pr][:, :], t2[:, :]), waits=[last_ob_dma[par]], inc="dvS")
                    ret_free = (r2, r3)
                    st_free = v1
                    od = p.dma("sync", yh[h * 128:(h + 1) * 128, qb * NB:(qb + 1) * NB],
                               obuf[par][:, :], waits=[yv], inc=f"st{par}")
                    last_ob_dma[par] = od
                    last_g_use[par] = gdma
                    continue
                y1 = p.emit("vector", lambda e, pr=par: e.tensor_tensor(
                    out=obuf[pr][:, :], in0=t2[:, :], in1=gt[pr][:, :], op=ALU.mult),
                    waits=[gdma, last_ob_dma[par]], inc="dvS")
                last_g_use[par] = y1
                od = p.dma("sync", yh[h * 128:(h + 1) * 128, qb * NB:(qb + 1) * NB],
                           obuf[par][:, :], waits=[y1], inc=f"st{par}")
                last_ob_dma[par] = od

        p.emit("sync", lambda e: e.nop(),
               waits=[p.mark(s_) for s_ in ("st0", "st1") if p.has(s_)])
        if mega is None:
            p.realize(nc, sems)
        else:
            mega["sems"].update(sems)
    return nc


def _launch2_inputs(l1out):
    maps = []
    masks = np.ascontiguousarray(_diag_masks())
    for c in range(NCORE):
        rows = slice(256 * c, 256 * c + 256)
        vT = np.empty((256, S), dtype=BF)
        qd = np.empty((256, 512), dtype=f32)
        kd = np.empty((256, 512), dtype=f32)
        pscl = np.empty((256, 131), dtype=f32)
        for hi in range(2):
            h = 2 * c + hi
            vh = l1out["v"][h * 128:(h + 1) * 128, :]
            vT[hi * 128:(hi + 1) * 128, :] = (
                vh.T.reshape(128, 128, 128).transpose(1, 0, 2).reshape(128, S))
            a, b, cc_ = _decay_tiles(h)
            qd[hi * 128:(hi + 1) * 128] = a
            kd[hi * 128:(hi + 1) * 128] = b
            pscl[hi * 128:(hi + 1) * 128] = cc_
        maps.append({
            "qrh": np.ascontiguousarray(l1out["qr"][rows]),
            "krh": np.ascontiguousarray(l1out["kr"][rows]),
            "vTh": vT,
            "gh": np.ascontiguousarray(l1out["g"][rows]),
            "qdec": qd, "kdec": kd, "pairsc": pscl, "masks": masks,
        })
    return maps


def run_launch2(l1out, nqb=32):
    nc = build_launch2(nqb=nqb)
    maps = _launch2_inputs(l1out)
    res = run_bass_kernel_spmd(nc, maps, core_ids=list(range(NCORE)))
    return np.concatenate([res.results[c]["yh"] for c in range(NCORE)], axis=0)


def build_launch3(mega=None):
    if mega is None:
        nc, io, ns, pre = bass.Bass("TRN2", target_bir_lowering=False), None, "", None
        p = Plan()
    else:
        nc, p, io, ns = mega["nc"], mega["p"], mega["io"], mega["ns"]
        pre = mega.get("pre")
    p.set_ns(ns)
    y_sl = _dram(nc, io, ns, "y_sl", [D, TC], BF16, "ExternalInput")
    Wo = _dram(nc, io, ns, "Wo", [D, D], F32, "ExternalInput")
    bo = _dram(nc, io, ns, "bo", [128, 16], F32, "ExternalInput")
    xT = _dram(nc, io, ns, "xT", [D, TC], F32, "ExternalInput")
    wfc_t = _dram(nc, io, ns, "wfc_t", [64 * 16 * 128, 128], F32, "ExternalInput")
    wproj_t = _dram(nc, io, ns, "wproj_t", [16 * 64 * 128, 128], F32, "ExternalInput")
    x2o = _dram(nc, io, ns, "x2o", [D, TC], F32, "ExternalOutput")
    x2bo = _dram(nc, io, ns, "x2bo", [D, TC], BF16, "ExternalOutput")
    x1buf = _dram(nc, io, ns, "x1buf", [D, TC], F32, "Internal")
    xnbuf = _dram(nc, io, ns, "xnbuf", [D, TC], BF16, "Internal")
    tfc = _dram(nc, io, ns, "tfc", [64 * 16 * 128, 128], BF16, "Internal")
    tproj = _dram(nc, io, ns, "tproj", [16 * 64 * 128, 128], BF16, "Internal")

    NB = 512
    with ExitStack() as ctx:
        e = ctx.enter_context
        A = e(nc.sbuf_tensor(ns + "A", [128, 16 * NB], BF16))
        B = e(nc.sbuf_tensor(ns + "B", [128, 16 * TC], BF16))
        s_sb = [e(nc.sbuf_tensor(ns + f"s_sb{i}", [128, NB], F32)) for i in range(16)]
        xr = [e(nc.sbuf_tensor(ns + f"xr{i}", [128, NB], F32)) for i in range(2)]
        if pre is None:
            wstr = [e(nc.sbuf_tensor(ns + f"wstr{i}", [128, 2048], F32)) for i in range(2)]
            aa_t = e(nc.sbuf_tensor(ns + "aa_t", [128, 2048], F32))
            msk_t = e(nc.sbuf_tensor(ns + "msk_t", [128, 2048], BF16))
            sgn_t = e(nc.sbuf_tensor(ns + "sgn_t", [128, 2048], BF16))
            tstr = [e(nc.sbuf_tensor(ns + f"tstr{i}", [128, 2048], BF16)) for i in range(2)]
        else:
            tstr = pre["tstr"]
        pstr = e(nc.sbuf_tensor(ns + "pstr", [128, 8192], BF16))
        sbf_t = e(nc.sbuf_tensor(ns + "sbf_t", [128, NB], BF16))
        xnb_t = e(nc.sbuf_tensor(ns + "xnb_t", [128, NB], BF16))
        sq_t = e(nc.sbuf_tensor(ns + "sq_t", [128, NB], BF16))
        tmp1 = e(nc.sbuf_tensor(ns + "tmp1", [128, NB], F32))
        tmp2 = e(nc.sbuf_tensor(ns + "tmp2", [128, NB], F32))
        Af_t = e(nc.sbuf_tensor(ns + "Af_t", [128, NB], F32))
        Bf_t = e(nc.sbuf_tensor(ns + "Bf_t", [128, NB], F32))
        bo_sb = e(nc.sbuf_tensor(ns + "bo_sb", [128, 16], F32))
        onescol = e(nc.sbuf_tensor(ns + "onescol", [128, 1], BF16))
        onesrow = e(nc.sbuf_tensor(ns + "onesrow", [1, 128], BF16))
        if pre is None:
            onescol_f = e(nc.sbuf_tensor(ns + "onescol_f", [128, 1], F32))
            onesrow_f = e(nc.sbuf_tensor(ns + "onesrow_f", [1, 128], F32))
            wacc = e(nc.sbuf_tensor(ns + "wacc", [128, 1], F32))
            wacc2 = e(nc.sbuf_tensor(ns + "wacc2", [128, 1], F32))
            red1 = e(nc.sbuf_tensor(ns + "red1", [128, 1], F32))
            sc1 = e(nc.sbuf_tensor(ns + "sc1", [1, 4], F32))
            thrb = e(nc.sbuf_tensor(ns + "thrb", [128, 1], F32))
            gamb = e(nc.sbuf_tensor(ns + "gamb", [128, 1], F32))
            thrb2 = e(nc.sbuf_tensor(ns + "thrb2", [128, 1], F32))
            gamb2 = e(nc.sbuf_tensor(ns + "gamb2", [128, 1], F32))
        else:
            gamb, thrb = pre["gamb"], pre["thrb"]
            gamb2, thrb2 = pre["gamb2"], pre["thrb2"]
        mu = e(nc.sbuf_tensor(ns + "mu", [1, NB], F32))
        msq = e(nc.sbuf_tensor(ns + "msq", [1, NB], F32))
        varr = e(nc.sbuf_tensor(ns + "varr", [1, NB], F32))
        rstd = e(nc.sbuf_tensor(ns + "rstd", [1, NB], F32))
        r2t = e(nc.sbuf_tensor(ns + "r2t", [1, NB], F32))
        var2 = e(nc.sbuf_tensor(ns + "var2", [1, NB], F32))
        rstd2 = e(nc.sbuf_tensor(ns + "rstd2", [1, NB], F32))
        A_bf = e(nc.sbuf_tensor(ns + "A_bf", [1, NB], BF16))
        B_bf = e(nc.sbuf_tensor(ns + "B_bf", [1, NB], BF16))
        C_bf = e(nc.sbuf_tensor(ns + "C_bf", [1, NB], BF16))
        ps_a = [e(nc.psum_tensor(ns + f"ps_a{i}", [128, NB], F32)) for i in range(2)]
        ps_s1 = e(nc.psum_tensor(ns + "ps_s1", [1, NB], F32))
        ps_s2 = e(nc.psum_tensor(ns + "ps_s2", [1, NB], F32))
        ps_b1 = e(nc.psum_tensor(ns + "ps_b1", [128, NB], F32))
        ps_b2 = e(nc.psum_tensor(ns + "ps_b2", [128, NB], F32))
        ps_b3 = e(nc.psum_tensor(ns + "ps_b3", [128, NB], F32))
        if pre is None:
            ps_sc = e(nc.psum_tensor(ns + "ps_sc", [1, 8], F32))

        sems = _fresh(mega["sem_ctx"] if mega else ctx, nc,
                      [ns + s_ for s_ in ("lda", "ldw", "ldx0", "ldx1b", "ldt",
                                          "ldstr0", "ldstr1", "ldpr0", "ldpr1",
                                          "ldpr2", "ldpr3", "wx1",
                                          "wxn", "wx2b", "mmS", "dvS", "evS",
                                          "plS", "st0", "stw0", "stw1")])

        p.emit("gpsimd", lambda e: e.memset(onescol[:, :], 1.0))
        ones_ready = p.emit("gpsimd", lambda e: e.memset(onesrow[:, :], 1.0))
        if pre is None:
            p.emit("gpsimd", lambda e: e.memset(onescol_f[:, :], 1.0))
            ones_ready = p.emit("gpsimd",
                                lambda e: e.memset(onesrow_f[:, :], 1.0))
        wld = None
        for i in range(16):
            wld = p.dma("gpsimd", B[:, i * TC:(i + 1) * TC],
                        Wo[i * 128:(i + 1) * 128, :], inc="ldw")
        p.dma("sync", bo_sb[:, :], bo[:, :], inc="ldt")
        tl = p.mark("ldt")

        if pre is None:
            macc = {}
            for wi, (wt, acc) in enumerate(((wfc_t, wacc), (wproj_t, wacc2))):
                for i in range(64):
                    srcv = wt[i * 2048:(i + 1) * 2048, :].rearrange(
                        "(a p) b -> p a b", p=128)
                    ld = p.dma("sync", wstr[wi][:, :].rearrange("p (a b) -> p a b", a=16), srcv,
                               waits=[macc.get((wi, i - 1))], inc=f"ldstr{wi}")
                    p.emit("vector", lambda e, w_=wi: e.tensor_reduce(
                        out=red1[:, :], in_=wstr[w_][:, :], axis=AX.X,
                        op=ALU.add, apply_absolute_value=True), waits=[ld])
                    ac = p.emit("vector", lambda e, a_=acc, first=(i == 0): (
                        e.tensor_copy(a_[:, :], red1[:, :]) if first else
                        e.tensor_tensor(out=a_[:, :], in0=a_[:, :], in1=red1[:, :],
                                        op=ALU.add)))
                    macc[(wi, i)] = ac

            def scalar_reduce(acc, gbuf, tbuf, denom, col, ps_big):
                t = p.emit("tensor", lambda e, a_=acc, c_=col: e.matmul(
                    ps_sc[0:1, c_:c_ + 1], onescol_f[:, :], a_[:, :],
                    start=True, stop=True),
                    waits=[macc[(0, 63)], macc[(1, 63)], ones_ready], inc="mmS")
                g1 = p.emit("vector", lambda e, c_=col: e.tensor_scalar(
                    out=sc1[0:1, c_:c_ + 1], in0=ps_sc[0:1, c_:c_ + 1],
                    scalar1=float(1.0 / denom), scalar2=None, op0=ALU.mult),
                    waits=[t])
                t2_ = p.emit("vector", lambda e, c_=col: e.tensor_scalar(
                    out=sc1[0:1, c_ + 1:c_ + 2], in0=sc1[0:1, c_:c_ + 1],
                    scalar1=float(1e-6), scalar2=float(0.5), op0=ALU.add,
                    op1=ALU.mult))
                bg = p.emit("tensor", lambda e, c_=col, ps=ps_big: e.matmul(
                    ps[:, 0:2], onesrow_f[:, :], sc1[0:1, c_:c_ + 2],
                    start=True, stop=True), waits=[t2_], inc="mmS")
                c1 = p.emit("scalar", lambda e, g_=gbuf, ps=ps_big: e.activation(
                    out=g_[:, :], in_=ps[:, 0:1], func=AF.Copy, scale=1.0),
                    waits=[bg], inc="evS")
                c2 = p.emit("scalar", lambda e, t_=tbuf, ps=ps_big: e.activation(
                    out=t_[:, :], in_=ps[:, 1:2], func=AF.Copy, scale=1.0),
                    inc="evS")
                return (g1, c2)
            thr_ready = scalar_reduce(wacc, gamb, thrb, 2048.0 * 8192.0, 0, ps_b1)
            thr2_ready = scalar_reduce(wacc2, gamb2, thrb2, 8192.0 * 2048.0, 2, ps_b2)
        else:
            thr_ready, thr2_ready = pre["thr_ready"], pre["thr2_ready"]


        last_xr = [None, None]
        st_done = None
        bc_done = None
        bank_free = [None, None]
        a_free = None
        prev_s1m = prev_s2m = None
        tmp1_dma = None
        xnb_dma = None
        p1_tmp2_free = None
        for tbi in range(4):
            alds = []
            for vt in range(16):
                alds.append(p.dma("sync", A[:, vt * NB:(vt + 1) * NB],
                                  y_sl[vt * 128:(vt + 1) * 128,
                                       tbi * NB:(tbi + 1) * NB],
                                  waits=[a_free] if vt == 0 else [], inc="lda"))
            a_rdy = alds[-1]
            for do in range(16):
                bank = do % 2
                for vt in range(16):
                    w1 = []
                    if vt == 0:
                        w1 = [a_rdy, wld, bank_free[bank]]
                        if do == 0 and tbi == 0:
                            w1 += [thr_ready[0], thr2_ready[0]]
                    p.emit("tensor",
                           lambda e, b=bank, v_=vt, d_=do: e.matmul(
                               ps_a[b][:, :],
                               B[:, v_ * TC + d_ * 128: v_ * TC + d_ * 128 + 128],
                               A[:, v_ * NB:(v_ + 1) * NB],
                               start=(v_ == 0), stop=(v_ == 15)),
                           waits=w1, inc="mmS" if vt == 15 else None)
                mm_done = p.mark("mmS")
                if do == 15:
                    a_free = mm_done
                xld = p.dma("sync", xr[bank][:, :],
                            xT[do * 128:(do + 1) * 128, tbi * NB:(tbi + 1) * NB],
                            waits=[last_xr[bank]],
                            inc=("ldx0" if bank == 0 else "ldx1b"))
                sv = p.emit("vector",
                            lambda e, b=bank, d_=do: e.scalar_tensor_tensor(
                                out=s_sb[d_][:, :], in0=ps_a[b][:, :],
                                scalar=bo_sb[:, d_:d_ + 1], in1=xr[b][:, :],
                                op0=ALU.add, op1=ALU.add),
                            waits=[mm_done, xld, tl], inc="dvS")
                last_xr[bank] = sv
                bank_free[bank] = sv
                sb_ = p.emit("vector", lambda e, d_=do: e.tensor_copy(
                    sbf_t[:, :], s_sb[d_][:, :]), waits=[prev_s1m], inc="dvS")
                sq_ = p.emit("scalar", lambda e: e.activation(
                    out=sq_t[:, :], in_=sbf_t[:, :], func=AF.Square),
                    waits=[sb_, prev_s2m], inc="evS")
                w_st1 = [sb_, ones_ready]
                if do == 0:
                    w_st1 += [st_done]
                prev_s1m = p.emit("tensor", lambda e, d_=do: e.matmul(
                    ps_s1[:, :], onescol[:, :], sbf_t[:, :],
                    start=(d_ == 0), stop=(d_ == 15)), waits=w_st1, inc="mmS")
                w_st2 = [sq_]
                if do == 0:
                    w_st2 += [st_done]
                prev_s2m = p.emit("tensor", lambda e, d_=do: e.matmul(
                    ps_s2[:, :], onescol[:, :], sq_t[:, :],
                    start=(d_ == 0), stop=(d_ == 15)), waits=w_st2, inc="mmS")
            stats_done = p.mark("mmS")
            m1 = p.emit("vector", lambda e: e.tensor_scalar(
                out=mu[:, :], in0=ps_s1[:, :], scalar1=float(1.0 / 2048.0),
                scalar2=None, op0=ALU.mult), waits=[stats_done], inc="dvS")
            m2 = p.emit("scalar", lambda e: e.activation(
                out=msq[:, :], in_=mu[:, :], func=AF.Square),
                waits=[m1], inc="evS")
            v1 = p.emit("vector", lambda e: e.tensor_scalar(
                out=varr[:, :], in0=ps_s2[:, :], scalar1=float(1.0 / 2048.0),
                scalar2=None, op0=ALU.mult), inc="dvS")
            st_done = v1
            v2 = p.emit("vector", lambda e: e.tensor_tensor(
                out=varr[:, :], in0=varr[:, :], in1=msq[:, :], op=ALU.subtract),
                waits=[m2], inc="dvS")
            l1 = p.emit("scalar", lambda e: e.activation(
                out=rstd[:, :], in_=varr[:, :], func=AF.Ln, bias=np.float32(1e-5)),
                waits=[v2], inc="evS")
            l2 = p.emit("scalar", lambda e: e.activation(
                out=rstd[:, :], in_=rstd[:, :], func=AF.Exp,
                scale=np.float32(-0.5)), inc="evS")
            w1_ = p.emit("vector", lambda e: e.tensor_scalar(
                out=r2t[:, :], in0=varr[:, :], scalar1=float(1e-5),
                scalar2=None, op0=ALU.add), inc="dvS")
            w2_ = p.emit("vector", lambda e: e.reciprocal(r2t[:, :], r2t[:, :]),
                         inc="dvS")
            w3_ = p.emit("vector", lambda e: e.tensor_tensor(
                out=var2[:, :], in0=varr[:, :], in1=r2t[:, :], op=ALU.mult),
                inc="dvS")
            w4_ = p.emit("scalar", lambda e: e.activation(
                out=rstd2[:, :], in_=var2[:, :], func=AF.Ln,
                bias=np.float32(1e-6)), waits=[w3_], inc="evS")
            w5_ = p.emit("scalar", lambda e: e.activation(
                out=rstd2[:, :], in_=rstd2[:, :], func=AF.Exp,
                scale=np.float32(-0.5)), inc="evS")
            a1 = p.emit("vector", lambda e: e.tensor_copy(A_bf[:, :], rstd[:, :]),
                        waits=[l2], inc="dvS")
            b1 = p.emit("vector", lambda e: e.scalar_tensor_tensor(
                out=B_bf[:, :], in0=mu[:, :], scalar=float(-1.0), in1=rstd[:, :],
                op0=ALU.mult, op1=ALU.mult), inc="dvS")
            c1_ = p.emit("vector", lambda e: e.tensor_copy(C_bf[:, :],
                                                           rstd2[:, :]),
                         waits=[w5_], inc="dvS")
            wb1_ = [a1, ones_ready, bc_done]
            if tbi == 0:
                wb1_ += [thr_ready[1], thr2_ready[1]]
            p.emit("tensor", lambda e: e.matmul(
                ps_b1[:, :], onesrow[:, :], A_bf[:, :], start=True, stop=True),
                waits=wb1_, inc="mmS")
            cA = p.mark("mmS")
            p.emit("tensor", lambda e: e.matmul(
                ps_b2[:, :], onesrow[:, :], B_bf[:, :], start=True, stop=True),
                waits=[b1], inc="mmS")
            cB = p.mark("mmS")
            p.emit("tensor", lambda e: e.matmul(
                ps_b3[:, :], onesrow[:, :], C_bf[:, :], start=True, stop=True),
                waits=[c1_], inc="mmS")
            cC = p.mark("mmS")
            u3 = None
            for do in range(16):
                u1 = p.emit("vector", lambda e, d_=do: e.tensor_tensor(
                    out=tmp1[:, :], in0=s_sb[d_][:, :], in1=ps_b1[:, :],
                    op=ALU.mult), waits=[cA, tmp1_dma], inc="dvS")
                u2 = p.emit("vector", lambda e: e.tensor_tensor(
                    out=tmp1[:, :], in0=tmp1[:, :], in1=ps_b2[:, :], op=ALU.add),
                    waits=[cB], inc="dvS")
                x1d = p.dma("sync", x1buf[do * 128:(do + 1) * 128,
                                          tbi * NB:(tbi + 1) * NB],
                            tmp1[:, :], waits=[u2], inc="wx1")
                tmp1_dma = x1d
                u3 = p.emit("vector", lambda e: e.tensor_tensor(
                    out=tmp2[:, :], in0=tmp1[:, :], in1=ps_b3[:, :],
                    op=ALU.mult), waits=[cC], inc="dvS")
                u4 = p.emit("vector", lambda e: e.tensor_copy(xnb_t[:, :],
                                                              tmp2[:, :]),
                            waits=[xnb_dma], inc="dvS")
                p1_tmp2_free = u4
                xnd = p.dma("sync", xnbuf[do * 128:(do + 1) * 128,
                                          tbi * NB:(tbi + 1) * NB],
                            xnb_t[:, :], waits=[u4], inc="wxn")
                xnb_dma = xnd
            bc_done = u3
        p1_xn_dma = p.mark("wxn")
        p1_x1_dma = p.mark("wx1")

        if pre is None:
            tern_mark = {}
            for wi, (wt, dst, tb_buf) in enumerate(((wfc_t, tfc, thrb),
                                                    (wproj_t, tproj, thrb2))):
                thr_m = thr_ready[1] if wi == 0 else thr2_ready[1]
                for j in range(64):
                    buf = j % 2
                    srcv = wt[j * 2048:(j + 1) * 2048, :].rearrange(
                        "(a p) b -> p a b", p=128)
                    ld = p.dma("sync", wstr[buf][:, :].rearrange("p (a b) -> p a b", a=16), srcv,
                               waits=[tern_mark.get(("sgn", wi, j - 2)),
                                      macc[(wi, 63)] if j < 2 else None,
                                      macc[(1, 63)] if (wi == 0 and j < 2) else None],
                               inc=f"ldstr{buf}")
                    a_ = p.emit("scalar", lambda e, b=buf: e.activation(
                        out=aa_t[:, :], in_=wstr[b][:, :], func=AF.Abs),
                        waits=[ld, thr_m], inc="evS")
                    m_ = p.emit("vector", lambda e, t_=tb_buf: e.tensor_scalar(
                        out=msk_t[:, :], in0=aa_t[:, :], scalar1=t_[:, :],
                        scalar2=None, op0=ALU.is_gt), waits=[a_], inc="dvS")
                    g_ = p.emit("scalar", lambda e, b=buf: e.activation(
                        out=sgn_t[:, :], in_=wstr[b][:, :], func=AF.Sign),
                        waits=[m_], inc="evS")
                    tern_mark[("sgn", wi, j)] = g_
                    t_ = p.emit("gpsimd", lambda e, b=buf: e.tensor_tensor(
                        out=tstr[b][:, :], in0=msk_t[:, :], in1=sgn_t[:, :],
                        op=ALU.mult),
                        waits=[g_, tern_mark.get(("st", wi, j - 2)),
                               tern_mark.get(("st", wi - 1, 62 + j))
                               if j < 2 else None],
                        inc="plS")
                    stm = p.dma("sync", dst[j * 2048:(j + 1) * 2048, :].rearrange(
                        "(a p) b -> p a b", p=128),
                        tstr[buf][:, :].rearrange("p (a b) -> p a b", a=16),
                        waits=[t_], inc=f"stw{buf}")
                    tern_mark[("st", wi, j)] = stm
            tern_ready = p.mark("stw0")
            tern_ready2 = p.mark("stw1")
        else:
            tern_ready, tern_ready2 = pre["tern_ready"], pre["tern_ready2"]

        strip_mark = {}
        x1r_mark = [None, None]
        pstr_mark = None
        for tbi in range(4):
            alds = []
            for dm in range(16):
                alds.append(p.dma("sync", A[:, dm * NB:(dm + 1) * NB],
                                  xnbuf[dm * 128:(dm + 1) * 128,
                                        tbi * NB:(tbi + 1) * NB],
                                  waits=[a_free, p1_xn_dma] if dm == 0 else [],
                                  inc="lda"))
            a_rdy = alds[-1]
            prev_s1m = prev_s2m = None
            h2_ready = [None] * 64
            for ot in range(64):
                buf = ot % 2
                ld = p.dma("sync", tstr[buf][:, :].rearrange("p (a b) -> p a b", a=16),
                           tfc[ot * 2048:(ot + 1) * 2048, :].rearrange(
                               "(a p) b -> p a b", p=128),
                           waits=[strip_mark.get(("fcmm", tbi, ot - 2)),
                                  strip_mark.get(("fcmm", tbi - 1,
                                                  62 + (ot % 2)))
                                  if (tbi > 0 and ot < 2) else None,
                                  tern_ready if (tbi == 0 and ot < 2) else None,
                                  tern_ready2 if (tbi == 0 and ot < 2) else None],
                           inc=f"ldstr{buf}")
                for dm in range(16):
                    w1 = []
                    if dm == 0:
                        w1 = [ld, a_rdy, bank_free[buf]]
                    p.emit("tensor",
                           lambda e, b=buf, d_=dm: e.matmul(
                               ps_a[b][:, :],
                               tstr[b][:, d_ * 128:(d_ + 1) * 128],
                               A[:, d_ * NB:(d_ + 1) * NB],
                               start=(d_ == 0), stop=(d_ == 15)),
                           waits=w1, inc="mmS" if dm == 15 else None)
                mmk = p.mark("mmS")
                strip_mark[("fcmm", tbi, ot)] = mmk
                if ot == 63:
                    a_free = mmk
                ge = p.emit("scalar", lambda e, b=buf, o_=ot: e.activation(
                    out=B[:, o_ * NB:(o_ + 1) * NB], in_=ps_a[b][:, :],
                    func=AF.Gelu, scale=gamb[:, :]), waits=[mmk], inc="evS")
                bank_free[buf] = ge
                h2_ready[ot] = ge
                sq_ = p.emit("scalar", lambda e, o_=ot: e.activation(
                    out=sq_t[:, :], in_=B[:, o_ * NB:(o_ + 1) * NB],
                    func=AF.Square), waits=[prev_s2m], inc="evS")
                w_st1 = [ge, ones_ready]
                if ot == 0:
                    w_st1 += [st_done]
                prev_s1m = p.emit("tensor", lambda e, o_=ot: e.matmul(
                    ps_s1[:, :], onescol[:, :], B[:, o_ * NB:(o_ + 1) * NB],
                    start=(o_ == 0), stop=(o_ == 63)), waits=w_st1, inc="mmS")
                w_st2 = [sq_]
                if ot == 0:
                    w_st2 += [st_done]
                prev_s2m = p.emit("tensor", lambda e, o_=ot: e.matmul(
                    ps_s2[:, :], onescol[:, :], sq_t[:, :],
                    start=(o_ == 0), stop=(o_ == 63)), waits=w_st2, inc="mmS")
            stats_done = p.mark("mmS")
            m1 = p.emit("vector", lambda e: e.tensor_scalar(
                out=mu[:, :], in0=ps_s1[:, :], scalar1=float(1.0 / 8192.0),
                scalar2=None, op0=ALU.mult), waits=[stats_done], inc="dvS")
            m2 = p.emit("scalar", lambda e: e.activation(
                out=msq[:, :], in_=mu[:, :], func=AF.Square),
                waits=[m1], inc="evS")
            v1 = p.emit("vector", lambda e: e.tensor_scalar(
                out=varr[:, :], in0=ps_s2[:, :], scalar1=float(1.0 / 8192.0),
                scalar2=None, op0=ALU.mult), inc="dvS")
            st_done = v1
            v2 = p.emit("vector", lambda e: e.tensor_tensor(
                out=varr[:, :], in0=varr[:, :], in1=msq[:, :], op=ALU.subtract),
                waits=[m2], inc="dvS")
            l1 = p.emit("scalar", lambda e: e.activation(
                out=rstd[:, :], in_=varr[:, :], func=AF.Ln, bias=np.float32(1e-6)),
                waits=[v2], inc="evS")
            l2 = p.emit("scalar", lambda e: e.activation(
                out=rstd[:, :], in_=rstd[:, :], func=AF.Exp,
                scale=np.float32(-0.5)), inc="evS")
            a1 = p.emit("vector", lambda e: e.tensor_copy(A_bf[:, :], rstd[:, :]),
                        waits=[l2], inc="dvS")
            b1 = p.emit("vector", lambda e: e.scalar_tensor_tensor(
                out=B_bf[:, :], in0=mu[:, :], scalar=float(-1.0), in1=rstd[:, :],
                op0=ALU.mult, op1=ALU.mult), inc="dvS")
            p.emit("tensor", lambda e: e.matmul(
                ps_b1[:, :], onesrow[:, :], A_bf[:, :], start=True, stop=True),
                waits=[a1, bc_done], inc="mmS")
            cA = p.mark("mmS")
            p.emit("tensor", lambda e: e.matmul(
                ps_b2[:, :], onesrow[:, :], B_bf[:, :], start=True, stop=True),
                waits=[b1], inc="mmS")
            cB = p.mark("mmS")
            eA = p.emit("scalar", lambda e: e.activation(
                out=Af_t[:, :], in_=ps_b1[:, :], func=AF.Copy, scale=1.0),
                waits=[cA], inc="evS")
            eB = p.emit("scalar", lambda e: e.activation(
                out=Bf_t[:, :], in_=ps_b2[:, :], func=AF.Copy, scale=1.0),
                waits=[cB], inc="evS")
            bc_done = eB
            xn2_ready = []
            for ot in range(64):
                eng = "vector" if ot % 2 == 0 else "gpsimd"
                ww = [eA, eB, h2_ready[ot]]
                if ot == 0:
                    ww.append(tmp1_dma)
                if ot == 1:
                    ww.append(p1_tmp2_free)
                p.emit(eng, lambda e, o_=ot: e.tensor_tensor(
                    out=tmp1[:, :] if o_ % 2 == 0 else tmp2[:, :],
                    in0=B[:, o_ * NB:(o_ + 1) * NB], in1=Af_t[:, :],
                    op=ALU.mult), waits=ww)
                q2 = p.emit(eng, lambda e, o_=ot: e.tensor_tensor(
                    out=B[:, o_ * NB:(o_ + 1) * NB],
                    in0=tmp1[:, :] if o_ % 2 == 0 else tmp2[:, :],
                    in1=Bf_t[:, :], op=ALU.add))
                xn2_ready.append(q2)
            prev_s1m = prev_s2m = None
            for do in range(16):
                bank = do % 2
                qlds = []
                for qq in range(4):
                    qlds.append(p.dma(
                        "sync",
                        pstr[:, qq * 2048:(qq + 1) * 2048].rearrange(
                            "p (a b) -> p a b", a=16),
                        tproj[do * 8192 + qq * 2048:
                              do * 8192 + (qq + 1) * 2048, :].rearrange(
                            "(a p) b -> p a b", p=128),
                        waits=[pstr_mark.get(qq) if isinstance(pstr_mark, dict)
                               else pstr_mark,
                               tern_ready if (tbi == 0 and do == 0) else None,
                               tern_ready2 if (tbi == 0 and do == 0)
                               else None],
                        inc=f"ldpr{qq}"))
                if not isinstance(pstr_mark, dict):
                    pstr_mark = {}
                for ht in range(64):
                    w1 = []
                    if ht % 16 == 0:
                        w1 = [qlds[ht // 16]]
                    if ht == 0:
                        w1 += [bank_free[bank]] + ([xn2_ready[-1],
                                                    xn2_ready[-2]]
                                                   if do == 0 else [])
                    p.emit("tensor",
                           lambda e, b=bank, h_=ht: e.matmul(
                               ps_a[b][:, :],
                               pstr[:, h_ * 128:(h_ + 1) * 128],
                               B[:, h_ * NB:(h_ + 1) * NB],
                               start=(h_ == 0), stop=(h_ == 63)),
                           waits=w1, inc="mmS")
                    if ht % 16 == 15:
                        pstr_mark[ht // 16] = p.mark("mmS")
                mmk = p.mark("mmS")
                x1ld = p.dma("sync", xr[bank][:, :],
                             x1buf[do * 128:(do + 1) * 128,
                                   tbi * NB:(tbi + 1) * NB],
                             waits=[x1r_mark[bank],
                                    p1_x1_dma if (tbi == 0 and do < 2) else None],
                             inc=("ldx0" if bank == 0 else "ldx1b"))
                sv = p.emit("vector", lambda e, b=bank, d_=do: e.scalar_tensor_tensor(
                    out=s_sb[d_][:, :], in0=ps_a[b][:, :], scalar=gamb2[:, :],
                    in1=xr[b][:, :], op0=ALU.mult, op1=ALU.add),
                    waits=[mmk, x1ld], inc="dvS")
                x1r_mark[bank] = sv
                bank_free[bank] = sv
                sb_ = p.emit("vector", lambda e, d_=do: e.tensor_copy(
                    sbf_t[:, :], s_sb[d_][:, :]), waits=[prev_s1m], inc="dvS")
                sq_ = p.emit("scalar", lambda e: e.activation(
                    out=sq_t[:, :], in_=sbf_t[:, :], func=AF.Square),
                    waits=[sb_, prev_s2m], inc="evS")
                w_st1 = [sb_]
                if do == 0:
                    w_st1 += [st_done]
                prev_s1m = p.emit("tensor", lambda e, d_=do: e.matmul(
                    ps_s1[:, :], onescol[:, :], sbf_t[:, :],
                    start=(d_ == 0), stop=(d_ == 15)), waits=w_st1, inc="mmS")
                w_st2 = [sq_]
                if do == 0:
                    w_st2 += [st_done]
                prev_s2m = p.emit("tensor", lambda e, d_=do: e.matmul(
                    ps_s2[:, :], onescol[:, :], sq_t[:, :],
                    start=(d_ == 0), stop=(d_ == 15)), waits=w_st2, inc="mmS")
            stats_done = p.mark("mmS")
            m1 = p.emit("vector", lambda e: e.tensor_scalar(
                out=mu[:, :], in0=ps_s1[:, :], scalar1=float(1.0 / 2048.0),
                scalar2=None, op0=ALU.mult), waits=[stats_done], inc="dvS")
            m2 = p.emit("scalar", lambda e: e.activation(
                out=msq[:, :], in_=mu[:, :], func=AF.Square),
                waits=[m1], inc="evS")
            v1 = p.emit("vector", lambda e: e.tensor_scalar(
                out=varr[:, :], in0=ps_s2[:, :], scalar1=float(1.0 / 2048.0),
                scalar2=None, op0=ALU.mult), inc="dvS")
            st_done = v1
            v2 = p.emit("vector", lambda e: e.tensor_tensor(
                out=varr[:, :], in0=varr[:, :], in1=msq[:, :], op=ALU.subtract),
                waits=[m2], inc="dvS")
            l1 = p.emit("scalar", lambda e: e.activation(
                out=rstd[:, :], in_=varr[:, :], func=AF.Ln, bias=np.float32(1e-5)),
                waits=[v2], inc="evS")
            l2 = p.emit("scalar", lambda e: e.activation(
                out=rstd[:, :], in_=rstd[:, :], func=AF.Exp,
                scale=np.float32(-0.5)), inc="evS")
            a1 = p.emit("vector", lambda e: e.tensor_copy(A_bf[:, :], rstd[:, :]),
                        waits=[l2], inc="dvS")
            b1 = p.emit("vector", lambda e: e.scalar_tensor_tensor(
                out=B_bf[:, :], in0=mu[:, :], scalar=float(-1.0), in1=rstd[:, :],
                op0=ALU.mult, op1=ALU.mult), inc="dvS")
            p.emit("tensor", lambda e: e.matmul(
                ps_b1[:, :], onesrow[:, :], A_bf[:, :], start=True, stop=True),
                waits=[a1, bc_done], inc="mmS")
            cA = p.mark("mmS")
            p.emit("tensor", lambda e: e.matmul(
                ps_b2[:, :], onesrow[:, :], B_bf[:, :], start=True, stop=True),
                waits=[b1], inc="mmS")
            cB = p.mark("mmS")
            u2m = None
            for do in range(16):
                u1 = p.emit("vector", lambda e, d_=do: e.tensor_tensor(
                    out=tmp1[:, :], in0=s_sb[d_][:, :], in1=ps_b1[:, :],
                    op=ALU.mult), waits=[cA, tmp1_dma], inc="dvS")
                u2 = p.emit("vector", lambda e: e.tensor_tensor(
                    out=tmp1[:, :], in0=tmp1[:, :], in1=ps_b2[:, :], op=ALU.add),
                    waits=[cB], inc="dvS")
                u2m = u2
                x2d = p.dma("sync", x2o[do * 128:(do + 1) * 128,
                                        tbi * NB:(tbi + 1) * NB],
                            tmp1[:, :], waits=[u2], inc="st0")
                tmp1_dma = x2d
                u4 = p.emit("vector", lambda e: e.tensor_copy(xnb_t[:, :],
                                                              tmp1[:, :]),
                            waits=[xnb_dma], inc="dvS")
                x2bd = p.dma("sync", x2bo[do * 128:(do + 1) * 128,
                                          tbi * NB:(tbi + 1) * NB],
                            xnb_t[:, :], waits=[u4], inc="wx2b")
                xnb_dma = x2bd
            bc_done = u2m
        p.emit("sync", lambda e: e.nop(),
               waits=[p.mark("st0"), p.mark("wx2b")])
        if mega is None:
            p.realize(nc, sems)
        else:
            mega["sems"].update(sems)
    return nc


def _launch3_inputs(inputs, y_full):
    x = np.asarray(inputs["L0_attn_x"])[0]
    xTfull = np.ascontiguousarray(x.T)
    wfcT = np.ascontiguousarray(np.asarray(inputs["L0_ffn_w_fc"]).T.astype(f32))
    wprT = np.ascontiguousarray(np.asarray(inputs["L0_ffn_w_proj"]).T.astype(f32))
    wfc_t = _tile_lhsT(wfcT, 2048, 8192)
    wproj_t = _tile_lhsT(wprT, 8192, 2048)
    Wo = np.ascontiguousarray(np.asarray(inputs["L0_attn_Wo"]).astype(f32))
    bo = _pack_bias(np.asarray(inputs["L0_attn_bo"]))
    maps = []
    for c in range(NCORE):
        sl = slice(c * TC, (c + 1) * TC)
        maps.append({
            "y_sl": np.ascontiguousarray(y_full[:, sl]),
            "Wo": Wo, "bo": bo,
            "xT": np.ascontiguousarray(xTfull[:, sl]),
            "wfc_t": wfc_t, "wproj_t": wproj_t,
        })
    return maps


def run_launch3(inputs, y_full):
    nc = build_launch3()
    maps = _launch3_inputs(inputs, y_full)
    res = run_bass_kernel_spmd(nc, maps, core_ids=list(range(NCORE)))
    x2 = np.concatenate([res.results[c]["x2o"] for c in range(NCORE)], axis=1)
    x2b = np.concatenate([res.results[c]["x2bo"] for c in range(NCORE)], axis=1)
    return x2, x2b


def build_launch4(mega=None):
    if mega is None:
        nc, io, ns = bass.Bass("TRN2", target_bir_lowering=False), None, ""
        p = Plan()
    else:
        nc, p, io, ns = mega["nc"], mega["p"], mega["io"], mega["ns"]
    p.set_ns(ns)
    x2b = _dram(nc, io, ns, "x2b", [D, TC], BF16, "ExternalInput")
    Wq1 = _dram(nc, io, ns, "Wq1", [D, 64], F32, "ExternalInput")
    Wk1 = _dram(nc, io, ns, "Wk1", [D, 64], F32, "ExternalInput")
    Wb1 = _dram(nc, io, ns, "Wb1", [D, 64], F32, "ExternalInput")
    Wd1 = _dram(nc, io, ns, "Wd1", [D, 64], F32, "ExternalInput")
    Wv1 = _dram(nc, io, ns, "Wv1", [D, 128], F32, "ExternalInput")
    bb1 = _dram(nc, io, ns, "bb1", [64, 1], F32, "ExternalInput")
    bd1 = _dram(nc, io, ns, "bd1", [64, 1], F32, "ExternalInput")
    cmask = _dram(nc, io, ns, "cmask", [128, 128], BF16, "ExternalInput")
    identb = _dram(nc, io, ns, "identb", [128, 128], BF16, "ExternalInput")
    olocal = _dram(nc, io, ns, "olocal", [128, TC], F32, "ExternalOutput")
    qhato = _dram(nc, io, ns, "qhato", [64, 128], BF16, "ExternalOutput")
    summo = _dram(nc, io, ns, "summo", [64, 130], F32, "ExternalOutput")

    NB = 512
    with ExitStack() as ctx:
        e = ctx.enter_context
        A = e(nc.sbuf_tensor(ns + "A", [128, 16 * NB], BF16))
        wl1 = e(nc.sbuf_tensor(ns + "wl1", [128, 16 * 64 * 4], BF16))
        wv1 = e(nc.sbuf_tensor(ns + "wv1", [128, 16 * 128], BF16))
        bb_sb = e(nc.sbuf_tensor(ns + "bb_sb", [64, 1], F32))
        bd_sb = e(nc.sbuf_tensor(ns + "bd_sb", [64, 1], F32))
        q_sb = e(nc.sbuf_tensor(ns + "q_sb", [64, TC], F32))
        k_sb = e(nc.sbuf_tensor(ns + "k_sb", [64, TC], F32))
        zb_sb = e(nc.sbuf_tensor(ns + "zb_sb", [64, TC], F32))
        zd_sb = e(nc.sbuf_tensor(ns + "zd_sb", [64, TC], F32))
        v_sb = e(nc.sbuf_tensor(ns + "v_sb", [128, TC], BF16))
        lnb_sb = e(nc.sbuf_tensor(ns + "lnb_sb", [64, TC], F32))
        kd_sb = e(nc.sbuf_tensor(ns + "kd_sb", [64, TC], F32))
        Lf_sb = e(nc.sbuf_tensor(ns + "Lf_sb", [64, TC], F32))
        zeros64 = e(nc.sbuf_tensor(ns + "zeros64", [64, TC], F32))
        ta64 = e(nc.sbuf_tensor(ns + "ta64", [64, TC], F32))
        tb64 = e(nc.sbuf_tensor(ns + "tb64", [64, TC], F32))
        Lrel = e(nc.sbuf_tensor(ns + "Lrel", [64, 128], F32))
        negc = e(nc.sbuf_tensor(ns + "negc", [64, 1], F32))
        posc = e(nc.sbuf_tensor(ns + "posc", [64, 1], F32))
        Eend = e(nc.sbuf_tensor(ns + "Eend", [64, 1], F32))
        E1 = e(nc.sbuf_tensor(ns + "E1", [64, 128], F32))
        E2 = e(nc.sbuf_tensor(ns + "E2", [64, 128], F32))
        E3 = e(nc.sbuf_tensor(ns + "E3", [64, 128], F32))
        E4 = e(nc.sbuf_tensor(ns + "E4", [64, 128], F32))
        qt_b = e(nc.sbuf_tensor(ns + "qt_b", [64, 128], BF16))
        kt_b = e(nc.sbuf_tensor(ns + "kt_b", [64, 128], BF16))
        qh_b = e(nc.sbuf_tensor(ns + "qh_b", [64, 128], BF16))
        kh_b = e(nc.sbuf_tensor(ns + "kh_b", [64, 128], BF16))
        khtok = e(nc.sbuf_tensor(ns + "khtok", [128, 64], BF16))
        vtok = e(nc.sbuf_tensor(ns + "vtok", [128, 128], BF16))
        ssb1 = e(nc.sbuf_tensor(ns + "ssb1", [128, 128], BF16))
        S_f = e(nc.sbuf_tensor(ns + "S_f", [64, 128], F32))
        S_b = e(nc.sbuf_tensor(ns + "S_b", [64, 128], BF16))
        Ldc = e(nc.sbuf_tensor(ns + "Ldc", [64, 1], F32))
        o_sb = e(nc.sbuf_tensor(ns + "o_sb", [128, TC], F32))
        qhat_sb = e(nc.sbuf_tensor(ns + "qhat_sb", [64, 128], BF16))
        summ_sb = e(nc.sbuf_tensor(ns + "summ_sb", [64, 130], F32))
        cm_sb = e(nc.sbuf_tensor(ns + "cm_sb", [128, 128], BF16))
        id_sb = e(nc.sbuf_tensor(ns + "id_sb", [128, 128], BF16))
        ps_a = [e(nc.psum_tensor(ns + f"ps_a{i}", [128, NB], F32)) for i in range(2)]
        ps_sc = e(nc.psum_tensor(ns + "ps_sc", [128, 128], F32))
        ps_o = e(nc.psum_tensor(ns + "ps_o", [128, 128], F32))
        ps_sd = e(nc.psum_tensor(ns + "ps_sd", [64, 128], F32))
        ps_tr = e(nc.psum_tensor(ns + "ps_tr", [128, 256], BF16))
        sems = _fresh(mega["sem_ctx"] if mega else ctx, nc,
                      [ns + s_ for s_ in ("lda", "ldw", "ldt", "mmS", "dvS",
                                          "evS", "plS", "st0")])

        p.emit("gpsimd", lambda e: e.memset(zeros64[:, :], 0.0))
        p.emit("gpsimd", lambda e: e.memset(S_f[:, :], 0.0))
        p.emit("gpsimd", lambda e: e.memset(S_b[:, :], 0.0))
        cst = p.emit("gpsimd", lambda e: e.memset(Ldc[:, :], 0.0))

        for wi, wt in enumerate((Wq1, Wk1, Wb1, Wd1)):
            p.dma("gpsimd", wl1[:, wi * 1024:(wi + 1) * 1024].rearrange(
                "p (a b) -> p a b", a=16),
                wt.rearrange("(a p) b -> p a b", p=128), inc="ldw")
        p.dma("gpsimd", wv1[:, :].rearrange("p (a b) -> p a b", a=16),
              Wv1.rearrange("(a p) b -> p a b", p=128), inc="ldw")
        w_rdy = p.mark("ldw")
        p.dma("sync", bb_sb[:, :], bb1[:, :], inc="ldt")
        p.dma("sync", bd_sb[:, :], bd1[:, :], inc="ldt")
        p.dma("sync", cm_sb[:, :], cmask[:, :], inc="ldt")
        p.dma("sync", id_sb[:, :], identb[:, :], inc="ldt")
        t_rdy = p.mark("ldt")

        a_free = None
        bank_free = [None, None]
        for tbi in range(4):
            alds = []
            for dm in range(16):
                alds.append(p.dma("sync", A[:, dm * NB:(dm + 1) * NB],
                                  x2b[dm * 128:(dm + 1) * 128,
                                      tbi * NB:(tbi + 1) * NB],
                                  waits=[a_free] if dm == 0 else [], inc="lda"))
            a_rdy = alds[-1]
            for zi in range(5):
                bank = zi % 2
                for dm in range(16):
                    w1 = []
                    if dm == 0:
                        w1 = [a_rdy, w_rdy, bank_free[bank]]
                    if zi < 4:
                        p.emit("tensor", lambda e, b=bank, d_=dm, z_=zi: e.matmul(
                            ps_a[b][0:64, :],
                            wl1[:, (z_ * 16 + d_) * 64:(z_ * 16 + d_) * 64 + 64],
                            A[:, d_ * NB:(d_ + 1) * NB],
                            start=(d_ == 0), stop=(d_ == 15)),
                            waits=w1, inc="mmS" if dm == 15 else None)
                    else:
                        p.emit("tensor", lambda e, b=bank, d_=dm: e.matmul(
                            ps_a[b][:, :],
                            wv1[:, d_ * 128:(d_ + 1) * 128],
                            A[:, d_ * NB:(d_ + 1) * NB],
                            start=(d_ == 0), stop=(d_ == 15)),
                            waits=w1, inc="mmS" if dm == 15 else None)
                mmk = p.mark("mmS")
                if zi == 4:
                    a_free = mmk
                if zi == 0:
                    ev = p.emit("vector", lambda e, b=bank, t_=tbi: e.tensor_copy(
                        q_sb[:, t_ * NB:(t_ + 1) * NB], ps_a[b][0:64, :]),
                        waits=[mmk], inc="dvS")
                elif zi == 1:
                    ev = p.emit("vector", lambda e, b=bank, t_=tbi: e.tensor_copy(
                        k_sb[:, t_ * NB:(t_ + 1) * NB], ps_a[b][0:64, :]),
                        waits=[mmk], inc="dvS")
                elif zi == 2:
                    ev = p.emit("vector", lambda e, b=bank, t_=tbi: e.tensor_scalar(
                        out=zb_sb[:, t_ * NB:(t_ + 1) * NB], in0=ps_a[b][0:64, :],
                        scalar1=bb_sb[:, :], scalar2=None, op0=ALU.add),
                        waits=[mmk, t_rdy], inc="dvS")
                elif zi == 3:
                    ev = p.emit("vector", lambda e, b=bank, t_=tbi: e.tensor_scalar(
                        out=zd_sb[:, t_ * NB:(t_ + 1) * NB], in0=ps_a[b][0:64, :],
                        scalar1=bd_sb[:, :], scalar2=None, op0=ALU.add),
                        waits=[mmk, t_rdy], inc="dvS")
                else:
                    ev = p.emit("vector", lambda e, b=bank, t_=tbi: e.tensor_copy(
                        v_sb[:, t_ * NB:(t_ + 1) * NB], ps_a[b][:, :]),
                        waits=[mmk], inc="dvS")
                bank_free[bank] = ev
        proj_done_dv = p.mark("dvS")

        r_ = p.emit("scalar", lambda e: e.activation(
            out=ta64[:, :], in_=zd_sb[:, :], func=AF.Relu),
            waits=[proj_done_dv], inc="evS")
        x_ = p.emit("scalar", lambda e: e.activation(
            out=tb64[:, :], in_=zd_sb[:, :], func=AF.Abs), inc="evS")
        x_ = p.emit("scalar", lambda e: e.activation(
            out=tb64[:, :], in_=tb64[:, :], func=AF.Exp, scale=np.float32(-1.0)),
            inc="evS")
        x_ = p.emit("scalar", lambda e: e.activation(
            out=tb64[:, :], in_=tb64[:, :], func=AF.Ln, bias=np.float32(1.0)),
            inc="evS")
        d_ = p.emit("vector", lambda e: e.tensor_tensor(
            out=ta64[:, :], in0=ta64[:, :], in1=tb64[:, :], op=ALU.add),
            waits=[x_], inc="dvS")
        kd_ = p.emit("vector", lambda e: e.tensor_tensor(
            out=kd_sb[:, :], in0=k_sb[:, :], in1=ta64[:, :], op=ALU.mult),
            inc="dvS")
        m_ = p.emit("vector", lambda e: e.tensor_scalar(
            out=ta64[:, :], in0=zb_sb[:, :], scalar1=float(0.0), scalar2=None,
            op0=ALU.min), inc="dvS")
        y_ = p.emit("scalar", lambda e: e.activation(
            out=tb64[:, :], in_=zb_sb[:, :], func=AF.Abs), waits=[d_], inc="evS")
        y_ = p.emit("scalar", lambda e: e.activation(
            out=tb64[:, :], in_=tb64[:, :], func=AF.Exp, scale=np.float32(-1.0)),
            inc="evS")
        y_ = p.emit("scalar", lambda e: e.activation(
            out=tb64[:, :], in_=tb64[:, :], func=AF.Ln, bias=np.float32(1.0)),
            inc="evS")
        lb_ = p.emit("vector", lambda e: e.tensor_tensor(
            out=lnb_sb[:, :], in0=ta64[:, :], in1=tb64[:, :], op=ALU.subtract),
            waits=[y_], inc="dvS")
        sc_ = p.emit("vector", lambda e: e.tensor_tensor_scan(
            Lf_sb[:, :], lnb_sb[:, :], zeros64[:, :], 0.0, ALU.add, ALU.add),
            inc="dvS")

        CH = 64
        prev_ssb_mm = None
        prev_tr_ev = None
        prev_o_ev = None
        prev_sd_rd = None
        prev_sc_ev = None
        prev_Sb = None
        for m in range(TC // CH):
            c0 = m * CH
            if m == 0:
                lr = p.emit("vector", lambda e: e.tensor_copy(
                    Lrel[:, 0:CH], Lf_sb[:, 0:CH]), waits=[sc_], inc="dvS")
            else:
                lr = p.emit("vector", lambda e, c_=c0: e.tensor_scalar(
                    out=Lrel[:, 0:CH], in0=Lf_sb[:, c_:c_ + CH],
                    scalar1=Lf_sb[:, c_ - 1:c_], scalar2=None,
                    op0=ALU.subtract), inc="dvS")
            n1 = p.emit("vector", lambda e: e.tensor_scalar(
                out=negc[:, :], in0=Lrel[:, CH - 1:CH], scalar1=float(-0.5),
                scalar2=None, op0=ALU.mult), inc="dvS")
            n2 = p.emit("vector", lambda e: e.tensor_scalar(
                out=posc[:, :], in0=Lrel[:, CH - 1:CH], scalar1=float(0.5),
                scalar2=None, op0=ALU.mult), inc="dvS")
            ee = p.emit("scalar", lambda e: e.activation(
                out=Eend[:, :], in_=Lrel[:, CH - 1:CH], func=AF.Exp),
                waits=[n2], inc="evS")
            e1 = p.emit("scalar", lambda e: e.activation(
                out=E1[:, 0:CH], in_=Lrel[:, 0:CH], func=AF.Exp, bias=negc[:, :]),
                inc="evS")
            e2 = p.emit("scalar", lambda e: e.activation(
                out=E2[:, 0:CH], in_=Lrel[:, 0:CH], func=AF.Exp,
                scale=np.float32(-1.0), bias=posc[:, :]), inc="evS")
            e3 = p.emit("scalar", lambda e: e.activation(
                out=E3[:, 0:CH], in_=Lrel[:, 0:CH], func=AF.Exp), inc="evS")
            e4 = p.emit("scalar", lambda e: e.activation(
                out=E4[:, 0:CH], in_=Lrel[:, 0:CH], func=AF.Exp,
                scale=np.float32(-1.0), bias=Lrel[:, CH - 1:CH]), inc="evS")
            qt = p.emit("vector", lambda e, c_=c0: e.tensor_tensor(
                out=qt_b[:, 0:CH], in0=q_sb[:, c_:c_ + CH], in1=E1[:, 0:CH],
                op=ALU.mult), waits=[e1, prev_ssb_mm], inc="dvS")
            kt = p.emit("vector", lambda e, c_=c0: e.tensor_tensor(
                out=kt_b[:, 0:CH], in0=kd_sb[:, c_:c_ + CH], in1=E2[:, 0:CH],
                op=ALU.mult), waits=[e2], inc="dvS")
            qh = p.emit("vector", lambda e, c_=c0: e.tensor_tensor(
                out=qh_b[:, 0:CH], in0=q_sb[:, c_:c_ + CH], in1=E3[:, 0:CH],
                op=ALU.mult), waits=[e3], inc="dvS")
            kh = p.emit("vector", lambda e, c_=c0: e.tensor_tensor(
                out=kh_b[:, 0:CH], in0=kd_sb[:, c_:c_ + CH], in1=E4[:, 0:CH],
                op=ALU.mult), waits=[e4], inc="dvS")
            if m in (0, 1):
                p.emit("vector", lambda e, m_=m: e.tensor_copy(
                    qhat_sb[:, m_ * CH:(m_ + 1) * CH], qh_b[:, 0:CH]),
                    inc="dvS")
            sc_mm = p.emit("tensor", lambda e: e.matmul(
                ps_sc[0:CH, 0:CH], kt_b[:, 0:CH], qt_b[:, 0:CH],
                start=True, stop=True),
                waits=[kt, qt, prev_sc_ev], inc="mmS")
            sse = p.emit("vector", lambda e: e.tensor_tensor(
                out=ssb1[0:CH, 0:CH], in0=ps_sc[0:CH, 0:CH],
                in1=cm_sb[0:CH, 0:CH], op=ALU.mult),
                waits=[sc_mm, t_rdy, prev_ssb_mm], inc="dvS")
            prev_sc_ev = sse
            tr1 = p.emit("tensor", lambda e: e.transpose(
                ps_tr[0:CH, 0:64], kh_b[:, 0:CH], id_sb[0:64, 0:64]),
                waits=[kh, prev_tr_ev], inc="mmS")
            tr2 = p.emit("tensor", lambda e, c_=c0: e.transpose(
                ps_tr[0:CH, 64:192], v_sb[:, c_:c_ + CH], id_sb[:, :]),
                waits=[proj_done_dv], inc="mmS")
            te1 = p.emit("vector", lambda e: e.tensor_copy(khtok[0:CH, :],
                                                           ps_tr[0:CH, 0:64]),
                         waits=[tr1], inc="dvS")
            te2 = p.emit("vector", lambda e: e.tensor_copy(vtok[0:CH, :],
                                                           ps_tr[0:CH, 64:192]),
                         waits=[tr2], inc="dvS")
            prev_tr_ev = te2
            om1 = p.emit("tensor", lambda e: e.matmul(
                ps_o[:, 0:CH], vtok[0:CH, :], ssb1[0:CH, 0:CH],
                start=True, stop=False),
                waits=[te2, sse, prev_o_ev], inc="mmS")
            om2 = p.emit("tensor", lambda e: e.matmul(
                ps_o[:, 0:CH], S_b[:, :], qh_b[:, 0:CH],
                start=False, stop=True),
                waits=[qh, prev_Sb], inc="mmS")
            prev_ssb_mm = om2
            oe = p.emit("scalar", lambda e, c_=c0: e.activation(
                out=o_sb[:, c_:c_ + CH], in_=ps_o[:, 0:CH], func=AF.Copy,
                scale=1.0), waits=[om2], inc="evS")
            prev_o_ev = oe
            sd = p.emit("tensor", lambda e: e.matmul(
                ps_sd[:, :], khtok[0:CH, :], vtok[0:CH, :],
                start=True, stop=True),
                waits=[te1, prev_sd_rd], inc="mmS")
            su = p.emit("vector", lambda e: e.scalar_tensor_tensor(
                out=S_f[:, :], in0=S_f[:, :], scalar=Eend[:, :],
                in1=ps_sd[:, :], op0=ALU.mult, op1=ALU.add),
                waits=[sd, ee, om2], inc="dvS")
            prev_sd_rd = su
            sb2 = p.emit("vector", lambda e: e.tensor_copy(S_b[:, :], S_f[:, :]),
                         inc="dvS")
            prev_Sb = sb2
            lda_ = p.emit("vector", lambda e: e.tensor_tensor(
                out=Ldc[:, :], in0=Ldc[:, :], in1=Lrel[:, CH - 1:CH],
                op=ALU.add), inc="dvS")
            if m == 0:
                p.emit("vector", lambda e: e.tensor_copy(
                    summ_sb[:, 129:130], Lrel[:, CH - 1:CH]), inc="dvS")

        fin1 = p.emit("vector", lambda e: e.tensor_copy(summ_sb[:, 0:128],
                                                        S_f[:, :]), inc="dvS")
        fin2 = p.emit("vector", lambda e: e.tensor_copy(summ_sb[:, 128:129],
                                                        Ldc[:, :]), inc="dvS")
        p.dma("sync", olocal[:, :], o_sb[:, :],
              waits=[p.mark("evS")], inc="st0")
        p.dma("sync", qhato[:, :], qhat_sb[:, :],
              waits=[p.mark("dvS")], inc="st0")
        p.dma("sync", summo[:, :], summ_sb[:, :], waits=[fin2], inc="st0")
        p.emit("sync", lambda e: e.nop(), waits=[p.mark("st0")])
        if mega is None:
            p.realize(nc, sems)
        else:
            mega["sems"].update(sems)
    return nc


def _launch4_inputs(inputs, x2b_full):
    maps = []
    cm = np.ascontiguousarray(_causal128())
    idn = np.ascontiguousarray(np.eye(128, dtype=f32).astype(BF))
    for c in range(NCORE):
        sl = slice(c * TC, (c + 1) * TC)
        m = {"x2b": np.ascontiguousarray(x2b_full[:, sl]),
             "cmask": cm, "identb": idn}
        for nm in ("Wq", "Wk", "Wb", "Wd", "Wv"):
            m[nm + "1"] = np.ascontiguousarray(
                np.asarray(inputs[f"L1_attn_{nm}"]).astype(f32))
        m["bb1"] = np.ascontiguousarray(
            np.asarray(inputs["L1_attn_bb"]).astype(f32).reshape(64, 1))
        m["bd1"] = np.ascontiguousarray(
            np.asarray(inputs["L1_attn_bd"]).astype(f32).reshape(64, 1))
        maps.append(m)
    return maps


def run_launch4(inputs, x2b_full):
    nc = build_launch4()
    maps = _launch4_inputs(inputs, x2b_full)
    res = run_bass_kernel_spmd(nc, maps, core_ids=list(range(NCORE)))
    ol = np.concatenate([res.results[c]["olocal"] for c in range(NCORE)], axis=1)
    qh = np.stack([res.results[c]["qhato"] for c in range(NCORE)])
    sm = np.stack([res.results[c]["summo"] for c in range(NCORE)])
    return ol, qh, sm


def build_launch5(mega=None):
    if mega is None:
        nc, io, ns, pre = bass.Bass("TRN2", target_bir_lowering=False), None, "", None
        p = Plan()
    else:
        nc, p, io, ns = mega["nc"], mega["p"], mega["io"], mega["ns"]
        pre = mega.get("pre")
    p.set_ns(ns)
    olocal = _dram(nc, io, ns, "olocal", [128, TC], F32, "ExternalInput")
    qhat = _dram(nc, io, ns, "qhat", [64, 128], BF16, "ExternalInput")
    summ_all = _dram(nc, io, ns, "summ_all", [512, 130], F32, "ExternalInput")
    own_sum = _dram(nc, io, ns, "own_sum", [64, 130], F32, "ExternalInput")
    ltm = _dram(nc, io, ns, "ltm", [64, 8], F32, "ExternalInput")
    x2 = _dram(nc, io, ns, "x2", [D, TC], F32, "ExternalInput")
    Wo1 = _dram(nc, io, ns, "Wo1", [128, D], F32, "ExternalInput")
    w1fc_t = _dram(nc, io, ns, "w1fc_t", [40 * 16 * 128, 128], F32, "ExternalInput")
    w1proj_t = _dram(nc, io, ns, "w1proj_t", [16 * 40 * 128, 128], F32, "ExternalInput")
    outx = _dram(nc, io, ns, "outx", [D, TC], F32, "ExternalOutput")
    x3buf = _dram(nc, io, ns, "x3buf", [D, TC], F32, "Internal")
    t1fc = _dram(nc, io, ns, "t1fc", [40 * 16 * 128, 128], BF16, "Internal")
    t1proj = _dram(nc, io, ns, "t1proj", [16 * 40 * 128, 128], BF16, "Internal")

    NB = 512
    CH = 64
    with ExitStack() as ctx:
        e = ctx.enter_context
        A = e(nc.sbuf_tensor(ns + "A", [128, 16 * NB], BF16))
        Bh = e(nc.sbuf_tensor(ns + "Bh", [128, 40 * NB], BF16))
        s_sb = [e(nc.sbuf_tensor(ns + f"s_sb{i}", [128, NB], F32)) for i in range(16)]
        xr = [e(nc.sbuf_tensor(ns + f"xr{i}", [128, NB], F32)) for i in range(2)]
        if pre is None:
            wstr = [e(nc.sbuf_tensor(ns + f"wstr{i}", [128, 2048], F32)) for i in range(2)]
            aa_t = e(nc.sbuf_tensor(ns + "aa_t", [128, 2048], F32))
            msk_t = e(nc.sbuf_tensor(ns + "msk_t", [128, 2048], BF16))
            sgn_t = e(nc.sbuf_tensor(ns + "sgn_t", [128, 2048], BF16))
            tstr = [e(nc.sbuf_tensor(ns + f"tstr{i}", [128, 2048], BF16)) for i in range(2)]
        else:
            tstr = pre["tstr"]
        pstr = e(nc.sbuf_tensor(ns + "pstr", [128, 5120], BF16))
        rt_t = e(nc.sbuf_tensor(ns + "rt_t", [128, NB], F32))
        sbf_t = e(nc.sbuf_tensor(ns + "sbf_t", [128, NB], BF16))
        xnb_t = e(nc.sbuf_tensor(ns + "xnb_t", [128, NB], BF16))
        sq_t = e(nc.sbuf_tensor(ns + "sq_t", [128, NB], BF16))
        tmp1 = e(nc.sbuf_tensor(ns + "tmp1", [128, NB], F32))
        onescol = e(nc.sbuf_tensor(ns + "onescol", [128, 1], BF16))
        onesrow = e(nc.sbuf_tensor(ns + "onesrow", [1, 128], BF16))
        if pre is None:
            onescol_f = e(nc.sbuf_tensor(ns + "onescol_f", [128, 1], F32))
            onesrow_f = e(nc.sbuf_tensor(ns + "onesrow_f", [1, 128], F32))
            wacc = e(nc.sbuf_tensor(ns + "wacc", [128, 1], F32))
            wacc2 = e(nc.sbuf_tensor(ns + "wacc2", [128, 1], F32))
            red1 = e(nc.sbuf_tensor(ns + "red1", [128, 1], F32))
            sc1 = e(nc.sbuf_tensor(ns + "sc1", [1, 4], F32))
            thrb = e(nc.sbuf_tensor(ns + "thrb", [128, 1], F32))
            gamb = e(nc.sbuf_tensor(ns + "gamb", [128, 1], F32))
            thrb2 = e(nc.sbuf_tensor(ns + "thrb2", [128, 1], F32))
            gamb2 = e(nc.sbuf_tensor(ns + "gamb2", [128, 1], F32))
        else:
            gamb, thrb = pre["gamb"], pre["thrb"]
            gamb2, thrb2 = pre["gamb2"], pre["thrb2"]
        mu = e(nc.sbuf_tensor(ns + "mu", [1, NB], F32))
        msq = e(nc.sbuf_tensor(ns + "msq", [1, NB], F32))
        varr = e(nc.sbuf_tensor(ns + "varr", [1, NB], F32))
        rstd = e(nc.sbuf_tensor(ns + "rstd", [1, NB], F32))
        A_bf = e(nc.sbuf_tensor(ns + "A_bf", [1, NB], BF16))
        B_bf = e(nc.sbuf_tensor(ns + "B_bf", [1, NB], BF16))
        o_sb = e(nc.sbuf_tensor(ns + "o_sb", [128, TC], F32))
        o_b = e(nc.sbuf_tensor(ns + "o_b", [128, TC], BF16))
        wo1_sb = e(nc.sbuf_tensor(ns + "wo1_sb", [128, D], BF16))
        gs = e(nc.sbuf_tensor(ns + "gs", [64, 8 * 130], F32))
        own_sb = e(nc.sbuf_tensor(ns + "own_sb", [64, 130], F32))
        ltm_sb = e(nc.sbuf_tensor(ns + "ltm_sb", [64, 8], F32))
        qh_sb = e(nc.sbuf_tensor(ns + "qh_sb", [64, 128], BF16))
        qh1s = e(nc.sbuf_tensor(ns + "qh1s", [64, CH], BF16))
        S_in = e(nc.sbuf_tensor(ns + "S_in", [64, 128], F32))
        S_in_b = e(nc.sbuf_tensor(ns + "S_in_b", [64, 128], BF16))
        Tacc = e(nc.sbuf_tensor(ns + "Tacc", [64, 1], F32))
        eT = e(nc.sbuf_tensor(ns + "eT", [64, 1], F32))
        fvec = e(nc.sbuf_tensor(ns + "fvec", [64, 1], F32))
        ldm = e(nc.sbuf_tensor(ns + "ldm", [64, 1], F32))
        eP = e(nc.sbuf_tensor(ns + "eP", [64, 1], F32))
        ps_a = [e(nc.psum_tensor(ns + f"ps_a{i}", [128, NB], F32)) for i in range(2)]
        ps_s1 = e(nc.psum_tensor(ns + "ps_s1", [1, NB], F32))
        ps_s2 = e(nc.psum_tensor(ns + "ps_s2", [1, NB], F32))
        ps_b1 = e(nc.psum_tensor(ns + "ps_b1", [128, NB], F32))
        ps_b2 = e(nc.psum_tensor(ns + "ps_b2", [128, NB], F32))
        ps_c = e(nc.psum_tensor(ns + "ps_c", [128, CH], F32))
        if pre is None:
            ps_sc = e(nc.psum_tensor(ns + "ps_sc", [1, 8], F32))
        sems = _fresh(mega["sem_ctx"] if mega else ctx, nc,
                      [ns + s_ for s_ in ("lda", "ldw", "ldx0", "ldx1b", "ldt",
                                          "ldstr0", "ldstr1", "ldpr0", "ldpr1",
                                          "ldpr2", "ldpr3", "wx3",
                                          "mmS", "dvS", "evS", "plS", "st0",
                                          "stw0", "stw1")])

        p.emit("gpsimd", lambda e: e.memset(onescol[:, :], 1.0))
        p.emit("gpsimd", lambda e: e.memset(onesrow[:, :], 1.0))
        if pre is None:
            p.emit("gpsimd", lambda e: e.memset(onescol_f[:, :], 1.0))
            p.emit("gpsimd", lambda e: e.memset(onesrow_f[:, :], 1.0))
        p.emit("gpsimd", lambda e: e.memset(Tacc[:, :], 0.0))
        ones_ready = p.emit("gpsimd", lambda e: e.memset(S_in[:, :], 0.0))

        old = p.dma("sync", o_sb[:, :], olocal[:, :], inc="ldt")
        p.dma("sync", qh_sb[:, :], qhat[:, :], inc="ldt")
        for b in range(8):
            p.dma("sync", gs[:, b * 130:(b + 1) * 130],
                  summ_all[b * 64:(b + 1) * 64, :], inc="ldt")
        p.dma("sync", own_sb[:, :], own_sum[:, :], inc="ldt")
        p.dma("sync", ltm_sb[:, :], ltm[:, :], inc="ldt")
        wo1ld = p.dma("gpsimd", wo1_sb[:, :], Wo1[:, :], inc="ldw")
        t_rdy = p.mark("ldt")

        if pre is None:
            macc = {}
            for wi, (wt, acc) in enumerate(((w1fc_t, wacc), (w1proj_t, wacc2))):
                for i in range(40):
                    srcv = wt[i * 2048:(i + 1) * 2048, :].rearrange(
                        "(a p) b -> p a b", p=128)
                    ld = p.dma("sync",
                               wstr[wi][:, :].rearrange("p (a b) -> p a b", a=16),
                               srcv, waits=[macc.get((wi, i - 1))],
                               inc=f"ldstr{wi}")
                    p.emit("vector", lambda e, w_=wi: e.tensor_reduce(
                        out=red1[:, :], in_=wstr[w_][:, :], axis=AX.X,
                        op=ALU.add, apply_absolute_value=True), waits=[ld])
                    ac = p.emit("vector", lambda e, a_=acc, first=(i == 0): (
                        e.tensor_copy(a_[:, :], red1[:, :]) if first else
                        e.tensor_tensor(out=a_[:, :], in0=a_[:, :], in1=red1[:, :],
                                        op=ALU.add)))
                    macc[(wi, i)] = ac

            def scalar_reduce(acc, gbuf, tbuf, denom, col, ps_big):
                t = p.emit("tensor", lambda e, a_=acc, c_=col: e.matmul(
                    ps_sc[0:1, c_:c_ + 1], onescol_f[:, :], a_[:, :],
                    start=True, stop=True),
                    waits=[macc[(0, 39)], macc[(1, 39)], ones_ready], inc="mmS")
                g1 = p.emit("vector", lambda e, c_=col: e.tensor_scalar(
                    out=sc1[0:1, c_:c_ + 1], in0=ps_sc[0:1, c_:c_ + 1],
                    scalar1=float(1.0 / denom), scalar2=float(1e-5),
                    op0=ALU.mult, op1=ALU.max), waits=[t])
                t2_ = p.emit("vector", lambda e, c_=col: e.tensor_scalar(
                    out=sc1[0:1, c_ + 1:c_ + 2], in0=sc1[0:1, c_:c_ + 1],
                    scalar1=float(0.7), scalar2=None, op0=ALU.mult))
                bg = p.emit("tensor", lambda e, c_=col, ps=ps_big: e.matmul(
                    ps[:, 0:2], onesrow_f[:, :], sc1[0:1, c_:c_ + 2],
                    start=True, stop=True), waits=[t2_], inc="mmS")
                c1 = p.emit("scalar", lambda e, g_=gbuf, ps=ps_big: e.activation(
                    out=g_[:, :], in_=ps[:, 0:1], func=AF.Copy, scale=1.0),
                    waits=[bg], inc="evS")
                c2 = p.emit("scalar", lambda e, t_=tbuf, ps=ps_big: e.activation(
                    out=t_[:, :], in_=ps[:, 1:2], func=AF.Copy, scale=1.0),
                    inc="evS")
                return (g1, c2)
            thr_ready = scalar_reduce(wacc, gamb, thrb, 5120.0 * 2048.0, 0, ps_b1)
            thr2_ready = scalar_reduce(wacc2, gamb2, thrb2, 5120.0 * 2048.0, 2, ps_b2)

            tern_mark = {}
            for wi, (wt, dst, tb_buf, nch) in enumerate(
                    ((w1fc_t, t1fc, thrb, 40), (w1proj_t, t1proj, thrb2, 40))):
                thr_m = thr_ready[1] if wi == 0 else thr2_ready[1]
                for j in range(nch):
                    buf = j % 2
                    srcv = wt[j * 2048:(j + 1) * 2048, :].rearrange(
                        "(a p) b -> p a b", p=128)
                    ld = p.dma("sync",
                               wstr[buf][:, :].rearrange("p (a b) -> p a b", a=16),
                               srcv,
                               waits=[tern_mark.get(("sgn", wi, j - 2)),
                                      macc[(wi, 39)] if j < 2 else None,
                                      macc[(1, 39)] if (wi == 0 and j < 2) else None],
                               inc=f"ldstr{buf}")
                    a_ = p.emit("scalar", lambda e, b=buf: e.activation(
                        out=aa_t[:, :], in_=wstr[b][:, :], func=AF.Abs),
                        waits=[ld, thr_m], inc="evS")
                    m_ = p.emit("vector", lambda e, t_=tb_buf: e.tensor_scalar(
                        out=msk_t[:, :], in0=aa_t[:, :], scalar1=t_[:, :],
                        scalar2=None, op0=ALU.is_gt), waits=[a_], inc="dvS")
                    g_ = p.emit("scalar", lambda e, b=buf: e.activation(
                        out=sgn_t[:, :], in_=wstr[b][:, :], func=AF.Sign),
                        waits=[m_], inc="evS")
                    tern_mark[("sgn", wi, j)] = g_
                    t_ = p.emit("gpsimd", lambda e, b=buf: e.tensor_tensor(
                        out=tstr[b][:, :], in0=msk_t[:, :], in1=sgn_t[:, :],
                        op=ALU.mult),
                        waits=[g_, tern_mark.get(("st", wi, j - 2)),
                               tern_mark.get(("st", wi - 1, 38 + j))
                               if j < 2 else None],
                        inc="plS")
                    stm = p.dma("sync", dst[j * 2048:(j + 1) * 2048, :].rearrange(
                        "(a p) b -> p a b", p=128),
                        tstr[buf][:, :].rearrange("p (a b) -> p a b", a=16),
                        waits=[t_], inc=f"stw{buf}")
                    tern_mark[("st", wi, j)] = stm
            tern_ready = p.mark("stw0")
            tern_ready2 = p.mark("stw1")
        else:
            thr_ready, thr2_ready = pre["thr_ready"], pre["thr2_ready"]
            tern_ready, tern_ready2 = pre["tern_ready"], pre["tern_ready2"]

        sin_m = None
        for b in range(7, -1, -1):
            x1_ = p.emit("scalar", lambda e: e.activation(
                out=eT[:, :], in_=Tacc[:, :], func=AF.Exp),
                waits=[t_rdy, ones_ready, sin_m], inc="evS")
            x2_ = p.emit("vector", lambda e, b_=b: e.tensor_scalar(
                out=fvec[:, :], in0=eT[:, :], scalar1=ltm_sb[:, b_:b_ + 1],
                scalar2=None, op0=ALU.mult), waits=[x1_], inc="dvS")
            x3_ = p.emit("vector", lambda e, b_=b: e.scalar_tensor_tensor(
                out=S_in[:, :], in0=gs[:, b_ * 130:b_ * 130 + 128],
                scalar=fvec[:, :], in1=S_in[:, :], op0=ALU.mult, op1=ALU.add),
                inc="dvS")
            x4_ = p.emit("vector", lambda e, b_=b: e.tensor_scalar(
                out=ldm[:, :], in0=gs[:, b_ * 130 + 128:b_ * 130 + 129],
                scalar1=ltm_sb[:, b_:b_ + 1], scalar2=None, op0=ALU.mult),
                inc="dvS")
            sin_m = p.emit("vector", lambda e: e.tensor_tensor(
                out=Tacc[:, :], in0=Tacc[:, :], in1=ldm[:, :], op=ALU.add),
                inc="dvS")
        sb_ = p.emit("vector", lambda e: e.tensor_copy(S_in_b[:, :], S_in[:, :]),
                     inc="dvS")
        ep_ = p.emit("scalar", lambda e: e.activation(
            out=eP[:, :], in_=own_sb[:, 129:130], func=AF.Exp),
            waits=[t_rdy], inc="evS")
        q1s = p.emit("vector", lambda e: e.tensor_scalar(
            out=qh1s[:, :], in0=qh_sb[:, CH:2 * CH], scalar1=eP[:, :],
            scalar2=None, op0=ALU.mult), waits=[ep_], inc="dvS")
        cm1 = p.emit("tensor", lambda e: e.matmul(
            ps_c[:, 0:CH], S_in_b[:, :], qh_sb[:, 0:CH],
            start=True, stop=True), waits=[sb_, t_rdy], inc="mmS")
        co1 = p.emit("vector", lambda e: e.tensor_tensor(
            out=o_sb[:, 0:CH], in0=o_sb[:, 0:CH], in1=ps_c[:, 0:CH],
            op=ALU.add), waits=[cm1], inc="dvS")
        cm2 = p.emit("tensor", lambda e: e.matmul(
            ps_c[:, 0:CH], S_in_b[:, :], qh1s[:, :],
            start=True, stop=True), waits=[q1s, co1], inc="mmS")
        co2 = p.emit("vector", lambda e: e.tensor_tensor(
            out=o_sb[:, CH:2 * CH], in0=o_sb[:, CH:2 * CH], in1=ps_c[:, 0:CH],
            op=ALU.add), waits=[cm2], inc="dvS")
        ob_ = p.emit("vector", lambda e: e.tensor_copy(o_b[:, :], o_sb[:, :]),
                     inc="dvS")

        st_done = None
        bc_done = None
        bank_free = [None, None]
        last_xr = [None, None]
        tmp1_dma = None
        prev_s1m = prev_s2m = None
        pstr_mark = None
        strip_mark = {}
        for tbi in range(4):
            for do in range(16):
                bank = do % 2
                p.emit("tensor", lambda e, b=bank, d_=do, t_=tbi: e.matmul(
                    ps_a[b][:, :], wo1_sb[:, d_ * 128:(d_ + 1) * 128],
                    o_b[:, t_ * NB:(t_ + 1) * NB], start=True, stop=True),
                    waits=[ob_, wo1ld, bank_free[bank]], inc="mmS")
                mmk = p.mark("mmS")
                xld = p.dma("sync", xr[bank][:, :],
                            x2[do * 128:(do + 1) * 128, tbi * NB:(tbi + 1) * NB],
                            waits=[last_xr[bank]],
                            inc=("ldx0" if bank == 0 else "ldx1b"))
                sv = p.emit("vector", lambda e, b=bank, d_=do: e.tensor_tensor(
                    out=s_sb[d_][:, :], in0=ps_a[b][:, :], in1=xr[b][:, :],
                    op=ALU.add), waits=[mmk, xld], inc="dvS")
                last_xr[bank] = sv
                bank_free[bank] = sv
                sb2 = p.emit("vector", lambda e, d_=do: e.tensor_copy(
                    sbf_t[:, :], s_sb[d_][:, :]), waits=[prev_s1m], inc="dvS")
                sq_ = p.emit("scalar", lambda e: e.activation(
                    out=sq_t[:, :], in_=sbf_t[:, :], func=AF.Square),
                    waits=[sb2, prev_s2m], inc="evS")
                w_st1 = [sb2, ones_ready]
                if do == 0:
                    w_st1 += [st_done]
                prev_s1m = p.emit("tensor", lambda e, d_=do: e.matmul(
                    ps_s1[:, :], onescol[:, :], sbf_t[:, :],
                    start=(d_ == 0), stop=(d_ == 15)), waits=w_st1, inc="mmS")
                w_st2 = [sq_]
                if do == 0:
                    w_st2 += [st_done]
                prev_s2m = p.emit("tensor", lambda e, d_=do: e.matmul(
                    ps_s2[:, :], onescol[:, :], sq_t[:, :],
                    start=(d_ == 0), stop=(d_ == 15)), waits=w_st2, inc="mmS")
            stats_done = p.mark("mmS")

            def ln_block(nfeat, eps, wait0):
                m1 = p.emit("vector", lambda e: e.tensor_scalar(
                    out=mu[:, :], in0=ps_s1[:, :], scalar1=float(1.0 / nfeat),
                    scalar2=None, op0=ALU.mult), waits=[wait0], inc="dvS")
                m2 = p.emit("scalar", lambda e: e.activation(
                    out=msq[:, :], in_=mu[:, :], func=AF.Square),
                    waits=[m1], inc="evS")
                v1 = p.emit("vector", lambda e: e.tensor_scalar(
                    out=varr[:, :], in0=ps_s2[:, :], scalar1=float(1.0 / nfeat),
                    scalar2=None, op0=ALU.mult), inc="dvS")
                v2 = p.emit("vector", lambda e: e.tensor_tensor(
                    out=varr[:, :], in0=varr[:, :], in1=msq[:, :],
                    op=ALU.subtract), waits=[m2], inc="dvS")
                l1 = p.emit("scalar", lambda e, e_=eps: e.activation(
                    out=rstd[:, :], in_=varr[:, :], func=AF.Ln,
                    bias=np.float32(e_)), waits=[v2], inc="evS")
                l2 = p.emit("scalar", lambda e: e.activation(
                    out=rstd[:, :], in_=rstd[:, :], func=AF.Exp,
                    scale=np.float32(-0.5)), inc="evS")
                a1 = p.emit("vector", lambda e: e.tensor_copy(A_bf[:, :],
                                                              rstd[:, :]),
                            waits=[l2], inc="dvS")
                b1 = p.emit("vector", lambda e: e.scalar_tensor_tensor(
                    out=B_bf[:, :], in0=mu[:, :], scalar=float(-1.0),
                    in1=rstd[:, :], op0=ALU.mult, op1=ALU.mult), inc="dvS")
                p.emit("tensor", lambda e: e.matmul(
                    ps_b1[:, :], onesrow[:, :], A_bf[:, :], start=True,
                    stop=True), waits=[a1, bc_done], inc="mmS")
                cA = p.mark("mmS")
                p.emit("tensor", lambda e: e.matmul(
                    ps_b2[:, :], onesrow[:, :], B_bf[:, :], start=True,
                    stop=True), waits=[b1], inc="mmS")
                cB = p.mark("mmS")
                return v1, cA, cB

            v1, cA, cB = ln_block(2048.0, 1e-5, stats_done)
            st_done = v1
            u2m = None
            for do in range(16):
                u1 = p.emit("vector", lambda e, d_=do: e.tensor_tensor(
                    out=tmp1[:, :], in0=s_sb[d_][:, :], in1=ps_b1[:, :],
                    op=ALU.mult), waits=[cA, tmp1_dma], inc="dvS")
                u2 = p.emit("vector", lambda e: e.tensor_tensor(
                    out=tmp1[:, :], in0=tmp1[:, :], in1=ps_b2[:, :],
                    op=ALU.add), waits=[cB], inc="dvS")
                u2m = u2
                x3d = p.dma("sync", x3buf[do * 128:(do + 1) * 128,
                                          tbi * NB:(tbi + 1) * NB],
                            tmp1[:, :], waits=[u2], inc="wx3")
                tmp1_dma = x3d
                p.emit("vector", lambda e, d_=do: e.tensor_copy(
                    A[:, d_ * NB:(d_ + 1) * NB], tmp1[:, :]),
                    waits=[strip_mark.get(("fcmm", tbi - 1, 39))], inc="dvS")
            bc_done = u2m
            x3_dma = p.mark("wx3")
            a_rdy = p.mark("dvS")
            prev_s1m = prev_s2m = None
            for ot in range(40):
                buf = ot % 2
                ld = p.dma("sync",
                           tstr[buf][:, :].rearrange("p (a b) -> p a b", a=16),
                           t1fc[ot * 2048:(ot + 1) * 2048, :].rearrange(
                               "(a p) b -> p a b", p=128),
                           waits=[strip_mark.get(("fcmm", tbi, ot - 2)),
                                  strip_mark.get(("fcmm", tbi - 1,
                                                  38 + (ot % 2)))
                                  if (tbi > 0 and ot < 2) else None,
                                  tern_ready if (tbi == 0 and ot < 2) else None,
                                  tern_ready2 if (tbi == 0 and ot < 2)
                                  else None],
                           inc=f"ldstr{buf}")
                for dm in range(16):
                    w1 = []
                    if dm == 0:
                        w1 = [ld, a_rdy, bank_free[buf]]
                    p.emit("tensor", lambda e, b=buf, d_=dm: e.matmul(
                        ps_a[b][:, :], tstr[b][:, d_ * 128:(d_ + 1) * 128],
                        A[:, d_ * NB:(d_ + 1) * NB],
                        start=(d_ == 0), stop=(d_ == 15)),
                        waits=w1, inc="mmS" if dm == 15 else None)
                mmk = p.mark("mmS")
                strip_mark[("fcmm", tbi, ot)] = mmk
                re_ = p.emit("scalar", lambda e, b=buf: e.activation(
                    out=rt_t[:, :], in_=ps_a[b][:, :], func=AF.Relu,
                    scale=gamb[:, :]), waits=[mmk], inc="evS")
                bank_free[buf] = re_
                h1 = p.emit("scalar", lambda e, o_=ot: e.activation(
                    out=Bh[:, o_ * NB:(o_ + 1) * NB], in_=rt_t[:, :],
                    func=AF.Square), inc="evS")
            h1_all = p.mark("evS")
            prev_s1m = prev_s2m = None
            for do in range(16):
                bank = do % 2
                qlds = []
                for qq in range(4):
                    qlds.append(p.dma(
                        "sync",
                        pstr[:, qq * 1280:(qq + 1) * 1280].rearrange(
                            "p (a b) -> p a b", a=10),
                        t1proj[do * 5120 + qq * 1280:
                               do * 5120 + (qq + 1) * 1280, :].rearrange(
                            "(a p) b -> p a b", p=128),
                        waits=[pstr_mark.get(qq) if isinstance(pstr_mark, dict)
                               else pstr_mark,
                               tern_ready if (tbi == 0 and do == 0)
                               else None,
                               tern_ready2 if (tbi == 0 and do == 0)
                               else None],
                        inc=f"ldpr{qq}"))
                if not isinstance(pstr_mark, dict):
                    pstr_mark = {}
                for ht in range(40):
                    w1 = []
                    if ht % 10 == 0:
                        w1 = [qlds[ht // 10]]
                    if ht == 0:
                        w1 += [bank_free[bank], h1_all]
                    p.emit("tensor", lambda e, b=bank, h_=ht: e.matmul(
                        ps_a[b][:, :], pstr[:, h_ * 128:(h_ + 1) * 128],
                        Bh[:, h_ * NB:(h_ + 1) * NB],
                        start=(h_ == 0), stop=(h_ == 39)),
                        waits=w1, inc="mmS")
                    if ht % 10 == 9:
                        pstr_mark[ht // 10] = p.mark("mmS")
                mmk = p.mark("mmS")
                x3ld = p.dma("sync", xr[bank][:, :],
                             x3buf[do * 128:(do + 1) * 128,
                                   tbi * NB:(tbi + 1) * NB],
                             waits=[last_xr[bank],
                                    x3_dma if do < 2 else None],
                             inc=("ldx0" if bank == 0 else "ldx1b"))
                sv = p.emit("vector", lambda e, b=bank, d_=do:
                            e.scalar_tensor_tensor(
                                out=s_sb[d_][:, :], in0=ps_a[b][:, :],
                                scalar=gamb2[:, :], in1=xr[b][:, :],
                                op0=ALU.mult, op1=ALU.add),
                            waits=[mmk, x3ld], inc="dvS")
                last_xr[bank] = sv
                bank_free[bank] = sv
                sb2 = p.emit("vector", lambda e, d_=do: e.tensor_copy(
                    sbf_t[:, :], s_sb[d_][:, :]), waits=[prev_s1m], inc="dvS")
                sq_ = p.emit("scalar", lambda e: e.activation(
                    out=sq_t[:, :], in_=sbf_t[:, :], func=AF.Square),
                    waits=[sb2, prev_s2m], inc="evS")
                w_st1 = [sb2]
                if do == 0:
                    w_st1 += [st_done]
                prev_s1m = p.emit("tensor", lambda e, d_=do: e.matmul(
                    ps_s1[:, :], onescol[:, :], sbf_t[:, :],
                    start=(d_ == 0), stop=(d_ == 15)), waits=w_st1, inc="mmS")
                w_st2 = [sq_]
                if do == 0:
                    w_st2 += [st_done]
                prev_s2m = p.emit("tensor", lambda e, d_=do: e.matmul(
                    ps_s2[:, :], onescol[:, :], sq_t[:, :],
                    start=(d_ == 0), stop=(d_ == 15)), waits=w_st2, inc="mmS")
            stats_done = p.mark("mmS")
            v1, cA, cB = ln_block(2048.0, 1e-5, stats_done)
            st_done = v1
            u2m = None
            for do in range(16):
                u1 = p.emit("vector", lambda e, d_=do: e.tensor_tensor(
                    out=tmp1[:, :], in0=s_sb[d_][:, :], in1=ps_b1[:, :],
                    op=ALU.mult), waits=[cA, tmp1_dma], inc="dvS")
                u2 = p.emit("vector", lambda e: e.tensor_tensor(
                    out=tmp1[:, :], in0=tmp1[:, :], in1=ps_b2[:, :],
                    op=ALU.add), waits=[cB], inc="dvS")
                u2m = u2
                od = p.dma("sync", outx[do * 128:(do + 1) * 128,
                                        tbi * NB:(tbi + 1) * NB],
                           tmp1[:, :], waits=[u2], inc="st0")
                tmp1_dma = od
            bc_done = u2m
        p.emit("sync", lambda e: e.nop(), waits=[p.mark("st0")])
        if mega is None:
            p.realize(nc, sems)
        else:
            mega["sems"].update(sems)
    return nc


def _launch5_inputs(inputs, x2_full, olocal_full, qhat_all, summ_all):
    w1fcT = np.ascontiguousarray(np.asarray(inputs["L1_ffn_w_fc"]).T.astype(f32))
    w1prT = np.ascontiguousarray(np.asarray(inputs["L1_ffn_w_proj"]).T.astype(f32))
    w1fc_t = _tile_lhsT(w1fcT, 2048, 5120)
    w1proj_t = _tile_lhsT(w1prT, 5120, 2048)
    Wo1 = np.ascontiguousarray(np.asarray(inputs["L1_attn_Wo"]).astype(f32))
    summ_stack = np.ascontiguousarray(summ_all.reshape(8 * 64, 130))
    maps = []
    for c in range(NCORE):
        sl = slice(c * TC, (c + 1) * TC)
        ltm = np.zeros((64, 8), dtype=f32)
        ltm[:, :c] = 1.0
        maps.append({
            "olocal": np.ascontiguousarray(olocal_full[:, sl]),
            "qhat": np.ascontiguousarray(qhat_all[c]),
            "summ_all": summ_stack,
            "own_sum": np.ascontiguousarray(summ_all[c]),
            "ltm": ltm,
            "x2": np.ascontiguousarray(x2_full[:, sl]),
            "Wo1": Wo1, "w1fc_t": w1fc_t, "w1proj_t": w1proj_t,
        })
    return maps


def run_launch5(inputs, x2_full, olocal_full, qhat_all, summ_all):
    nc = build_launch5()
    maps = _launch5_inputs(inputs, x2_full, olocal_full, qhat_all, summ_all)
    res = run_bass_kernel_spmd(nc, maps, core_ids=list(range(NCORE)))
    return np.concatenate([res.results[c]["outx"] for c in range(NCORE)], axis=1)


def build_phaseA(mega=None):
    if mega is None:
        nc, io, ns = bass.Bass("TRN2", target_bir_lowering=False), None, ""
        p = Plan()
    else:
        nc, p, io, ns = mega["nc"], mega["p"], mega["io"], mega["ns"]
    p.set_ns(ns)
    xTf = _dram(nc, io, ns, "xTf", [D, S], F32, "ExternalInput")
    WS, BS = {}, {}
    for w in "qkvg":
        WS[w] = _dram(nc, io, ns, f"W{w}S", [D, 256], F32, "ExternalInput")
        BS[w] = _dram(nc, io, ns, f"b{w}S", [128, 2], F32, "ExternalInput")
    tabs = {t: _dram(nc, io, ns, t, [128, S], F32, "ExternalInput")
            for t in ("cosq", "sinq", "cosk", "sink")}
    identb = _dram(nc, io, ns, "identA", [128, 128], BF16, "ExternalInput")
    qrh = _dram(nc, io, ns, "qrh", [256, S], BF16, "ExternalOutput")
    krh = _dram(nc, io, ns, "krh", [256, S], BF16, "ExternalOutput")
    vth = _dram(nc, io, ns, "vth", [256, S], BF16, "ExternalOutput")
    gh = _dram(nc, io, ns, "gh", [256, S], BF16, "ExternalOutput")
    outmap = {"q": qrh, "k": krh, "g": gh}

    NB = 512
    with ExitStack() as ctx:
        e = ctx.enter_context
        xb = e(nc.sbuf_tensor(ns + "xb", [128, 2 * 16 * NB], BF16))
        wb = e(nc.sbuf_tensor(ns + "wb", [128, 4 * 16 * 256], BF16))
        bb = e(nc.sbuf_tensor(ns + "bb", [128, 8], F32))
        tb_ = {t: [e(nc.sbuf_tensor(f"{ns}tb_{t}{i}", [128, NB], F32))
                   for i in range(2)] for t in tabs}
        id_sb = e(nc.sbuf_tensor(ns + "id_sb", [128, 128], BF16))
        qa = e(nc.sbuf_tensor(ns + "qa", [128, NB], F32))
        qs = e(nc.sbuf_tensor(ns + "qs", [128, NB], F32))
        t1 = e(nc.sbuf_tensor(ns + "t1", [128, NB], F32))
        t2 = e(nc.sbuf_tensor(ns + "t2", [128, NB], F32))
        vt_s = e(nc.sbuf_tensor(ns + "vt_s", [128, NB], BF16))
        ob = [e(nc.sbuf_tensor(f"{ns}ob{i}", [128, NB], BF16)) for i in range(4)]
        ps = [e(nc.psum_tensor(f"{ns}ps{i}", [128, NB], F32)) for i in range(2)]
        ps_tr = e(nc.psum_tensor(ns + "ps_tr", [128, NB], BF16))
        sems = _fresh(mega["sem_ctx"] if mega else ctx, nc,
                      [ns + s_ for s_ in ("ldx0", "ldx1", "ldw", "ldt", "ldt0",
                                          "ldt1", "mmS", "dvS", "evS", "plS",
                                          "st0", "st1", "st2", "st3")])

        wld = None
        for wi, w in enumerate("qkvg"):
            wld = p.dma("gpsimd",
                        wb[:, wi * 4096:(wi + 1) * 4096].rearrange(
                            "p (a b) -> p a b", a=16),
                        WS[w].rearrange("(a p) b -> p a b", p=128), inc="ldw")
            p.dma("sync", bb[:, wi * 2:(wi + 1) * 2], BS[w][:, :], inc="ldt")
        p.dma("sync", id_sb[:, :], identb[:, :], inc="ldt")
        consts_rdy = p.mark("ldt")

        pairswap = []
        for i in range(0, 32, 2):
            pairswap += [i + 1, i]

        last_bank_free = [None, None]
        last_ob_dma = [None, None, None, None]
        last_tab_use = [None, None]
        x_consumed = [None, None]
        obi = 0
        for tbi in range(32):
            par = tbi % 2
            xoff = par * 16 * NB
            xlds = []
            for dm in range(16):
                xlds.append(p.dma("gpsimd",
                                  xb[:, xoff + dm * NB:xoff + (dm + 1) * NB],
                                  xTf[dm * 128:(dm + 1) * 128,
                                      tbi * NB:(tbi + 1) * NB],
                                  waits=[x_consumed[par]] if dm == 0 else [],
                                  inc=f"ldx{par}"))
            x_rdy = xlds[-1]
            tld = None
            for t in tabs:
                tld = p.dma("sync", tb_[t][par][:, :],
                            tabs[t][:, tbi * NB:(tbi + 1) * NB],
                            waits=[last_tab_use[par]] if t == "cosq" else [],
                            inc=f"ldt{par}")
            tabs_rdy = tld
            for gi, (w, h) in enumerate(
                    [(w, h) for w in "qkvg" for h in range(2)]):
                wi = "qkvg".index(w)
                bank = gi % 2
                for dm in range(16):
                    w1 = []
                    if dm == 0:
                        w1 = [x_rdy, wld, last_bank_free[bank]]
                    p.emit("tensor",
                           lambda e, b=bank, d_=dm, w_=wi, h_=h, xo=xoff:
                           e.matmul(
                               ps[b][:, :],
                               wb[:, (w_ * 16 + d_) * 256 + h_ * 128:
                                  (w_ * 16 + d_) * 256 + h_ * 128 + 128],
                               xb[:, xo + d_ * NB:xo + (d_ + 1) * NB],
                               start=(d_ == 0), stop=(d_ == 15)),
                           waits=w1, inc="mmS" if dm == 15 else None)
                mm_done = p.mark("mmS")
                if gi == 7:
                    x_consumed[par] = mm_done
                if w in ("q", "k"):
                    ct, st_ = ("cosq", "sinq") if w == "q" else ("cosk", "sink")
                    bf_ = p.emit("vector",
                                 lambda e, b=bank, w_=wi, h_=h: e.tensor_scalar(
                                     out=qa[:, :], in0=ps[b][:, :],
                                     scalar1=bb[:, w_ * 2 + h_:w_ * 2 + h_ + 1],
                                     scalar2=None, op0=ALU.add),
                                 waits=[mm_done, tabs_rdy, consts_rdy],
                                 inc="dvS")
                    last_bank_free[bank] = bf_
                    p.emit("vector", lambda e: e.stream_shuffle(
                        qs[:, :], qa[:, :], pairswap))
                    p.emit("vector", lambda e, c=ct, pr=par: e.tensor_tensor(
                        out=t1[:, :], in0=qa[:, :], in1=tb_[c][pr][:, :],
                        op=ALU.mult))
                    p.emit("vector", lambda e, s_=st_, pr=par: e.tensor_tensor(
                        out=t2[:, :], in0=qs[:, :], in1=tb_[s_][pr][:, :],
                        op=ALU.mult))
                    rdy = p.emit("vector",
                                 lambda e, o=obi: e.tensor_tensor(
                                     out=ob[o][:, :], in0=t1[:, :],
                                     in1=t2[:, :], op=ALU.add),
                                 waits=[last_ob_dma[obi]], inc="dvS")
                    last_tab_use[par] = rdy
                elif w == "g":
                    rdy = p.emit("scalar",
                                 lambda e, b=bank, o=obi, h_=h: e.activation(
                                     out=ob[o][:, :], in_=ps[b][:, :],
                                     func=AF.Silu,
                                     bias=bb[:, 6 + h_:7 + h_]),
                                 waits=[mm_done, consts_rdy,
                                        last_ob_dma[obi]], inc="evS")
                    last_bank_free[bank] = rdy
                else:
                    vv = p.emit("vector",
                                lambda e, b=bank, h_=h: e.tensor_scalar(
                                    out=vt_s[:, :], in0=ps[b][:, :],
                                    scalar1=bb[:, 4 + h_:5 + h_], scalar2=None,
                                    op0=ALU.add),
                                waits=[mm_done, consts_rdy], inc="dvS")
                    last_bank_free[bank] = vv
                    trm = None
                    for i4 in range(4):
                        trm = p.emit("tensor",
                                     lambda e, i_=i4: e.transpose(
                                         ps_tr[:, i_ * 128:(i_ + 1) * 128],
                                         vt_s[:, i_ * 128:(i_ + 1) * 128],
                                         id_sb[:, :]),
                                     waits=[vv] if i4 == 0 else [],
                                     inc="mmS")
                    rdy = p.emit("vector",
                                 lambda e, o=obi: e.tensor_copy(
                                     ob[o][:, :], ps_tr[:, :]),
                                 waits=[trm, last_ob_dma[obi]], inc="dvS")
                dst = vth if w == "v" else outmap[w]
                st2_ = p.dma("sync", dst[h * 128:(h + 1) * 128,
                                         tbi * NB:(tbi + 1) * NB],
                             ob[obi][:, :], waits=[rdy], inc=f"st{obi}")
                last_ob_dma[obi] = st2_
                obi = (obi + 1) % 4
        p.emit("sync", lambda e: e.nop(),
               waits=[p.mark(f"st{i}") for i in range(4) if p.has(f"st{i}")])
        if mega is None:
            p.realize(nc, sems)
        else:
            mega["sems"].update(sems)
    return nc


def _phaseA_inputs(inputs):
    global _ROPE_CACHE
    if _ROPE_CACHE is None:
        _ROPE_CACHE = _rope_tables()
    cosq, sinq, cosk, sink = _ROPE_CACHE
    x = np.asarray(inputs["L0_attn_x"])[0]
    xTfull = np.ascontiguousarray(x.T)
    idn = np.ascontiguousarray(np.eye(128, dtype=f32).astype(BF))
    maps = []
    for c in range(NCORE):
        cols = slice(256 * c, 256 * c + 256)
        m = {"xTf": xTfull, "cosq": cosq, "sinq": sinq, "cosk": cosk,
             "sink": sink, "identA": idn}
        for w, nm in (("q", "Wq"), ("k", "Wk"), ("v", "Wv"), ("g", "Wg")):
            m[f"W{w}S"] = np.ascontiguousarray(
                np.asarray(inputs[f"L0_attn_{nm}"]).astype(f32)[:, cols])
            m[f"b{w}S"] = np.ascontiguousarray(
                np.asarray(inputs[f"L0_attn_b{w}"]).astype(f32)[cols]
                .reshape(2, 128).T)
        maps.append(m)
    return maps




def emit_meanpair(p, nc, bufs, wt_fc, wt_proj, nch):
    wstr = bufs["wstr"]
    red1 = bufs["red1"]
    wacc, wacc2 = bufs["wacc"], bufs["wacc2"]
    prev_pair_done = bufs.get("prev_pair_done")
    macc = {}
    chain = [bufs.get("prev_pair_done"), bufs.get("prev_pair_done")]
    for wi, (wt, acc) in enumerate(((wt_fc, wacc), (wt_proj, wacc2))):
        for i in range(nch):
            buf = i % 2
            srcv = wt[i * 2048:(i + 1) * 2048, :].rearrange(
                "(a p) b -> p a b", p=128)
            ld = p.dma("gpsimd", wstr[buf][:, :].rearrange(
                "p (a b) -> p a b", a=16), srcv,
                waits=[chain[buf]], inc=f"ldstr{buf}")
            rd = p.emit("vector", lambda e, b_=buf: e.tensor_reduce(
                out=red1[:, :], in_=wstr[b_][:, :], axis=AX.X,
                op=ALU.add, apply_absolute_value=True), waits=[ld])
            chain[buf] = rd
            ac = p.emit("vector", lambda e, a_=acc, first=(i == 0): (
                e.tensor_copy(a_[:, :], red1[:, :]) if first else
                e.tensor_tensor(out=a_[:, :], in0=a_[:, :], in1=red1[:, :],
                                op=ALU.add)))
            macc[(wi, i)] = ac
    return macc


def emit_thrpair(p, nc, bufs, denom_fc, denom_proj, mode, macc,
                 chain=None):
    nch = max(k[1] for k in macc) + 1
    onescol_f, onesrow_f = bufs["onescol_f"], bufs["onesrow_f"]
    red1, sc1 = bufs["red1"], bufs["sc1"]
    gamb, thrb, gamb2, thrb2 = bufs["gamb"], bufs["thrb"], bufs["gamb2"], bufs["thrb2"]
    ps_small, ps_big = bufs["ps_small"], bufs["ps_big"]
    wacc, wacc2 = bufs["wacc"], bufs["wacc2"]
    ones_ready = bufs["ones_ready"]

    def scalar_reduce(acc, gbuf, tbuf, denom, col, chain):
        t = p.emit("tensor", lambda e, a_=acc, c_=col: e.matmul(
            ps_small[0:1, c_:c_ + 1], onescol_f[:, :], a_[:, :],
            start=True, stop=True),
            waits=[macc[(0, nch - 1)], macc[(1, nch - 1)], ones_ready,
                   chain],
            inc="mmS")
        if mode == "bitlinear":
            g1 = p.emit("vector", lambda e, c_=col: e.tensor_scalar(
                out=sc1[0:1, c_:c_ + 1], in0=ps_small[0:1, c_:c_ + 1],
                scalar1=float(1.0 / denom), scalar2=None, op0=ALU.mult),
                waits=[t])
            t2_ = p.emit("vector", lambda e, c_=col: e.tensor_scalar(
                out=sc1[0:1, c_ + 1:c_ + 2], in0=sc1[0:1, c_:c_ + 1],
                scalar1=float(1e-6), scalar2=float(0.5), op0=ALU.add,
                op1=ALU.mult))
        else:
            g1 = p.emit("vector", lambda e, c_=col: e.tensor_scalar(
                out=sc1[0:1, c_:c_ + 1], in0=ps_small[0:1, c_:c_ + 1],
                scalar1=float(1.0 / denom), scalar2=float(1e-5),
                op0=ALU.mult, op1=ALU.max), waits=[t])
            t2_ = p.emit("vector", lambda e, c_=col: e.tensor_scalar(
                out=sc1[0:1, c_ + 1:c_ + 2], in0=sc1[0:1, c_:c_ + 1],
                scalar1=float(0.7), scalar2=None, op0=ALU.mult))
        bg = p.emit("tensor", lambda e, c_=col: e.matmul(
            ps_big[:, c_:c_ + 2], onesrow_f[:, :], sc1[0:1, c_:c_ + 2],
            start=True, stop=True), waits=[t2_], inc="mmS")
        c1 = p.emit("scalar", lambda e, g_=gbuf, c_=col: e.activation(
            out=g_[:, :], in_=ps_big[:, c_:c_ + 1], func=AF.Copy, scale=1.0),
            waits=[bg], inc="evS")
        c2 = p.emit("scalar", lambda e, t_=tbuf, c_=col: e.activation(
            out=t_[:, :], in_=ps_big[:, c_ + 1:c_ + 2], func=AF.Copy,
            scale=1.0), inc="evS")
        return (g1, c2)
    thr_ready = scalar_reduce(wacc, gamb, thrb, denom_fc, 0, chain)
    thr2_ready = scalar_reduce(wacc2, gamb2, thrb2, denom_proj, 2,
                               thr_ready[1])
    return (thr_ready, thr2_ready)


def emit_ternonly(p, nc, bufs, wt_fc, wt_proj, tfc, tproj, nch, macc, thrs,
                  extra_wait=None, prev=None):
    wstr, tstr = bufs["wstr"], bufs["tstr"]
    aa_t, msk_t, sgn_t = bufs["aa"], bufs["msk"], bufs["sgn"]
    gamb, thrb, gamb2, thrb2 = (bufs["gamb"], bufs["thrb"], bufs["gamb2"],
                                bufs["thrb2"])
    thr_ready, thr2_ready = thrs
    tern_mark = {}
    if prev is not None:
        tern_mark[("sgn", 0, -2)] = prev["last_sgn0"]
        tern_mark[("sgn", 0, -1)] = prev["last_sgn1"]
        tern_mark[("st", -1, nch - 2)] = prev["last_st0"]
        tern_mark[("st", -1, nch - 1)] = prev["last_st1"]
    for wi, (wt, dst, tb_buf) in enumerate(((wt_fc, tfc, thrb),
                                            (wt_proj, tproj, thrb2))):
        thr_m = thr_ready[1] if wi == 0 else thr2_ready[1]
        for j in range(nch):
            buf = j % 2
            srcv = wt[j * 2048:(j + 1) * 2048, :].rearrange(
                "(a p) b -> p a b", p=128)
            ld = p.dma("gpsimd", wstr[buf][:, :].rearrange(
                "p (a b) -> p a b", a=16), srcv,
                waits=[tern_mark.get(("sgn", wi, j - 2)),
                       macc[(wi, nch - 1)] if j < 2 else None,
                       macc[(1, nch - 1)] if (wi == 0 and j < 2) else None,
                       extra_wait if j < 2 else None],
                inc=f"ldstr{buf}")
            a_ = p.emit("scalar", lambda e, b=buf: e.activation(
                out=aa_t[:, :], in_=wstr[b][:, :], func=AF.Abs),
                waits=[ld, thr_m], inc="evS")
            m_ = p.emit("vector", lambda e, t_=tb_buf: e.tensor_scalar(
                out=msk_t[:, :], in0=aa_t[:, :], scalar1=t_[:, :],
                scalar2=None, op0=ALU.is_gt), waits=[a_], inc="dvS")
            g_ = p.emit("scalar", lambda e, b=buf: e.activation(
                out=sgn_t[:, :], in_=wstr[b][:, :], func=AF.Sign),
                waits=[m_], inc="evS")
            tern_mark[("sgn", wi, j)] = g_
            t_ = p.emit("gpsimd", lambda e, b=buf: e.tensor_tensor(
                out=tstr[b][:, :], in0=msk_t[:, :], in1=sgn_t[:, :],
                op=ALU.mult),
                waits=[g_, tern_mark.get(("st", wi, j - 2)),
                       tern_mark.get(("st", wi - 1, nch - 2 + j))
                       if j < 2 else None],
                inc="plS")
            stm = p.dma("gpsimd", dst[j * 2048:(j + 1) * 2048, :].rearrange(
                "(a p) b -> p a b", p=128),
                tstr[buf][:, :].rearrange("p (a b) -> p a b", a=16),
                waits=[t_], inc=f"stw{buf}")
            tern_mark[("st", wi, j)] = stm
    return {
        "thr_ready": thr_ready, "thr2_ready": thr2_ready,
        "tern_ready": p.mark("stw0"), "tern_ready2": p.mark("stw1"),
        "gamb": gamb, "thrb": thrb, "gamb2": gamb2, "thrb2": thrb2,
        "pair_done": tern_mark[("st", 1, nch - 1)], "tstr": tstr,
        "last_sgn0": tern_mark[("sgn", 1, nch - 2)],
        "last_sgn1": tern_mark[("sgn", 1, nch - 1)],
        "last_st0": tern_mark[("st", 1, nch - 2 if nch % 2 == 0 else nch - 1)],
        "last_st1": tern_mark[("st", 1, nch - 1 if nch % 2 == 0 else nch - 2)],
    }


U32 = mybir.dt.uint32


def build_mega():
    import os
    tern_mode = os.environ.get("MEGA_TERN", "serial")
    nc = bass.Bass("TRN2", target_bir_lowering=False)
    p = Plan()
    sems = {}
    qrh_t = nc.dram_tensor("qrh", [256, S], BF16)
    krh_t = nc.dram_tensor("krh", [256, S], BF16)
    vth_t = nc.dram_tensor("vth", [256, S], BF16)
    gh_t = nc.dram_tensor("gh", [256, S], BF16)
    yh_t = nc.dram_tensor("yh", [256, S], BF16)
    yall_t = nc.dram_tensor("y_all", [16384, TC], BF16, addr_space="Shared")
    ymine_t = nc.dram_tensor("y_mine", [D, TC], BF16)
    x2_t = nc.dram_tensor("x2buf", [D, TC], F32)
    x2b_t = nc.dram_tensor("x2bbuf", [D, TC], BF16)
    ol_t = nc.dram_tensor("olocalb", [128, TC], F32)
    qh_t = nc.dram_tensor("qhatb", [64, 128], BF16)
    sm_t = nc.dram_tensor("summb", [64, 130], F32)
    small_t = nc.dram_tensor("summ_allb", [512, 130], F32, addr_space="Shared")
    yidx = nc.dram_tensor("yidx", [128, 16], U32, kind="ExternalInput").ap()

    wfc_t = nc.dram_tensor("C_wfc_t", [64 * 16 * 128, 128], F32,
                           kind="ExternalInput").ap()
    wproj_t = nc.dram_tensor("C_wproj_t", [16 * 64 * 128, 128], F32,
                             kind="ExternalInput").ap()
    w1fc_t = nc.dram_tensor("E_w1fc_t", [40 * 16 * 128, 128], F32,
                            kind="ExternalInput").ap()
    w1proj_t = nc.dram_tensor("E_w1proj_t", [16 * 40 * 128, 128], F32,
                              kind="ExternalInput").ap()
    tfc_t = nc.dram_tensor("tfcb", [64 * 16 * 128, 128], BF16)
    tproj_t = nc.dram_tensor("tprojb", [16 * 64 * 128, 128], BF16)
    t1fc_t = nc.dram_tensor("t1fcb", [40 * 16 * 128, 128], BF16)
    t1proj_t = nc.dram_tensor("t1projb", [16 * 40 * 128, 128], BF16)

    with ExitStack() as outer:
        mega = {"nc": nc, "p": p, "sems": sems, "sem_ctx": outer}

        if tern_mode != "off":
            e = outer.enter_context
            tb_ = {
                "wstr": [e(nc.sbuf_tensor(f"T_wstr{i}", [128, 2048], F32))
                         for i in range(2)],
                "tstr": [e(nc.sbuf_tensor(f"T_tstr{i}", [128, 2048], BF16))
                         for i in range(2)],
                "aa": e(nc.sbuf_tensor("T_aa", [128, 2048], F32)),
                "msk": e(nc.sbuf_tensor("T_msk", [128, 2048], BF16)),
                "sgn": e(nc.sbuf_tensor("T_sgn", [128, 2048], BF16)),
                "onescol_f": e(nc.sbuf_tensor("T_onescol_f", [128, 1], F32)),
                "onesrow_f": e(nc.sbuf_tensor("T_onesrow_f", [1, 128], F32)),
                "red1": e(nc.sbuf_tensor("T_red1", [128, 1], F32)),
                "sc1": e(nc.sbuf_tensor("T_sc1", [1, 8], F32)),
            }
            psT = e(nc.psum_tensor("T_ps", [128, 16], F32))
            tb_["ps_small"] = psT[0:1, 8:16]
            tb_["ps_big"] = psT[:, 0:8]
            gtiles = {}
            for nm in ("C_gamb", "C_thrb", "C_gamb2", "C_thrb2",
                       "E_gamb", "E_thrb", "E_gamb2", "E_thrb2",
                       "C_wacc", "C_wacc2", "E_wacc", "E_wacc2"):
                gtiles[nm] = e(nc.sbuf_tensor("T_" + nm, [128, 1], F32))
            sems.update(_fresh(outer, nc, ["T_" + s_ for s_ in (
                "ldstr0", "ldstr1", "stw0", "stw1", "mmS", "dvS", "evS", "plS")]))
            sems.update(_fresh(outer, nc, ["U_" + s_ for s_ in (
                "ldstr0", "ldstr1", "stw0", "stw1", "dvS", "evS", "plS")]))
            bufsC = dict(tb_, gamb=gtiles["C_gamb"], thrb=gtiles["C_thrb"],
                         gamb2=gtiles["C_gamb2"], thrb2=gtiles["C_thrb2"],
                         wacc=gtiles["C_wacc"], wacc2=gtiles["C_wacc2"])
            bufsE = dict(tb_, gamb=gtiles["E_gamb"], thrb=gtiles["E_thrb"],
                         gamb2=gtiles["E_gamb2"], thrb2=gtiles["E_thrb2"],
                         wacc=gtiles["E_wacc"], wacc2=gtiles["E_wacc2"])

        snapA = p.snapshot()
        mega["ns"] = "A_"
        mega["io"] = {"qrh": qrh_t.ap(), "krh": krh_t.ap(),
                      "vth": vth_t.ap(), "gh": gh_t.ap()}
        build_phaseA(mega=mega)
        p.set_ns("G_")
        p.barrier()
        if tern_mode != "off":
            p.set_ns("T_")
            ones2 = p.emit("gpsimd",
                           lambda e: e.memset(tb_["onescol_f"][:, :], 1.0))
            ones2 = p.emit("gpsimd",
                           lambda e: e.memset(tb_["onesrow_f"][:, :], 1.0))
            bufsC["ones_ready"] = ones2
            bufsE["ones_ready"] = ones2
            maccC = emit_meanpair(p, nc, bufsC, wfc_t, wproj_t, 64)
            bufsE["prev_pair_done"] = maccC[(1, 63)]
            maccE = emit_meanpair(p, nc, bufsE, w1fc_t, w1proj_t, 40)
            thrsC = emit_thrpair(p, nc, bufsC, 2048.0 * 8192.0,
                                 8192.0 * 2048.0, "bitlinear", maccC)
            thrsE = emit_thrpair(p, nc, bufsE, 5120.0 * 2048.0,
                                 5120.0 * 2048.0, "maxthr", maccE,
                                 chain=thrsC[1][1])
        snapB = p.snapshot()
        mega["ns"] = "B_"
        mega["io"] = {"qrh": qrh_t.ap(), "krh": krh_t.ap(),
                      "vTh": vth_t.ap(), "gh": gh_t.ap(), "yh": yh_t.ap()}
        build_launch2(mega=mega)
        if tern_mode != "off":
            pT2 = Plan()
            pT2.set_ns("U_")
            preC = emit_ternonly(pT2, nc, bufsC, wfc_t, wproj_t, tfc_t.ap(),
                                 tproj_t.ap(), 64, maccC, thrsC,
                                 extra_wait=maccE[(1, 39)])
            preE = emit_ternonly(pT2, nc, bufsE, w1fc_t, w1proj_t, t1fc_t.ap(),
                                 t1proj_t.ap(), 40, maccE, thrsE,
                                 extra_wait=preC["last_sgn1"], prev=preC)
            if tern_mode == "merge":
                p.merge_from(pT2, since=snapB, defer_self={"gpsimd": 4})
            else:
                p.merge_from(pT2, since=p.snapshot())
        p.set_ns("G_")
        p.barrier()

        p.set_ns("X1_")
        sems.update(_fresh(outer, nc, ["X1_cc", "X1_ld", "X1_st"]))
        with (nc.sbuf_tensor("X1_ysb", [128, TC], BF16) as ysb,
              nc.sbuf_tensor("X1_idx", [128, 16], U32) as idx_sb):
            cc = p.emit("gpsimd", lambda e: e.collective_compute(
                "AllGather", ALU.bypass, replica_groups=[list(range(NCORE))],
                ins=[yh_t.ap().opt()], outs=[yall_t.ap().opt()]),
                inc="cc", inc_k=1, _dma=True)
            ild = p.dma("sync", idx_sb[:, :], yidx[:, :], inc="ld")
            prev_st = None
            for g in range(16):
                gm = p.emit("gpsimd", lambda e, g_=g: e.indirect_dma_start(
                    out=ysb[:, :], out_offset=None, in_=yall_t.ap(),
                    in_offset=bass.IndirectOffsetOnAxis(
                        ap=idx_sb[:, g_:g_ + 1], axis=0)),
                    waits=[cc, ild, prev_st], inc="ld", inc_k=16, _dma=True)
                prev_st = p.dma("sync",
                                ymine_t.ap()[g * 128:(g + 1) * 128, :],
                                ysb[:, :], waits=[gm], inc="st")
        p.set_ns("G_")
        p.barrier()

        mega["ns"] = "C_"
        mega["io"] = {"y_sl": ymine_t.ap(), "x2o": x2_t.ap(),
                      "x2bo": x2b_t.ap(), "wfc_t": wfc_t, "wproj_t": wproj_t,
                      "tfc": tfc_t.ap(), "tproj": tproj_t.ap()}
        if tern_mode != "off":
            mega["pre"] = preC
        build_launch3(mega=mega)
        mega.pop("pre", None)
        p.set_ns("G_")
        p.barrier()

        mega["ns"] = "D_"
        mega["io"] = {"x2b": x2b_t.ap(), "olocal": ol_t.ap(),
                      "qhato": qh_t.ap(), "summo": sm_t.ap()}
        build_launch4(mega=mega)
        p.set_ns("G_")
        p.barrier()

        p.set_ns("X2_")
        sems.update(_fresh(outer, nc, ["X2_cc"]))
        p.emit("gpsimd", lambda e: e.collective_compute(
            "AllGather", ALU.bypass, replica_groups=[list(range(NCORE))],
            ins=[sm_t.ap().opt()], outs=[small_t.ap().opt()]),
            inc="cc", inc_k=1, _dma=True)
        p.set_ns("G_")
        p.barrier()

        mega["ns"] = "E_"
        mega["io"] = {"olocal": ol_t.ap(), "qhat": qh_t.ap(),
                      "summ_all": small_t.ap(), "own_sum": sm_t.ap(),
                      "x2": x2_t.ap(), "w1fc_t": w1fc_t, "w1proj_t": w1proj_t,
                      "t1fc": t1fc_t.ap(), "t1proj": t1proj_t.ap()}
        if tern_mode != "off":
            mega["pre"] = preE
        build_launch5(mega=mega)
        mega.pop("pre", None)

        p.realize(nc, sems)
    return nc


def _mega_inputs(inputs):
    mapsA = _phaseA_inputs(inputs)
    masks = np.ascontiguousarray(_diag_masks())
    x = np.asarray(inputs["L0_attn_x"])[0]
    xTfull = np.ascontiguousarray(x.T)
    Wo = np.ascontiguousarray(np.asarray(inputs["L0_attn_Wo"]).astype(f32))
    bo = _pack_bias(np.asarray(inputs["L0_attn_bo"]))
    wfcT = np.ascontiguousarray(np.asarray(inputs["L0_ffn_w_fc"]).T.astype(f32))
    wprT = np.ascontiguousarray(np.asarray(inputs["L0_ffn_w_proj"]).T.astype(f32))
    wfc_t = _tile_lhsT(wfcT, 2048, 8192)
    wproj_t = _tile_lhsT(wprT, 8192, 2048)
    w1fcT = np.ascontiguousarray(np.asarray(inputs["L1_ffn_w_fc"]).T.astype(f32))
    w1prT = np.ascontiguousarray(np.asarray(inputs["L1_ffn_w_proj"]).T.astype(f32))
    w1fc_t = _tile_lhsT(w1fcT, 2048, 5120)
    w1proj_t = _tile_lhsT(w1prT, 5120, 2048)
    Wo1 = np.ascontiguousarray(np.asarray(inputs["L1_attn_Wo"]).astype(f32))
    cm = np.ascontiguousarray(_causal128())
    idn = np.ascontiguousarray(np.eye(128, dtype=f32).astype(BF))
    maps = []
    for c in range(NCORE):
        sl = slice(c * TC, (c + 1) * TC)
        m = {}
        for k, v in mapsA[c].items():
            m["A_" + k] = v
        qd = np.empty((256, 512), dtype=f32)
        kd = np.empty((256, 512), dtype=f32)
        pscl = np.empty((256, 131), dtype=f32)
        for hi in range(2):
            a, b, cc_ = _decay_tiles(2 * c + hi)
            qd[hi * 128:(hi + 1) * 128] = a
            kd[hi * 128:(hi + 1) * 128] = b
            pscl[hi * 128:(hi + 1) * 128] = cc_
        m.update({"B_qdec": qd, "B_kdec": kd, "B_pairsc": pscl,
                  "B_masks": masks})
        dg = np.arange(2048)
        vrow = (dg // 256) * 2048 + (dg % 256) * 8 + c
        m["yidx"] = np.ascontiguousarray(
            vrow.reshape(16, 128).T.astype(np.uint32))
        m.update({"C_Wo": Wo, "C_bo": bo,
                  "C_xT": np.ascontiguousarray(xTfull[:, sl]),
                  "C_wfc_t": wfc_t, "C_wproj_t": wproj_t})
        dd = {}
        for nm in ("Wq", "Wk", "Wb", "Wd", "Wv"):
            dd["D_" + nm + "1"] = np.ascontiguousarray(
                np.asarray(inputs[f"L1_attn_{nm}"]).astype(f32))
        dd["D_bb1"] = np.ascontiguousarray(
            np.asarray(inputs["L1_attn_bb"]).astype(f32).reshape(64, 1))
        dd["D_bd1"] = np.ascontiguousarray(
            np.asarray(inputs["L1_attn_bd"]).astype(f32).reshape(64, 1))
        dd["D_cmask"] = cm
        dd["D_identb"] = idn
        m.update(dd)
        ltm = np.zeros((64, 8), dtype=f32)
        ltm[:, :c] = 1.0
        m.update({"E_ltm": ltm, "E_Wo1": Wo1, "E_w1fc_t": w1fc_t,
                  "E_w1proj_t": w1proj_t})
        maps.append(m)
    return maps


def kernel_mega(**inputs):
    cores = list(range(NCORE))
    res = run_bass_kernel_spmd(_nc("mega", build_mega), _mega_inputs(inputs),
                               core_ids=cores)
    outx = np.concatenate([res.results[c]["E_outx"] for c in cores], axis=1)
    return np.ascontiguousarray(outx.T).reshape(1, S, D)


# version 12 (final)
# speedup vs baseline: 1.1250x; 1.1250x over previous
import numpy as np
import ml_dtypes
import concourse.bass as bass
import concourse.mybir as mybir
from concourse.bass_utils import run_bass_kernel_spmd
from contextlib import ExitStack

F32 = mybir.dt.float32
BF16 = mybir.dt.bfloat16
AF = mybir.ActivationFunctionType
ALU = mybir.AluOpType
AX = mybir.AxisListType

S, D, H, Dh = 16384, 2048, 16, 128
NCORE = 8
TC = S // NCORE
BF = ml_dtypes.bfloat16

f32 = np.float32


def _rope_tables():
    pos = np.arange(S, dtype=np.float64)
    inv_freq = 1.0 / 10000.0 ** (np.arange(0, Dh, 2, dtype=np.float64) / Dh)
    freqs = np.repeat(pos[:, None] * inv_freq[None, :], 2, axis=-1)
    cos, sin = np.cos(freqs), np.sin(freqs)
    sbase = (np.arange(0, Dh, 2, dtype=np.float64) + 0.4 * Dh) / (1.4 * Dh)
    power = (pos - S // 2) / 512.0
    scale = np.repeat(sbase[None, :] ** power[:, None], 2, axis=-1)
    sgn = np.tile(np.array([-1.0, 1.0]), Dh // 2)[None, :]
    cosq = (cos * scale).T.astype(f32)
    sinq = (sin * scale * sgn).T.astype(f32)
    cosk = (cos / scale).T.astype(f32)
    sink = (sin / scale * sgn).T.astype(f32)
    return cosq, sinq, cosk, sink


_GAMMA = (1.0 - 2.0 ** (-5.0 - np.arange(H, dtype=np.float64)))


def _decay_tiles(h):
    g = _GAMMA[h]
    qd = np.tile((g ** np.arange(512, dtype=np.float64))[None, :], (128, 1)).astype(f32)
    kd1 = (g ** (-np.arange(128, dtype=np.float64)))[None, :]
    kd = np.tile(np.concatenate([kd1] * 4, axis=1), (128, 1)).astype(f32)
    m = np.arange(-3, 128, dtype=np.float64)
    with np.errstate(under="ignore"):
        ps = np.tile((g ** (128.0 * m))[None, :], (128, 1)).astype(f32)
    return qd, kd, ps


def _diag_masks():
    j = np.arange(128)[:, None]
    i = np.arange(512)[None, :]
    tiles = [(128 * d + j <= i).astype(f32) for d in range(4)]
    return np.concatenate(tiles, axis=1).astype(BF)


def _causal128():
    j = np.arange(128)[:, None]
    i = np.arange(128)[None, :]
    return (j <= i).astype(f32).astype(BF)


def _pack_bias(b):
    return np.ascontiguousarray(b.reshape(-1, 128).T.astype(f32))


def _tile_lhsT(wT, kdim, odim):
    nk, no = kdim // 128, odim // 128
    t = wT.reshape(nk, 128, no, 128).transpose(2, 0, 1, 3)
    return np.ascontiguousarray(t.reshape(no * nk * 128, 128))


class Plan:

    ENGINES = ("tensor", "scalar", "vector", "gpsimd", "sync")
    AUTO = {"scalar": "evS", "vector": "dvS", "gpsimd": "plS"}

    def __init__(self):
        self.ops = {e: [] for e in self.ENGINES}
        self.counts = {}
        self.ns = ""

    def set_ns(self, ns):
        self.ns = ns

    def mark(self, name):
        return (self.ns + name, self.counts[self.ns + name])

    def mark0(self, name):
        n = self.ns + name
        return (n, self.counts.get(n, 0))

    def has(self, name):
        return (self.ns + name) in self.counts

    def cnt(self, name):
        return self.counts[self.ns + name]

    def snapshot(self):
        return {eng: len(self.ops[eng]) for eng in self.ENGINES}

    def merge_from(self, other, since=None, defer_self=None):
        for eng in self.ENGINES:
            s0 = since[eng] if since else 0
            if defer_self and eng in defer_self:
                s0 += defer_self[eng]
            head, a = self.ops[eng][:s0], self.ops[eng][s0:]
            b = other.ops[eng]
            if not b:
                continue
            if not a:
                self.ops[eng] = head + list(b)
                continue
            out = []
            ratio = len(a) / (len(a) + len(b))
            ia = ib = 0
            for k in range(len(a) + len(b)):
                if (ia + 1) / (k + 2) <= ratio and ia < len(a) or ib >= len(b):
                    out.append(a[ia]); ia += 1
                else:
                    out.append(b[ib]); ib += 1
            self.ops[eng] = head + out
        for k, v in other.counts.items():
            assert k not in self.counts, k
            self.counts[k] = v

    def barrier(self, extra=()):
        waits = [(s, c) for s, c in self.counts.items()] + list(extra)
        for eng in self.ENGINES:
            self.ops[eng].append((lambda e: e.nop(), list(waits), None, 1))

    def emit(self, eng, fn, waits=(), inc=None, inc_k=1, _dma=False):
        waits = [w for w in waits if w is not None]
        if inc is not None:
            inc = self.ns + inc
        auto = (self.ns + self.AUTO[eng]) if (eng in self.AUTO and not _dma) \
            else None
        if auto is not None:
            prev = self.counts.get(auto, 0)
            if prev:
                waits = [(auto, prev)] + waits
            if inc is None:
                inc, inc_k = auto, 1
            assert inc == auto and inc_k == 1, (eng, inc)
        self.ops[eng].append((fn, waits, inc, inc_k))
        if inc is not None:
            self.counts[inc] = self.counts.get(inc, 0) + inc_k
            return (inc, self.counts[inc])
        return None

    def dma(self, eng, out, in_, waits=(), inc=None):
        return self.emit(eng, lambda e, o=out, i=in_: e.dma_start(out=o, in_=i),
                         waits=waits, inc=inc, inc_k=16, _dma=True)

    def realize(self, nc, sems):
        for name in self.counts:
            assert name in sems, f"semaphore {name} not allocated"
        with nc.Block() as block:
            for eng in self.ENGINES:
                if not self.ops[eng]:
                    continue

                def make(body, eng=eng):
                    def _(e):
                        for fn, waits, inc, inc_k in body:
                            for sname, cnt in waits:
                                e.wait_ge(sems[sname], cnt)
                            inst = fn(e)
                            if inc is not None:
                                inst.then_inc(sems[inc], inc_k)
                    return _

                getattr(block, eng)(make(self.ops[eng]))


def _fresh(ctx, nc, names):
    sems = {}
    for n in names:
        sems[n] = ctx.enter_context(nc.semaphore(n))
    return sems



def _dram(nc, io, ns, name, shape, dtype, kind):
    if io is not None and name in io:
        return io[name]
    return nc.dram_tensor(ns + name, shape, dtype, kind=kind).ap()


def build_launch1():
    nc = bass.Bass("TRN2", target_bir_lowering=False)
    xT = nc.dram_tensor("xT", [D, TC], F32, kind="ExternalInput").ap()
    Ws, Bs, tabs, outs = {}, {}, {}, {}
    for w in "qkvg":
        Ws[w] = nc.dram_tensor(f"W{w}", [D, D], F32, kind="ExternalInput").ap()
        Bs[w] = nc.dram_tensor(f"b{w}", [128, 16], F32, kind="ExternalInput").ap()
    for t in ("cosq", "sinq", "cosk", "sink"):
        tabs[t] = nc.dram_tensor(t, [128, TC], F32, kind="ExternalInput").ap()
    for o in ("qr", "kr", "v", "g"):
        outs[o] = nc.dram_tensor(f"o{o}", [D, TC], BF16, kind="ExternalOutput").ap()

    NT = 4
    NB = 512
    p = Plan()
    with ExitStack() as ctx:
        e = ctx.enter_context
        xb = e(nc.sbuf_tensor("xb", [128, 16 * TC], BF16))
        wb = e(nc.sbuf_tensor("wb", [128, 16 * D], BF16))
        tb_ = {t: e(nc.sbuf_tensor(f"sb_{t}", [128, TC], F32)) for t in tabs}
        bb = {w: e(nc.sbuf_tensor(f"bb{w}", [128, 16], F32)) for w in "qkvg"}
        qa = e(nc.sbuf_tensor("qa", [128, NB], F32))
        qs = e(nc.sbuf_tensor("qs", [128, NB], F32))
        t1 = e(nc.sbuf_tensor("t1", [128, NB], F32))
        t2 = e(nc.sbuf_tensor("t2", [128, NB], F32))
        ob = [e(nc.sbuf_tensor(f"ob{i}", [128, NB], BF16)) for i in range(4)]
        ps = [e(nc.psum_tensor(f"ps{i}", [128, NB], F32)) for i in range(2)]
        sems = _fresh(ctx, nc, ["ldx", "ldw", "ldt", "mmS", "dvS", "evS",
                                "st0", "st1", "st2", "st3"])

        for dm in range(16):
            p.dma("gpsimd", xb[:, dm * TC:(dm + 1) * TC], xT[dm * 128:(dm + 1) * 128, :], inc="ldx")
        for t in tabs:
            p.dma("sync", tb_[t][:, :], tabs[t][:, :], inc="ldt")
        for w in "qkvg":
            p.dma("sync", bb[w][:, :], Bs[w][:, :], inc="ldt")
        x_ready = p.mark("ldx")
        tabs_ready = p.mark("ldt")

        pairswap = []
        for i in range(0, 32, 2):
            pairswap += [i + 1, i]

        last_bank_free = [None, None]
        last_ob_dma = [None, None, None, None]
        wload_done = None
        n_mm_groups = 0
        for wi, w in enumerate("qkvg"):
            wconsumed = (p.ns + "mmS", n_mm_groups) if wi > 0 else None
            wld = None
            for dm in range(16):
                wld = p.dma("gpsimd", wb[:, dm * D:(dm + 1) * D],
                            Ws[w][dm * 128:(dm + 1) * 128, :],
                            waits=(wconsumed,) if dm == 0 and wconsumed else (),
                            inc="ldw")
            for tbi in range(NT):
                for do in range(16):
                    bank = do % 2
                    idx = tbi * 16 + do
                    for dm in range(16):
                        waits = []
                        if dm == 0:
                            waits = [x_ready, wld, last_bank_free[bank]]
                        p.emit("tensor",
                               lambda e, b=bank, dmm=dm, doo=do, tt=tbi, ww=w: e.matmul(
                                   ps[b][:, :],
                                   wb[:, dmm * D + doo * 128: dmm * D + doo * 128 + 128],
                                   xb[:, dmm * TC + tt * NB: dmm * TC + tt * NB + NB],
                                   start=(dmm == 0), stop=(dmm == 15)),
                               waits=waits,
                               inc="mmS" if dm == 15 else None)
                    mm_done = p.mark("mmS")
                    n_mm_groups += 1
                    obi = idx % 4
                    if w in ("q", "k"):
                        ct = "cosq" if w == "q" else "cosk"
                        st = "sinq" if w == "q" else "sink"
                        bf_ = p.emit("vector",
                                     lambda e, b=bank, ww=w, doo=do: e.tensor_scalar(
                                         out=qa[:, :], in0=ps[b][:, :],
                                         scalar1=bb[ww][:, doo:doo + 1], scalar2=None,
                                         op0=ALU.add),
                                     waits=[mm_done, tabs_ready], inc="dvS")
                        last_bank_free[bank] = bf_
                        p.emit("vector", lambda e: e.stream_shuffle(qs[:, :], qa[:, :], pairswap))
                        p.emit("vector",
                               lambda e, tt=tbi, c=ct: e.tensor_tensor(
                                   out=t1[:, :], in0=qa[:, :],
                                   in1=tb_[c][:, tt * NB:(tt + 1) * NB], op=ALU.mult))
                        p.emit("vector",
                               lambda e, tt=tbi, s_=st: e.tensor_tensor(
                                   out=t2[:, :], in0=qs[:, :],
                                   in1=tb_[s_][:, tt * NB:(tt + 1) * NB], op=ALU.mult))
                        dv = p.emit("vector",
                                    lambda e, o=obi: e.tensor_tensor(
                                        out=ob[o][:, :], in0=t1[:, :], in1=t2[:, :],
                                        op=ALU.add),
                                    waits=[last_ob_dma[obi]], inc="dvS")
                        ready = dv
                    elif w == "v":
                        dv = p.emit("vector",
                                    lambda e, b=bank, o=obi, doo=do: e.tensor_scalar(
                                        out=ob[o][:, :], in0=ps[b][:, :],
                                        scalar1=bb["v"][:, doo:doo + 1], scalar2=None,
                                        op0=ALU.add),
                                    waits=[mm_done, last_ob_dma[obi]], inc="dvS")
                        last_bank_free[bank] = dv
                        ready = dv
                    else:
                        ev = p.emit("scalar",
                                    lambda e, b=bank, o=obi, doo=do: e.activation(
                                        out=ob[o][:, :], in_=ps[b][:, :], func=AF.Silu,
                                        bias=bb["g"][:, doo:doo + 1]),
                                    waits=[mm_done, last_ob_dma[obi]], inc="evS")
                        last_bank_free[bank] = ev
                        ready = ev
                    oname = {"q": "qr", "k": "kr", "v": "v", "g": "g"}[w]
                    st_ = p.dma("sync", outs[oname][do * 128:(do + 1) * 128,
                                                    tbi * NB:(tbi + 1) * NB],
                                ob[obi][:, :], waits=[ready], inc=f"st{obi}")
                    last_ob_dma[obi] = st_
        p.emit("sync", lambda e: e.nop(),
               waits=[p.mark(f"st{i}") for i in range(4) if p.has(f"st{i}")])
        p.realize(nc, sems)
    return nc


_ROPE_CACHE = None


def _launch1_inputs(inputs):
    global _ROPE_CACHE
    if _ROPE_CACHE is None:
        _ROPE_CACHE = _rope_tables()
    cosq, sinq, cosk, sink = _ROPE_CACHE
    x = np.asarray(inputs["L0_attn_x"])[0]
    xTfull = np.ascontiguousarray(x.T)
    maps = []
    for c in range(NCORE):
        sl = slice(c * TC, (c + 1) * TC)
        m = {
            "xT": np.ascontiguousarray(xTfull[:, sl]),
            "cosq": np.ascontiguousarray(cosq[:, sl]),
            "sinq": np.ascontiguousarray(sinq[:, sl]),
            "cosk": np.ascontiguousarray(cosk[:, sl]),
            "sink": np.ascontiguousarray(sink[:, sl]),
        }
        for w, nm in (("q", "Wq"), ("k", "Wk"), ("v", "Wv"), ("g", "Wg")):
            m[f"W{w}"] = np.ascontiguousarray(inputs[f"L0_attn_{nm}"].astype(f32))
            m[f"b{w}"] = _pack_bias(np.asarray(inputs[f"L0_attn_b{w}"]))
        maps.append(m)
    return maps


def run_launch1(inputs):
    nc = build_launch1()
    maps = _launch1_inputs(inputs)
    res = run_bass_kernel_spmd(nc, maps, core_ids=list(range(NCORE)))
    full = {}
    for o in ("qr", "kr", "v", "g"):
        full[o] = np.concatenate([res.results[c][f"o{o}"] for c in range(NCORE)], axis=1)
    return full


_NC_CACHE = {}


def _nc(name, builder):
    if name not in _NC_CACHE:
        _NC_CACHE[name] = builder()
    return _NC_CACHE[name]


def kernel(**inputs):
    return kernel_mega(**inputs)


def kernel_multilaunch(**inputs):
    cores = list(range(NCORE))
    res = run_bass_kernel_spmd(_nc("l1", build_launch1),
                               _launch1_inputs(inputs), core_ids=cores)
    l1out = {o: np.concatenate([res.results[c][f"o{o}"] for c in cores], axis=1)
             for o in ("qr", "kr", "v", "g")}
    res = run_bass_kernel_spmd(_nc("l2", build_launch2),
                               _launch2_inputs(l1out), core_ids=cores)
    y_full = np.concatenate([res.results[c]["yh"] for c in cores], axis=0)
    res = run_bass_kernel_spmd(_nc("l3", build_launch3),
                               _launch3_inputs(inputs, y_full), core_ids=cores)
    x2 = np.concatenate([res.results[c]["x2o"] for c in cores], axis=1)
    x2b = np.concatenate([res.results[c]["x2bo"] for c in cores], axis=1)
    res = run_bass_kernel_spmd(_nc("l4", build_launch4),
                               _launch4_inputs(inputs, x2b), core_ids=cores)
    olocal = np.concatenate([res.results[c]["olocal"] for c in cores], axis=1)
    qhat_all = np.stack([res.results[c]["qhato"] for c in cores])
    summ_all = np.stack([res.results[c]["summo"] for c in cores])
    res = run_bass_kernel_spmd(
        _nc("l5", build_launch5),
        _launch5_inputs(inputs, x2, olocal, qhat_all, summ_all),
        core_ids=cores)
    outx = np.concatenate([res.results[c]["outx"] for c in cores], axis=1)
    return np.ascontiguousarray(outx.T).reshape(1, S, D)


def build_launch2(nqb=32, nheads=2, mega=None):
    import os
    stop = os.environ.get("L2STOP", "full")
    if mega is None:
        nc, p, io, ns = bass.Bass("TRN2", target_bir_lowering=False), Plan(), None, ""
    else:
        nc, p, io, ns = mega["nc"], mega["p"], mega["io"], mega["ns"]
    p.set_ns(ns)
    qrh = _dram(nc, io, ns, "qrh", [256, S], BF16, "ExternalInput")
    krh = _dram(nc, io, ns, "krh", [256, S], BF16, "ExternalInput")
    vTh = _dram(nc, io, ns, "vTh", [256, S], BF16, "ExternalInput")
    gh = _dram(nc, io, ns, "gh", [256, S], BF16, "ExternalInput")
    qdec = _dram(nc, io, ns, "qdec", [256, 512], F32, "ExternalInput")
    kdec = _dram(nc, io, ns, "kdec", [256, 512], F32, "ExternalInput")
    pairsc = _dram(nc, io, ns, "pairsc", [256, 131], F32, "ExternalInput")
    masks = _dram(nc, io, ns, "masks", [128, 2048], BF16, "ExternalInput")
    yh = _dram(nc, io, ns, "yh", [256, S], BF16, "ExternalOutput")

    NB = 512
    with ExitStack() as ctx:
        e = ctx.enter_context
        ksb = e(nc.sbuf_tensor(ns + "ksb", [128, S], BF16))
        vsb = e(nc.sbuf_tensor(ns + "vsb", [128, S], BF16))
        qdec_sb = e(nc.sbuf_tensor(ns + "qdec_sb", [128, 1024], F32))
        kdec_sb = e(nc.sbuf_tensor(ns + "kdec_sb", [128, 1024], F32))
        prs_sb = e(nc.sbuf_tensor(ns + "prs_sb", [128, 262], F32))
        msk_sb = e(nc.sbuf_tensor(ns + "msk_sb", [128, 2048], BF16))
        qt_raw = [e(nc.sbuf_tensor(ns + f"qt_raw{i}", [128, NB], BF16)) for i in range(2)]
        gt = [e(nc.sbuf_tensor(ns + f"gt{i}", [128, NB], BF16)) for i in range(2)]
        qtil = e(nc.sbuf_tensor(ns + "qtil", [128, NB], BF16))
        ssb = [e(nc.sbuf_tensor(ns + f"ssb{i}", [128, NB], BF16)) for i in range(3)]
        rsb = e(nc.sbuf_tensor(ns + "rsb", [128, NB], F32))
        rbf = e(nc.sbuf_tensor(ns + "rbf", [128, NB], BF16))
        sqb = e(nc.sbuf_tensor(ns + "sqb", [128, NB], BF16))
        mu = e(nc.sbuf_tensor(ns + "mu", [1, NB], F32))
        msq = e(nc.sbuf_tensor(ns + "msq", [1, NB], F32))
        varr = e(nc.sbuf_tensor(ns + "varr", [1, NB], F32))
        rstd = e(nc.sbuf_tensor(ns + "rstd", [1, NB], F32))
        A_bf = e(nc.sbuf_tensor(ns + "A_bf", [1, NB], BF16))
        B_bf = e(nc.sbuf_tensor(ns + "B_bf", [1, NB], BF16))
        t1 = e(nc.sbuf_tensor(ns + "t1", [128, NB], F32))
        t2 = e(nc.sbuf_tensor(ns + "t2", [128, NB], F32))
        obuf = [e(nc.sbuf_tensor(ns + f"obuf{i}", [128, NB], BF16)) for i in range(2)]
        onescol = e(nc.sbuf_tensor(ns + "onescol", [128, 1], BF16))
        onesrow = e(nc.sbuf_tensor(ns + "onesrow", [1, 128], BF16))
        psA = e(nc.psum_tensor(ns + "psA", [128, NB], F32))
        psB = e(nc.psum_tensor(ns + "psB", [128, NB], F32))
        psC = e(nc.psum_tensor(ns + "psC", [128, NB], F32))
        ps_sc = [psA, psB, psC]
        ps_ret = e(nc.psum_tensor(ns + "ps_ret", [128, NB], F32))
        ps_st1 = e(nc.psum_tensor(ns + "ps_st1", [1, NB], F32))
        ps_st2 = e(nc.psum_tensor(ns + "ps_st2", [1, NB], F32))
        ps_bc1 = ps_ret
        ps_bc2 = e(nc.psum_tensor(ns + "ps_bc2", [128, NB], F32))
        sems = _fresh(mega["sem_ctx"] if mega else ctx, nc,
                      [ns + s_ for s_ in ("ldt", "ldk", "ldv", "ldq0", "ldq1",
                                          "ldg0", "ldg1", "mmS", "dvS", "evS",
                                          "plS", "st0", "st1")])

        p.emit("gpsimd", lambda e: e.memset(onescol[:, :], 1.0), inc="plS")
        p.emit("gpsimd", lambda e: e.memset(onesrow[:, :], 1.0), inc="plS")
        ones_ready = p.mark("plS")
        p.dma("sync", qdec_sb[:, 0:512], qdec[0:128, :], inc="ldt")
        p.dma("sync", qdec_sb[:, 512:1024], qdec[128:256, :], inc="ldt")
        p.dma("sync", kdec_sb[:, 0:512], kdec[0:128, :], inc="ldt")
        p.dma("sync", kdec_sb[:, 512:1024], kdec[128:256, :], inc="ldt")
        p.dma("sync", prs_sb[:, 0:131], pairsc[0:128, :], inc="ldt")
        p.dma("sync", prs_sb[:, 131:262], pairsc[128:256, :], inc="ldt")
        p.dma("sync", msk_sb[:, :], masks[:, :], inc="ldt")
        tabs_ready = p.mark("ldt")

        last_sc_free = [None, None, None]
        last_ssb_freed = [None, None, None]
        last_q_use = [None, None]
        last_g_use = [None, None]
        last_ob_dma = [None, None]
        ret_free = None
        st_free = None
        bc_free = None

        for h in range(nheads):
            kv_war = p.mark0("mmS") if h > 0 else None
            for i in range(4):
                w = [kv_war] if (kv_war and i == 0) else []
                kld = p.dma("sync", ksb[:, i * 4096:(i + 1) * 4096],
                            krh[h * 128:(h + 1) * 128, i * 4096:(i + 1) * 4096],
                            waits=w, inc="ldk")
            for i in range(4):
                vld = p.dma("sync", vsb[:, i * 4096:(i + 1) * 4096],
                            vTh[h * 128:(h + 1) * 128, i * 4096:(i + 1) * 4096],
                            waits=[kv_war] if (kv_war and i == 0) else [],
                            inc="ldv")
            pre = []
            for ci in range(32):
                pre.append(p.emit("vector",
                                  lambda e, c=ci, hh=h: e.tensor_tensor(
                                      out=ksb[:, c * NB:(c + 1) * NB],
                                      in0=ksb[:, c * NB:(c + 1) * NB],
                                      in1=kdec_sb[:, hh * NB:(hh + 1) * NB],
                                      op=ALU.mult),
                                  waits=[kld, tabs_ready] if ci == 0 else [],
                                  inc="dvS"))
            k_ready = pre[-1]
            v_ready = vld

            for qb in range(nqb):
                par = qb % 2
                qdma = p.dma("sync", qt_raw[par][:, :],
                             qrh[h * 128:(h + 1) * 128, qb * NB:(qb + 1) * NB],
                             waits=[last_q_use[par]], inc=f"ldq{par}")
                gdma = p.dma("sync", gt[par][:, :],
                             gh[h * 128:(h + 1) * 128, qb * NB:(qb + 1) * NB],
                             waits=[last_g_use[par]], inc=f"ldg{par}")
                qprev_mm = p.mark0("mmS") if (qb > 0 or h > 0) else None
                qrdy = p.emit("vector",
                              lambda e, pr=par, hh=h: e.tensor_tensor(
                                  out=qtil[:, :], in0=qt_raw[pr][:, :],
                                  in1=qdec_sb[:, hh * NB:(hh + 1) * NB], op=ALU.mult),
                              waits=[qdma, tabs_ready, qprev_mm], inc="dvS")
                last_q_use[par] = qrdy

                nkt = 4 * qb + 4
                evac_done = [None] * nkt

                def emit_mm2(kt):
                    w2 = [evac_done[kt]]
                    if kt == 0:
                        w2 += [v_ready, bc_free]
                        w2 += list(ret_free or ())
                    p.emit("tensor",
                           lambda e, k_=kt, last=(kt == nkt - 1): e.matmul(
                               ps_ret[:, :], vsb[:, k_ * 128:(k_ + 1) * 128],
                               ssb[k_ % 3][:, :], start=(k_ == 0), stop=last),
                           waits=w2, inc="mmS")
                    last_ssb_freed[kt % 3] = p.mark("mmS")

                for kt in range(nkt):
                    b = kt % 3
                    w1 = [qrdy] if kt == 0 else []
                    if kt == 1:
                        w1 = [qrdy]
                    if last_sc_free[b] is not None:
                        w1.append(last_sc_free[b])
                    if kt == 0:
                        w1.append(k_ready)
                    p.emit("tensor",
                           lambda e, bb=b, k_=kt: e.matmul(
                               ps_sc[bb][:, :], ksb[:, k_ * 128:(k_ + 1) * 128],
                               qtil[:, :], start=True, stop=True),
                           waits=w1, inc="mmS")
                    mm1_done = p.mark("mmS")

                    m = 4 * qb - kt
                    col = h * 131 + (m + 3)
                    diag = kt > 4 * qb - 1 - 0
                    diag = kt >= 4 * qb
                    ssb_w = [mm1_done]
                    if last_ssb_freed[b] is not None:
                        ssb_w.append(last_ssb_freed[b])
                    if diag:
                        d = kt - 4 * qb
                        ev = p.emit("vector",
                                    lambda e, bb=b, c=col: e.tensor_scalar(
                                        out=ssb[bb][:, :], in0=ps_sc[bb][:, :],
                                        scalar1=prs_sb[:, c:c + 1], scalar2=None,
                                        op0=ALU.mult),
                                    waits=ssb_w, inc="dvS")
                        ev = p.emit("vector",
                                    lambda e, bb=b, dd=d: e.tensor_tensor(
                                        out=ssb[bb][:, :], in0=ssb[bb][:, :],
                                        in1=msk_sb[:, dd * NB:(dd + 1) * NB],
                                        op=ALU.mult),
                                    inc="dvS")
                    elif kt % 2 == 0:
                        ev = p.emit("scalar",
                                    lambda e, bb=b, c=col: e.activation(
                                        out=ssb[bb][:, :], in_=ps_sc[bb][:, :],
                                        func=AF.Copy, scale=prs_sb[:, c:c + 1]),
                                    waits=ssb_w, inc="evS")
                    else:
                        ev = p.emit("vector",
                                    lambda e, bb=b, c=col: e.tensor_scalar(
                                        out=ssb[bb][:, :], in0=ps_sc[bb][:, :],
                                        scalar1=prs_sb[:, c:c + 1], scalar2=None,
                                        op0=ALU.mult),
                                    waits=ssb_w, inc="dvS")
                    last_sc_free[b] = ev
                    evac_done[kt] = ev
                    if stop == "full" and kt >= 2:
                        emit_mm2(kt - 2)

                    if stop == "evac":
                        if kt == nkt - 1:
                            od = p.dma("sync",
                                       yh[h * 128:(h + 1) * 128, qb * NB:(qb + 1) * NB],
                                       ssb[kt % 2][:, :], waits=[ev], inc=f"st{par}")
                            last_ob_dma[par] = od
                            last_ssb_freed[b] = od
                        else:
                            last_ssb_freed[b] = ev
                        continue

                    if stop != "full":
                        w2 = [ev]
                        if kt == 0:
                            w2 += [v_ready, bc_free]
                            w2 += list(ret_free or ())
                        p.emit("tensor",
                               lambda e, k_=kt, last=(kt == nkt - 1): e.matmul(
                                   ps_ret[:, :],
                                   vsb[:, k_ * 128:(k_ + 1) * 128],
                                   ssb[k_ % 3][:, :], start=(k_ == 0),
                                   stop=last),
                               waits=w2, inc="mmS")
                        last_ssb_freed[b] = p.mark("mmS")
                if stop == "full":
                    for kt in range(max(nkt - 2, 0), nkt):
                        emit_mm2(kt)
                ret_done = p.mark("mmS")
                if stop == "evac":
                    continue
                if stop == "ret":
                    r1 = p.emit("scalar", lambda e: e.activation(
                        out=rsb[:, :], in_=ps_ret[:, :], func=AF.Copy, scale=1.0),
                        waits=[ret_done], inc="evS")
                    yv = p.emit("vector", lambda e, pr=par: e.tensor_copy(
                        obuf[pr][:, :], rsb[:, :]), waits=[r1, last_ob_dma[par]],
                        inc="dvS")
                    ret_free = (yv,)
                    od = p.dma("sync", yh[h * 128:(h + 1) * 128, qb * NB:(qb + 1) * NB],
                               obuf[par][:, :], waits=[yv], inc=f"st{par}")
                    last_ob_dma[par] = od
                    last_g_use[par] = gdma
                    continue

                r1 = p.emit("scalar", lambda e: e.activation(
                    out=rsb[:, :], in_=ps_ret[:, :], func=AF.Copy, scale=1.0),
                    waits=[ret_done, bc_free], inc="evS")
                r3 = p.emit("scalar", lambda e: e.activation(
                    out=sqb[:, :], in_=ps_ret[:, :], func=AF.Square),
                    waits=[ret_done], inc="evS")
                r2 = p.emit("vector", lambda e: e.tensor_copy(rbf[:, :], ps_ret[:, :]),
                            waits=[ret_done, r3], inc="dvS")
                ret_free = (r2, r3)
                p.emit("tensor", lambda e: e.matmul(
                    ps_st1[:, :], onescol[:, :], rbf[:, :], start=True, stop=True),
                    waits=[r2, ones_ready, st_free], inc="mmS")
                s1 = p.mark("mmS")
                p.emit("tensor", lambda e: e.matmul(
                    ps_st2[:, :], onescol[:, :], sqb[:, :], start=True, stop=True),
                    waits=[r3], inc="mmS")
                s2 = p.mark("mmS")
                m1 = p.emit("vector", lambda e: e.tensor_scalar(
                    out=mu[:, :], in0=ps_st1[:, :], scalar1=float(1.0 / 128.0),
                    scalar2=None, op0=ALU.mult), waits=[s1], inc="dvS")
                m2 = p.emit("scalar", lambda e: e.activation(
                    out=msq[:, :], in_=mu[:, :], func=AF.Square), waits=[m1], inc="evS")
                v1 = p.emit("vector", lambda e: e.tensor_scalar(
                    out=varr[:, :], in0=ps_st2[:, :], scalar1=float(1.0 / 128.0),
                    scalar2=None, op0=ALU.mult), waits=[s2], inc="dvS")
                st_free = v1
                v2 = p.emit("vector", lambda e: e.tensor_tensor(
                    out=varr[:, :], in0=varr[:, :], in1=msq[:, :], op=ALU.subtract),
                    waits=[m2], inc="dvS")
                l1 = p.emit("scalar", lambda e: e.activation(
                    out=rstd[:, :], in_=varr[:, :], func=AF.Ln, bias=np.float32(1e-5)),
                    waits=[v2], inc="evS")
                l2 = p.emit("scalar", lambda e: e.activation(
                    out=rstd[:, :], in_=rstd[:, :], func=AF.Exp, scale=np.float32(-0.5)),
                    inc="evS")
                if stop == "stats":
                    yv = p.emit("vector", lambda e, pr=par: e.tensor_copy(
                        obuf[pr][:, :], rsb[:, :]), waits=[r1, l2, last_ob_dma[par]],
                        inc="dvS")
                    ret_free = (r2, r3)
                    st_free = v1
                    od = p.dma("sync", yh[h * 128:(h + 1) * 128, qb * NB:(qb + 1) * NB],
                               obuf[par][:, :], waits=[yv], inc=f"st{par}")
                    last_ob_dma[par] = od
                    last_g_use[par] = gdma
                    continue
                a1 = p.emit("vector", lambda e: e.tensor_copy(A_bf[:, :], rstd[:, :]),
                            waits=[l2], inc="dvS")
                b1 = p.emit("vector", lambda e: e.scalar_tensor_tensor(
                    out=B_bf[:, :], in0=mu[:, :], scalar=float(-1.0),
                    in1=rstd[:, :], op0=ALU.mult, op1=ALU.mult),
                    waits=[l2], inc="dvS")
                p.emit("tensor", lambda e: e.matmul(
                    ps_bc1[:, :], onesrow[:, :], A_bf[:, :], start=True, stop=True),
                    waits=[a1, bc_free], inc="mmS")
                c1 = p.mark("mmS")
                p.emit("tensor", lambda e: e.matmul(
                    ps_bc2[:, :], onesrow[:, :], B_bf[:, :], start=True, stop=True),
                    waits=[b1], inc="mmS")
                c2 = p.mark("mmS")
                u1 = p.emit("vector", lambda e: e.tensor_tensor(
                    out=t1[:, :], in0=rsb[:, :], in1=ps_bc1[:, :], op=ALU.mult),
                    waits=[c1, r1], inc="dvS")
                u2 = p.emit("vector", lambda e: e.tensor_tensor(
                    out=t2[:, :], in0=t1[:, :], in1=ps_bc2[:, :], op=ALU.add),
                    waits=[c2], inc="dvS")
                bc_free = u2
                if stop == "bc":
                    yv = p.emit("vector", lambda e, pr=par: e.tensor_copy(
                        obuf[pr][:, :], t2[:, :]), waits=[last_ob_dma[par]], inc="dvS")
                    ret_free = (r2, r3)
                    st_free = v1
                    od = p.dma("sync", yh[h * 128:(h + 1) * 128, qb * NB:(qb + 1) * NB],
                               obuf[par][:, :], waits=[yv], inc=f"st{par}")
                    last_ob_dma[par] = od
                    last_g_use[par] = gdma
                    continue
                y1 = p.emit("vector", lambda e, pr=par: e.tensor_tensor(
                    out=obuf[pr][:, :], in0=t2[:, :], in1=gt[pr][:, :], op=ALU.mult),
                    waits=[gdma, last_ob_dma[par]], inc="dvS")
                last_g_use[par] = y1
                od = p.dma("sync", yh[h * 128:(h + 1) * 128, qb * NB:(qb + 1) * NB],
                           obuf[par][:, :], waits=[y1], inc=f"st{par}")
                last_ob_dma[par] = od

        p.emit("sync", lambda e: e.nop(),
               waits=[p.mark(s_) for s_ in ("st0", "st1") if p.has(s_)])
        if mega is None:
            p.realize(nc, sems)
        else:
            mega["sems"].update(sems)
    return nc


def _launch2_inputs(l1out):
    maps = []
    masks = np.ascontiguousarray(_diag_masks())
    for c in range(NCORE):
        rows = slice(256 * c, 256 * c + 256)
        vT = np.empty((256, S), dtype=BF)
        qd = np.empty((256, 512), dtype=f32)
        kd = np.empty((256, 512), dtype=f32)
        pscl = np.empty((256, 131), dtype=f32)
        for hi in range(2):
            h = 2 * c + hi
            vh = l1out["v"][h * 128:(h + 1) * 128, :]
            vT[hi * 128:(hi + 1) * 128, :] = (
                vh.T.reshape(128, 128, 128).transpose(1, 0, 2).reshape(128, S))
            a, b, cc_ = _decay_tiles(h)
            qd[hi * 128:(hi + 1) * 128] = a
            kd[hi * 128:(hi + 1) * 128] = b
            pscl[hi * 128:(hi + 1) * 128] = cc_
        maps.append({
            "qrh": np.ascontiguousarray(l1out["qr"][rows]),
            "krh": np.ascontiguousarray(l1out["kr"][rows]),
            "vTh": vT,
            "gh": np.ascontiguousarray(l1out["g"][rows]),
            "qdec": qd, "kdec": kd, "pairsc": pscl, "masks": masks,
        })
    return maps


def run_launch2(l1out, nqb=32):
    nc = build_launch2(nqb=nqb)
    maps = _launch2_inputs(l1out)
    res = run_bass_kernel_spmd(nc, maps, core_ids=list(range(NCORE)))
    return np.concatenate([res.results[c]["yh"] for c in range(NCORE)], axis=0)


def build_launch3(mega=None):
    if mega is None:
        nc, io, ns, pre = bass.Bass("TRN2", target_bir_lowering=False), None, "", None
        p = Plan()
    else:
        nc, p, io, ns = mega["nc"], mega["p"], mega["io"], mega["ns"]
        pre = mega.get("pre")
    p.set_ns(ns)
    y_sl = _dram(nc, io, ns, "y_sl", [D, TC], BF16, "ExternalInput")
    Wo = _dram(nc, io, ns, "Wo", [D, D], F32, "ExternalInput")
    bo = _dram(nc, io, ns, "bo", [128, 16], F32, "ExternalInput")
    xT = _dram(nc, io, ns, "xT", [D, TC], F32, "ExternalInput")
    wfc_t = _dram(nc, io, ns, "wfc_t", [64 * 16 * 128, 128], F32, "ExternalInput")
    wproj_t = _dram(nc, io, ns, "wproj_t", [16 * 64 * 128, 128], F32, "ExternalInput")
    x2o = _dram(nc, io, ns, "x2o", [D, TC], F32, "ExternalOutput")
    x2bo = _dram(nc, io, ns, "x2bo", [D, TC], BF16, "ExternalOutput")
    x1buf = _dram(nc, io, ns, "x1buf", [D, TC], F32, "Internal")
    xnbuf = _dram(nc, io, ns, "xnbuf", [D, TC], BF16, "Internal")
    tfc = _dram(nc, io, ns, "tfc", [64 * 16 * 128, 128], BF16, "Internal")
    tproj = _dram(nc, io, ns, "tproj", [16 * 64 * 128, 128], BF16, "Internal")

    NB = 512
    with ExitStack() as ctx:
        e = ctx.enter_context
        A = e(nc.sbuf_tensor(ns + "A", [128, 16 * NB], BF16))
        B = e(nc.sbuf_tensor(ns + "B", [128, 16 * TC], BF16))
        s_sb = [e(nc.sbuf_tensor(ns + f"s_sb{i}", [128, NB], F32)) for i in range(16)]
        xr = [e(nc.sbuf_tensor(ns + f"xr{i}", [128, NB], F32)) for i in range(2)]
        if pre is None:
            wstr = [e(nc.sbuf_tensor(ns + f"wstr{i}", [128, 2048], F32)) for i in range(2)]
            aa_t = e(nc.sbuf_tensor(ns + "aa_t", [128, 2048], F32))
            msk_t = e(nc.sbuf_tensor(ns + "msk_t", [128, 2048], BF16))
            sgn_t = e(nc.sbuf_tensor(ns + "sgn_t", [128, 2048], BF16))
            tstr = [e(nc.sbuf_tensor(ns + f"tstr{i}", [128, 2048], BF16)) for i in range(2)]
        else:
            tstr = pre["tstr"]
        pstr = e(nc.sbuf_tensor(ns + "pstr", [128, 8192], BF16))
        sbf_t = e(nc.sbuf_tensor(ns + "sbf_t", [128, NB], BF16))
        xnb_t = e(nc.sbuf_tensor(ns + "xnb_t", [128, NB], BF16))
        sq_t = e(nc.sbuf_tensor(ns + "sq_t", [128, NB], BF16))
        tmp1 = e(nc.sbuf_tensor(ns + "tmp1", [128, NB], F32))
        tmp2 = e(nc.sbuf_tensor(ns + "tmp2", [128, NB], F32))
        Af_t = e(nc.sbuf_tensor(ns + "Af_t", [128, NB], F32))
        Bf_t = e(nc.sbuf_tensor(ns + "Bf_t", [128, NB], F32))
        bo_sb = e(nc.sbuf_tensor(ns + "bo_sb", [128, 16], F32))
        onescol = e(nc.sbuf_tensor(ns + "onescol", [128, 1], BF16))
        onesrow = e(nc.sbuf_tensor(ns + "onesrow", [1, 128], BF16))
        if pre is None:
            onescol_f = e(nc.sbuf_tensor(ns + "onescol_f", [128, 1], F32))
            onesrow_f = e(nc.sbuf_tensor(ns + "onesrow_f", [1, 128], F32))
            wacc = e(nc.sbuf_tensor(ns + "wacc", [128, 1], F32))
            wacc2 = e(nc.sbuf_tensor(ns + "wacc2", [128, 1], F32))
            red1 = e(nc.sbuf_tensor(ns + "red1", [128, 1], F32))
            sc1 = e(nc.sbuf_tensor(ns + "sc1", [1, 4], F32))
            thrb = e(nc.sbuf_tensor(ns + "thrb", [128, 1], F32))
            gamb = e(nc.sbuf_tensor(ns + "gamb", [128, 1], F32))
            thrb2 = e(nc.sbuf_tensor(ns + "thrb2", [128, 1], F32))
            gamb2 = e(nc.sbuf_tensor(ns + "gamb2", [128, 1], F32))
        else:
            gamb, thrb = pre["gamb"], pre["thrb"]
            gamb2, thrb2 = pre["gamb2"], pre["thrb2"]
        mu = e(nc.sbuf_tensor(ns + "mu", [1, NB], F32))
        msq = e(nc.sbuf_tensor(ns + "msq", [1, NB], F32))
        varr = e(nc.sbuf_tensor(ns + "varr", [1, NB], F32))
        rstd = e(nc.sbuf_tensor(ns + "rstd", [1, NB], F32))
        r2t = e(nc.sbuf_tensor(ns + "r2t", [1, NB], F32))
        var2 = e(nc.sbuf_tensor(ns + "var2", [1, NB], F32))
        rstd2 = e(nc.sbuf_tensor(ns + "rstd2", [1, NB], F32))
        A_bf = e(nc.sbuf_tensor(ns + "A_bf", [1, NB], BF16))
        B_bf = e(nc.sbuf_tensor(ns + "B_bf", [1, NB], BF16))
        C_bf = e(nc.sbuf_tensor(ns + "C_bf", [1, NB], BF16))
        ps_a = [e(nc.psum_tensor(ns + f"ps_a{i}", [128, NB], F32)) for i in range(2)]
        ps_s1 = e(nc.psum_tensor(ns + "ps_s1", [1, NB], F32))
        ps_s2 = e(nc.psum_tensor(ns + "ps_s2", [1, NB], F32))
        ps_b1 = e(nc.psum_tensor(ns + "ps_b1", [128, NB], F32))
        ps_b2 = e(nc.psum_tensor(ns + "ps_b2", [128, NB], F32))
        ps_b3 = e(nc.psum_tensor(ns + "ps_b3", [128, NB], F32))
        if pre is None:
            ps_sc = e(nc.psum_tensor(ns + "ps_sc", [1, 8], F32))

        sems = _fresh(mega["sem_ctx"] if mega else ctx, nc,
                      [ns + s_ for s_ in ("lda", "ldw", "ldx0", "ldx1b", "ldt",
                                          "ldstr0", "ldstr1", "ldpr0", "ldpr1",
                                          "ldpr2", "ldpr3", "wx1",
                                          "wxn", "wx2b", "mmS", "dvS", "evS",
                                          "plS", "st0", "stw0", "stw1")])

        p.emit("gpsimd", lambda e: e.memset(onescol[:, :], 1.0))
        ones_ready = p.emit("gpsimd", lambda e: e.memset(onesrow[:, :], 1.0))
        if pre is None:
            p.emit("gpsimd", lambda e: e.memset(onescol_f[:, :], 1.0))
            ones_ready = p.emit("gpsimd",
                                lambda e: e.memset(onesrow_f[:, :], 1.0))
        wld = None
        for i in range(16):
            wld = p.dma("gpsimd", B[:, i * TC:(i + 1) * TC],
                        Wo[i * 128:(i + 1) * 128, :], inc="ldw")
        p.dma("sync", bo_sb[:, :], bo[:, :], inc="ldt")
        tl = p.mark("ldt")

        if pre is None:
            macc = {}
            for wi, (wt, acc) in enumerate(((wfc_t, wacc), (wproj_t, wacc2))):
                for i in range(64):
                    srcv = wt[i * 2048:(i + 1) * 2048, :].rearrange(
                        "(a p) b -> p a b", p=128)
                    ld = p.dma("sync", wstr[wi][:, :].rearrange("p (a b) -> p a b", a=16), srcv,
                               waits=[macc.get((wi, i - 1))], inc=f"ldstr{wi}")
                    p.emit("vector", lambda e, w_=wi: e.tensor_reduce(
                        out=red1[:, :], in_=wstr[w_][:, :], axis=AX.X,
                        op=ALU.add, apply_absolute_value=True), waits=[ld])
                    ac = p.emit("vector", lambda e, a_=acc, first=(i == 0): (
                        e.tensor_copy(a_[:, :], red1[:, :]) if first else
                        e.tensor_tensor(out=a_[:, :], in0=a_[:, :], in1=red1[:, :],
                                        op=ALU.add)))
                    macc[(wi, i)] = ac

            def scalar_reduce(acc, gbuf, tbuf, denom, col, ps_big):
                t = p.emit("tensor", lambda e, a_=acc, c_=col: e.matmul(
                    ps_sc[0:1, c_:c_ + 1], onescol_f[:, :], a_[:, :],
                    start=True, stop=True),
                    waits=[macc[(0, 63)], macc[(1, 63)], ones_ready], inc="mmS")
                g1 = p.emit("vector", lambda e, c_=col: e.tensor_scalar(
                    out=sc1[0:1, c_:c_ + 1], in0=ps_sc[0:1, c_:c_ + 1],
                    scalar1=float(1.0 / denom), scalar2=None, op0=ALU.mult),
                    waits=[t])
                t2_ = p.emit("vector", lambda e, c_=col: e.tensor_scalar(
                    out=sc1[0:1, c_ + 1:c_ + 2], in0=sc1[0:1, c_:c_ + 1],
                    scalar1=float(1e-6), scalar2=float(0.5), op0=ALU.add,
                    op1=ALU.mult))
                bg = p.emit("tensor", lambda e, c_=col, ps=ps_big: e.matmul(
                    ps[:, 0:2], onesrow_f[:, :], sc1[0:1, c_:c_ + 2],
                    start=True, stop=True), waits=[t2_], inc="mmS")
                c1 = p.emit("scalar", lambda e, g_=gbuf, ps=ps_big: e.activation(
                    out=g_[:, :], in_=ps[:, 0:1], func=AF.Copy, scale=1.0),
                    waits=[bg], inc="evS")
                c2 = p.emit("scalar", lambda e, t_=tbuf, ps=ps_big: e.activation(
                    out=t_[:, :], in_=ps[:, 1:2], func=AF.Copy, scale=1.0),
                    inc="evS")
                return (g1, c2)
            thr_ready = scalar_reduce(wacc, gamb, thrb, 2048.0 * 8192.0, 0, ps_b1)
            thr2_ready = scalar_reduce(wacc2, gamb2, thrb2, 8192.0 * 2048.0, 2, ps_b2)
        else:
            thr_ready, thr2_ready = pre["thr_ready"], pre["thr2_ready"]


        last_xr = [None, None]
        st_done = None
        bc_done = None
        bank_free = [None, None]
        a_free = None
        prev_s1m = prev_s2m = None
        tmp1_dma = None
        xnb_dma = None
        p1_tmp2_free = None
        for tbi in range(4):
            alds = []
            for vt in range(16):
                alds.append(p.dma("sync", A[:, vt * NB:(vt + 1) * NB],
                                  y_sl[vt * 128:(vt + 1) * 128,
                                       tbi * NB:(tbi + 1) * NB],
                                  waits=[a_free] if vt == 0 else [], inc="lda"))
            a_rdy = alds[-1]
            for do in range(16):
                bank = do % 2
                for vt in range(16):
                    w1 = []
                    if vt == 0:
                        w1 = [a_rdy, wld, bank_free[bank]]
                        if do == 0 and tbi == 0:
                            w1 += [thr_ready[0], thr2_ready[0]]
                    p.emit("tensor",
                           lambda e, b=bank, v_=vt, d_=do: e.matmul(
                               ps_a[b][:, :],
                               B[:, v_ * TC + d_ * 128: v_ * TC + d_ * 128 + 128],
                               A[:, v_ * NB:(v_ + 1) * NB],
                               start=(v_ == 0), stop=(v_ == 15)),
                           waits=w1, inc="mmS" if vt == 15 else None)
                mm_done = p.mark("mmS")
                if do == 15:
                    a_free = mm_done
                xld = p.dma("sync", xr[bank][:, :],
                            xT[do * 128:(do + 1) * 128, tbi * NB:(tbi + 1) * NB],
                            waits=[last_xr[bank]],
                            inc=("ldx0" if bank == 0 else "ldx1b"))
                sv = p.emit("vector",
                            lambda e, b=bank, d_=do: e.scalar_tensor_tensor(
                                out=s_sb[d_][:, :], in0=ps_a[b][:, :],
                                scalar=bo_sb[:, d_:d_ + 1], in1=xr[b][:, :],
                                op0=ALU.add, op1=ALU.add),
                            waits=[mm_done, xld, tl], inc="dvS")
                last_xr[bank] = sv
                bank_free[bank] = sv
                sb_ = p.emit("vector", lambda e, d_=do: e.tensor_copy(
                    sbf_t[:, :], s_sb[d_][:, :]), waits=[prev_s1m], inc="dvS")
                sq_ = p.emit("scalar", lambda e: e.activation(
                    out=sq_t[:, :], in_=sbf_t[:, :], func=AF.Square),
                    waits=[sb_, prev_s2m], inc="evS")
                w_st1 = [sb_, ones_ready]
                if do == 0:
                    w_st1 += [st_done]
                prev_s1m = p.emit("tensor", lambda e, d_=do: e.matmul(
                    ps_s1[:, :], onescol[:, :], sbf_t[:, :],
                    start=(d_ == 0), stop=(d_ == 15)), waits=w_st1, inc="mmS")
                w_st2 = [sq_]
                if do == 0:
                    w_st2 += [st_done]
                prev_s2m = p.emit("tensor", lambda e, d_=do: e.matmul(
                    ps_s2[:, :], onescol[:, :], sq_t[:, :],
                    start=(d_ == 0), stop=(d_ == 15)), waits=w_st2, inc="mmS")
            stats_done = p.mark("mmS")
            m1 = p.emit("vector", lambda e: e.tensor_scalar(
                out=mu[:, :], in0=ps_s1[:, :], scalar1=float(1.0 / 2048.0),
                scalar2=None, op0=ALU.mult), waits=[stats_done], inc="dvS")
            m2 = p.emit("scalar", lambda e: e.activation(
                out=msq[:, :], in_=mu[:, :], func=AF.Square),
                waits=[m1], inc="evS")
            v1 = p.emit("vector", lambda e: e.tensor_scalar(
                out=varr[:, :], in0=ps_s2[:, :], scalar1=float(1.0 / 2048.0),
                scalar2=None, op0=ALU.mult), inc="dvS")
            st_done = v1
            v2 = p.emit("vector", lambda e: e.tensor_tensor(
                out=varr[:, :], in0=varr[:, :], in1=msq[:, :], op=ALU.subtract),
                waits=[m2], inc="dvS")
            l1 = p.emit("scalar", lambda e: e.activation(
                out=rstd[:, :], in_=varr[:, :], func=AF.Ln, bias=np.float32(1e-5)),
                waits=[v2], inc="evS")
            l2 = p.emit("scalar", lambda e: e.activation(
                out=rstd[:, :], in_=rstd[:, :], func=AF.Exp,
                scale=np.float32(-0.5)), inc="evS")
            w1_ = p.emit("vector", lambda e: e.tensor_scalar(
                out=r2t[:, :], in0=varr[:, :], scalar1=float(1e-5),
                scalar2=None, op0=ALU.add), inc="dvS")
            w2_ = p.emit("vector", lambda e: e.reciprocal(r2t[:, :], r2t[:, :]),
                         inc="dvS")
            w3_ = p.emit("vector", lambda e: e.tensor_tensor(
                out=var2[:, :], in0=varr[:, :], in1=r2t[:, :], op=ALU.mult),
                inc="dvS")
            w4_ = p.emit("scalar", lambda e: e.activation(
                out=rstd2[:, :], in_=var2[:, :], func=AF.Ln,
                bias=np.float32(1e-6)), waits=[w3_], inc="evS")
            w5_ = p.emit("scalar", lambda e: e.activation(
                out=rstd2[:, :], in_=rstd2[:, :], func=AF.Exp,
                scale=np.float32(-0.5)), inc="evS")
            a1 = p.emit("vector", lambda e: e.tensor_copy(A_bf[:, :], rstd[:, :]),
                        waits=[l2], inc="dvS")
            b1 = p.emit("vector", lambda e: e.scalar_tensor_tensor(
                out=B_bf[:, :], in0=mu[:, :], scalar=float(-1.0), in1=rstd[:, :],
                op0=ALU.mult, op1=ALU.mult), inc="dvS")
            c1_ = p.emit("vector", lambda e: e.tensor_copy(C_bf[:, :],
                                                           rstd2[:, :]),
                         waits=[w5_], inc="dvS")
            wb1_ = [a1, ones_ready, bc_done]
            if tbi == 0:
                wb1_ += [thr_ready[1], thr2_ready[1]]
            p.emit("tensor", lambda e: e.matmul(
                ps_b1[:, :], onesrow[:, :], A_bf[:, :], start=True, stop=True),
                waits=wb1_, inc="mmS")
            cA = p.mark("mmS")
            p.emit("tensor", lambda e: e.matmul(
                ps_b2[:, :], onesrow[:, :], B_bf[:, :], start=True, stop=True),
                waits=[b1], inc="mmS")
            cB = p.mark("mmS")
            p.emit("tensor", lambda e: e.matmul(
                ps_b3[:, :], onesrow[:, :], C_bf[:, :], start=True, stop=True),
                waits=[c1_], inc="mmS")
            cC = p.mark("mmS")
            u3 = None
            for do in range(16):
                u1 = p.emit("vector", lambda e, d_=do: e.tensor_tensor(
                    out=tmp1[:, :], in0=s_sb[d_][:, :], in1=ps_b1[:, :],
                    op=ALU.mult), waits=[cA, tmp1_dma], inc="dvS")
                u2 = p.emit("vector", lambda e: e.tensor_tensor(
                    out=tmp1[:, :], in0=tmp1[:, :], in1=ps_b2[:, :], op=ALU.add),
                    waits=[cB], inc="dvS")
                x1d = p.dma("sync", x1buf[do * 128:(do + 1) * 128,
                                          tbi * NB:(tbi + 1) * NB],
                            tmp1[:, :], waits=[u2], inc="wx1")
                tmp1_dma = x1d
                u3 = p.emit("vector", lambda e: e.tensor_tensor(
                    out=tmp2[:, :], in0=tmp1[:, :], in1=ps_b3[:, :],
                    op=ALU.mult), waits=[cC], inc="dvS")
                u4 = p.emit("vector", lambda e: e.tensor_copy(xnb_t[:, :],
                                                              tmp2[:, :]),
                            waits=[xnb_dma], inc="dvS")
                p1_tmp2_free = u4
                xnd = p.dma("sync", xnbuf[do * 128:(do + 1) * 128,
                                          tbi * NB:(tbi + 1) * NB],
                            xnb_t[:, :], waits=[u4], inc="wxn")
                xnb_dma = xnd
            bc_done = u3
        p1_xn_dma = p.mark("wxn")
        p1_x1_dma = p.mark("wx1")

        if pre is None:
            tern_mark = {}
            for wi, (wt, dst, tb_buf) in enumerate(((wfc_t, tfc, thrb),
                                                    (wproj_t, tproj, thrb2))):
                thr_m = thr_ready[1] if wi == 0 else thr2_ready[1]
                for j in range(64):
                    buf = j % 2
                    srcv = wt[j * 2048:(j + 1) * 2048, :].rearrange(
                        "(a p) b -> p a b", p=128)
                    ld = p.dma("sync", wstr[buf][:, :].rearrange("p (a b) -> p a b", a=16), srcv,
                               waits=[tern_mark.get(("sgn", wi, j - 2)),
                                      macc[(wi, 63)] if j < 2 else None,
                                      macc[(1, 63)] if (wi == 0 and j < 2) else None],
                               inc=f"ldstr{buf}")
                    a_ = p.emit("scalar", lambda e, b=buf: e.activation(
                        out=aa_t[:, :], in_=wstr[b][:, :], func=AF.Abs),
                        waits=[ld, thr_m], inc="evS")
                    m_ = p.emit("vector", lambda e, t_=tb_buf: e.tensor_scalar(
                        out=msk_t[:, :], in0=aa_t[:, :], scalar1=t_[:, :],
                        scalar2=None, op0=ALU.is_gt), waits=[a_], inc="dvS")
                    g_ = p.emit("scalar", lambda e, b=buf: e.activation(
                        out=sgn_t[:, :], in_=wstr[b][:, :], func=AF.Sign),
                        waits=[m_], inc="evS")
                    tern_mark[("sgn", wi, j)] = g_
                    t_ = p.emit("gpsimd", lambda e, b=buf: e.tensor_tensor(
                        out=tstr[b][:, :], in0=msk_t[:, :], in1=sgn_t[:, :],
                        op=ALU.mult),
                        waits=[g_, tern_mark.get(("st", wi, j - 2)),
                               tern_mark.get(("st", wi - 1, 62 + j))
                               if j < 2 else None],
                        inc="plS")
                    stm = p.dma("sync", dst[j * 2048:(j + 1) * 2048, :].rearrange(
                        "(a p) b -> p a b", p=128),
                        tstr[buf][:, :].rearrange("p (a b) -> p a b", a=16),
                        waits=[t_], inc=f"stw{buf}")
                    tern_mark[("st", wi, j)] = stm
            tern_ready = p.mark("stw0")
            tern_ready2 = p.mark("stw1")
        else:
            tern_ready, tern_ready2 = pre["tern_ready"], pre["tern_ready2"]

        strip_mark = {}
        x1r_mark = [None, None]
        pstr_mark = None
        for tbi in range(4):
            alds = []
            for dm in range(16):
                alds.append(p.dma("sync", A[:, dm * NB:(dm + 1) * NB],
                                  xnbuf[dm * 128:(dm + 1) * 128,
                                        tbi * NB:(tbi + 1) * NB],
                                  waits=[a_free, p1_xn_dma] if dm == 0 else [],
                                  inc="lda"))
            a_rdy = alds[-1]
            prev_s1m = prev_s2m = None
            h2_ready = [None] * 64
            for ot in range(64):
                buf = ot % 2
                ld = p.dma("sync", tstr[buf][:, :].rearrange("p (a b) -> p a b", a=16),
                           tfc[ot * 2048:(ot + 1) * 2048, :].rearrange(
                               "(a p) b -> p a b", p=128),
                           waits=[strip_mark.get(("fcmm", tbi, ot - 2)),
                                  strip_mark.get(("fcmm", tbi - 1,
                                                  62 + (ot % 2)))
                                  if (tbi > 0 and ot < 2) else None,
                                  tern_ready if (tbi == 0 and ot < 2) else None,
                                  tern_ready2 if (tbi == 0 and ot < 2) else None],
                           inc=f"ldstr{buf}")
                for dm in range(16):
                    w1 = []
                    if dm == 0:
                        w1 = [ld, a_rdy, bank_free[buf]]
                    p.emit("tensor",
                           lambda e, b=buf, d_=dm: e.matmul(
                               ps_a[b][:, :],
                               tstr[b][:, d_ * 128:(d_ + 1) * 128],
                               A[:, d_ * NB:(d_ + 1) * NB],
                               start=(d_ == 0), stop=(d_ == 15)),
                           waits=w1, inc="mmS" if dm == 15 else None)
                mmk = p.mark("mmS")
                strip_mark[("fcmm", tbi, ot)] = mmk
                if ot == 63:
                    a_free = mmk
                ge = p.emit("scalar", lambda e, b=buf, o_=ot: e.activation(
                    out=B[:, o_ * NB:(o_ + 1) * NB], in_=ps_a[b][:, :],
                    func=AF.Gelu, scale=gamb[:, :]), waits=[mmk], inc="evS")
                bank_free[buf] = ge
                h2_ready[ot] = ge
                sq_ = p.emit("scalar", lambda e, o_=ot: e.activation(
                    out=sq_t[:, :], in_=B[:, o_ * NB:(o_ + 1) * NB],
                    func=AF.Square), waits=[prev_s2m], inc="evS")
                w_st1 = [ge, ones_ready]
                if ot == 0:
                    w_st1 += [st_done]
                prev_s1m = p.emit("tensor", lambda e, o_=ot: e.matmul(
                    ps_s1[:, :], onescol[:, :], B[:, o_ * NB:(o_ + 1) * NB],
                    start=(o_ == 0), stop=(o_ == 63)), waits=w_st1, inc="mmS")
                w_st2 = [sq_]
                if ot == 0:
                    w_st2 += [st_done]
                prev_s2m = p.emit("tensor", lambda e, o_=ot: e.matmul(
                    ps_s2[:, :], onescol[:, :], sq_t[:, :],
                    start=(o_ == 0), stop=(o_ == 63)), waits=w_st2, inc="mmS")
            stats_done = p.mark("mmS")
            m1 = p.emit("vector", lambda e: e.tensor_scalar(
                out=mu[:, :], in0=ps_s1[:, :], scalar1=float(1.0 / 8192.0),
                scalar2=None, op0=ALU.mult), waits=[stats_done], inc="dvS")
            m2 = p.emit("scalar", lambda e: e.activation(
                out=msq[:, :], in_=mu[:, :], func=AF.Square),
                waits=[m1], inc="evS")
            v1 = p.emit("vector", lambda e: e.tensor_scalar(
                out=varr[:, :], in0=ps_s2[:, :], scalar1=float(1.0 / 8192.0),
                scalar2=None, op0=ALU.mult), inc="dvS")
            st_done = v1
            v2 = p.emit("vector", lambda e: e.tensor_tensor(
                out=varr[:, :], in0=varr[:, :], in1=msq[:, :], op=ALU.subtract),
                waits=[m2], inc="dvS")
            l1 = p.emit("scalar", lambda e: e.activation(
                out=rstd[:, :], in_=varr[:, :], func=AF.Ln, bias=np.float32(1e-6)),
                waits=[v2], inc="evS")
            l2 = p.emit("scalar", lambda e: e.activation(
                out=rstd[:, :], in_=rstd[:, :], func=AF.Exp,
                scale=np.float32(-0.5)), inc="evS")
            a1 = p.emit("vector", lambda e: e.tensor_copy(A_bf[:, :], rstd[:, :]),
                        waits=[l2], inc="dvS")
            b1 = p.emit("vector", lambda e: e.scalar_tensor_tensor(
                out=B_bf[:, :], in0=mu[:, :], scalar=float(-1.0), in1=rstd[:, :],
                op0=ALU.mult, op1=ALU.mult), inc="dvS")
            p.emit("tensor", lambda e: e.matmul(
                ps_b1[:, :], onesrow[:, :], A_bf[:, :], start=True, stop=True),
                waits=[a1, bc_done], inc="mmS")
            cA = p.mark("mmS")
            p.emit("tensor", lambda e: e.matmul(
                ps_b2[:, :], onesrow[:, :], B_bf[:, :], start=True, stop=True),
                waits=[b1], inc="mmS")
            cB = p.mark("mmS")
            eA = p.emit("scalar", lambda e: e.activation(
                out=Af_t[:, :], in_=ps_b1[:, :], func=AF.Copy, scale=1.0),
                waits=[cA], inc="evS")
            eB = p.emit("scalar", lambda e: e.activation(
                out=Bf_t[:, :], in_=ps_b2[:, :], func=AF.Copy, scale=1.0),
                waits=[cB], inc="evS")
            bc_done = eB
            xn2_ready = []
            for ot in range(64):
                eng = "vector" if ot % 2 == 0 else "gpsimd"
                ww = [eA, eB, h2_ready[ot]]
                if ot == 0:
                    ww.append(tmp1_dma)
                if ot == 1:
                    ww.append(p1_tmp2_free)
                p.emit(eng, lambda e, o_=ot: e.tensor_tensor(
                    out=tmp1[:, :] if o_ % 2 == 0 else tmp2[:, :],
                    in0=B[:, o_ * NB:(o_ + 1) * NB], in1=Af_t[:, :],
                    op=ALU.mult), waits=ww)
                q2 = p.emit(eng, lambda e, o_=ot: e.tensor_tensor(
                    out=B[:, o_ * NB:(o_ + 1) * NB],
                    in0=tmp1[:, :] if o_ % 2 == 0 else tmp2[:, :],
                    in1=Bf_t[:, :], op=ALU.add))
                xn2_ready.append(q2)
            prev_s1m = prev_s2m = None
            for do in range(16):
                bank = do % 2
                qlds = []
                for qq in range(4):
                    qlds.append(p.dma(
                        "sync",
                        pstr[:, qq * 2048:(qq + 1) * 2048].rearrange(
                            "p (a b) -> p a b", a=16),
                        tproj[do * 8192 + qq * 2048:
                              do * 8192 + (qq + 1) * 2048, :].rearrange(
                            "(a p) b -> p a b", p=128),
                        waits=[pstr_mark.get(qq) if isinstance(pstr_mark, dict)
                               else pstr_mark,
                               tern_ready if (tbi == 0 and do == 0) else None,
                               tern_ready2 if (tbi == 0 and do == 0)
                               else None],
                        inc=f"ldpr{qq}"))
                if not isinstance(pstr_mark, dict):
                    pstr_mark = {}
                for ht in range(64):
                    w1 = []
                    if ht % 16 == 0:
                        w1 = [qlds[ht // 16]]
                    if ht == 0:
                        w1 += [bank_free[bank]] + ([xn2_ready[-1],
                                                    xn2_ready[-2]]
                                                   if do == 0 else [])
                    p.emit("tensor",
                           lambda e, b=bank, h_=ht: e.matmul(
                               ps_a[b][:, :],
                               pstr[:, h_ * 128:(h_ + 1) * 128],
                               B[:, h_ * NB:(h_ + 1) * NB],
                               start=(h_ == 0), stop=(h_ == 63)),
                           waits=w1, inc="mmS")
                    if ht % 16 == 15:
                        pstr_mark[ht // 16] = p.mark("mmS")
                mmk = p.mark("mmS")
                x1ld = p.dma("sync", xr[bank][:, :],
                             x1buf[do * 128:(do + 1) * 128,
                                   tbi * NB:(tbi + 1) * NB],
                             waits=[x1r_mark[bank],
                                    p1_x1_dma if (tbi == 0 and do < 2) else None],
                             inc=("ldx0" if bank == 0 else "ldx1b"))
                sv = p.emit("vector", lambda e, b=bank, d_=do: e.scalar_tensor_tensor(
                    out=s_sb[d_][:, :], in0=ps_a[b][:, :], scalar=gamb2[:, :],
                    in1=xr[b][:, :], op0=ALU.mult, op1=ALU.add),
                    waits=[mmk, x1ld], inc="dvS")
                x1r_mark[bank] = sv
                bank_free[bank] = sv
                sb_ = p.emit("vector", lambda e, d_=do: e.tensor_copy(
                    sbf_t[:, :], s_sb[d_][:, :]), waits=[prev_s1m], inc="dvS")
                sq_ = p.emit("scalar", lambda e: e.activation(
                    out=sq_t[:, :], in_=sbf_t[:, :], func=AF.Square),
                    waits=[sb_, prev_s2m], inc="evS")
                w_st1 = [sb_]
                if do == 0:
                    w_st1 += [st_done]
                prev_s1m = p.emit("tensor", lambda e, d_=do: e.matmul(
                    ps_s1[:, :], onescol[:, :], sbf_t[:, :],
                    start=(d_ == 0), stop=(d_ == 15)), waits=w_st1, inc="mmS")
                w_st2 = [sq_]
                if do == 0:
                    w_st2 += [st_done]
                prev_s2m = p.emit("tensor", lambda e, d_=do: e.matmul(
                    ps_s2[:, :], onescol[:, :], sq_t[:, :],
                    start=(d_ == 0), stop=(d_ == 15)), waits=w_st2, inc="mmS")
            stats_done = p.mark("mmS")
            m1 = p.emit("vector", lambda e: e.tensor_scalar(
                out=mu[:, :], in0=ps_s1[:, :], scalar1=float(1.0 / 2048.0),
                scalar2=None, op0=ALU.mult), waits=[stats_done], inc="dvS")
            m2 = p.emit("scalar", lambda e: e.activation(
                out=msq[:, :], in_=mu[:, :], func=AF.Square),
                waits=[m1], inc="evS")
            v1 = p.emit("vector", lambda e: e.tensor_scalar(
                out=varr[:, :], in0=ps_s2[:, :], scalar1=float(1.0 / 2048.0),
                scalar2=None, op0=ALU.mult), inc="dvS")
            st_done = v1
            v2 = p.emit("vector", lambda e: e.tensor_tensor(
                out=varr[:, :], in0=varr[:, :], in1=msq[:, :], op=ALU.subtract),
                waits=[m2], inc="dvS")
            l1 = p.emit("scalar", lambda e: e.activation(
                out=rstd[:, :], in_=varr[:, :], func=AF.Ln, bias=np.float32(1e-5)),
                waits=[v2], inc="evS")
            l2 = p.emit("scalar", lambda e: e.activation(
                out=rstd[:, :], in_=rstd[:, :], func=AF.Exp,
                scale=np.float32(-0.5)), inc="evS")
            a1 = p.emit("vector", lambda e: e.tensor_copy(A_bf[:, :], rstd[:, :]),
                        waits=[l2], inc="dvS")
            b1 = p.emit("vector", lambda e: e.scalar_tensor_tensor(
                out=B_bf[:, :], in0=mu[:, :], scalar=float(-1.0), in1=rstd[:, :],
                op0=ALU.mult, op1=ALU.mult), inc="dvS")
            p.emit("tensor", lambda e: e.matmul(
                ps_b1[:, :], onesrow[:, :], A_bf[:, :], start=True, stop=True),
                waits=[a1, bc_done], inc="mmS")
            cA = p.mark("mmS")
            p.emit("tensor", lambda e: e.matmul(
                ps_b2[:, :], onesrow[:, :], B_bf[:, :], start=True, stop=True),
                waits=[b1], inc="mmS")
            cB = p.mark("mmS")
            u2m = None
            for do in range(16):
                u1 = p.emit("vector", lambda e, d_=do: e.tensor_tensor(
                    out=tmp1[:, :], in0=s_sb[d_][:, :], in1=ps_b1[:, :],
                    op=ALU.mult), waits=[cA, tmp1_dma], inc="dvS")
                u2 = p.emit("vector", lambda e: e.tensor_tensor(
                    out=tmp1[:, :], in0=tmp1[:, :], in1=ps_b2[:, :], op=ALU.add),
                    waits=[cB], inc="dvS")
                u2m = u2
                x2d = p.dma("sync", x2o[do * 128:(do + 1) * 128,
                                        tbi * NB:(tbi + 1) * NB],
                            tmp1[:, :], waits=[u2], inc="st0")
                tmp1_dma = x2d
                u4 = p.emit("vector", lambda e: e.tensor_copy(xnb_t[:, :],
                                                              tmp1[:, :]),
                            waits=[xnb_dma], inc="dvS")
                x2bd = p.dma("sync", x2bo[do * 128:(do + 1) * 128,
                                          tbi * NB:(tbi + 1) * NB],
                            xnb_t[:, :], waits=[u4], inc="wx2b")
                xnb_dma = x2bd
            bc_done = u2m
        p.emit("sync", lambda e: e.nop(),
               waits=[p.mark("st0"), p.mark("wx2b")])
        if mega is None:
            p.realize(nc, sems)
        else:
            mega["sems"].update(sems)
    return nc


def _launch3_inputs(inputs, y_full):
    x = np.asarray(inputs["L0_attn_x"])[0]
    xTfull = np.ascontiguousarray(x.T)
    wfcT = np.ascontiguousarray(np.asarray(inputs["L0_ffn_w_fc"]).T.astype(f32))
    wprT = np.ascontiguousarray(np.asarray(inputs["L0_ffn_w_proj"]).T.astype(f32))
    wfc_t = _tile_lhsT(wfcT, 2048, 8192)
    wproj_t = _tile_lhsT(wprT, 8192, 2048)
    Wo = np.ascontiguousarray(np.asarray(inputs["L0_attn_Wo"]).astype(f32))
    bo = _pack_bias(np.asarray(inputs["L0_attn_bo"]))
    maps = []
    for c in range(NCORE):
        sl = slice(c * TC, (c + 1) * TC)
        maps.append({
            "y_sl": np.ascontiguousarray(y_full[:, sl]),
            "Wo": Wo, "bo": bo,
            "xT": np.ascontiguousarray(xTfull[:, sl]),
            "wfc_t": wfc_t, "wproj_t": wproj_t,
        })
    return maps


def run_launch3(inputs, y_full):
    nc = build_launch3()
    maps = _launch3_inputs(inputs, y_full)
    res = run_bass_kernel_spmd(nc, maps, core_ids=list(range(NCORE)))
    x2 = np.concatenate([res.results[c]["x2o"] for c in range(NCORE)], axis=1)
    x2b = np.concatenate([res.results[c]["x2bo"] for c in range(NCORE)], axis=1)
    return x2, x2b


def build_launch4(mega=None):
    if mega is None:
        nc, io, ns = bass.Bass("TRN2", target_bir_lowering=False), None, ""
        p = Plan()
    else:
        nc, p, io, ns = mega["nc"], mega["p"], mega["io"], mega["ns"]
    p.set_ns(ns)
    x2b = _dram(nc, io, ns, "x2b", [D, TC], BF16, "ExternalInput")
    Wq1 = _dram(nc, io, ns, "Wq1", [D, 64], F32, "ExternalInput")
    Wk1 = _dram(nc, io, ns, "Wk1", [D, 64], F32, "ExternalInput")
    Wb1 = _dram(nc, io, ns, "Wb1", [D, 64], F32, "ExternalInput")
    Wd1 = _dram(nc, io, ns, "Wd1", [D, 64], F32, "ExternalInput")
    Wv1 = _dram(nc, io, ns, "Wv1", [D, 128], F32, "ExternalInput")
    bb1 = _dram(nc, io, ns, "bb1", [64, 1], F32, "ExternalInput")
    bd1 = _dram(nc, io, ns, "bd1", [64, 1], F32, "ExternalInput")
    cmask = _dram(nc, io, ns, "cmask", [128, 128], BF16, "ExternalInput")
    identb = _dram(nc, io, ns, "identb", [128, 128], BF16, "ExternalInput")
    olocal = _dram(nc, io, ns, "olocal", [128, TC], F32, "ExternalOutput")
    qhato = _dram(nc, io, ns, "qhato", [64, 128], BF16, "ExternalOutput")
    summo = _dram(nc, io, ns, "summo", [64, 130], F32, "ExternalOutput")

    NB = 512
    with ExitStack() as ctx:
        e = ctx.enter_context
        A = e(nc.sbuf_tensor(ns + "A", [128, 16 * NB], BF16))
        wl1 = e(nc.sbuf_tensor(ns + "wl1", [128, 16 * 64 * 4], BF16))
        wv1 = e(nc.sbuf_tensor(ns + "wv1", [128, 16 * 128], BF16))
        bb_sb = e(nc.sbuf_tensor(ns + "bb_sb", [64, 1], F32))
        bd_sb = e(nc.sbuf_tensor(ns + "bd_sb", [64, 1], F32))
        q_sb = e(nc.sbuf_tensor(ns + "q_sb", [64, TC], F32))
        k_sb = e(nc.sbuf_tensor(ns + "k_sb", [64, TC], F32))
        zb_sb = e(nc.sbuf_tensor(ns + "zb_sb", [64, TC], F32))
        zd_sb = e(nc.sbuf_tensor(ns + "zd_sb", [64, TC], F32))
        v_sb = e(nc.sbuf_tensor(ns + "v_sb", [128, TC], BF16))
        lnb_sb = e(nc.sbuf_tensor(ns + "lnb_sb", [64, TC], F32))
        kd_sb = e(nc.sbuf_tensor(ns + "kd_sb", [64, TC], F32))
        Lf_sb = e(nc.sbuf_tensor(ns + "Lf_sb", [64, TC], F32))
        zeros64 = e(nc.sbuf_tensor(ns + "zeros64", [64, TC], F32))
        ta64 = e(nc.sbuf_tensor(ns + "ta64", [64, TC], F32))
        tb64 = e(nc.sbuf_tensor(ns + "tb64", [64, TC], F32))
        Lrel = e(nc.sbuf_tensor(ns + "Lrel", [64, 128], F32))
        negc = e(nc.sbuf_tensor(ns + "negc", [64, 1], F32))
        posc = e(nc.sbuf_tensor(ns + "posc", [64, 1], F32))
        Eend = e(nc.sbuf_tensor(ns + "Eend", [64, 1], F32))
        E1 = e(nc.sbuf_tensor(ns + "E1", [64, 128], F32))
        E2 = e(nc.sbuf_tensor(ns + "E2", [64, 128], F32))
        E3 = e(nc.sbuf_tensor(ns + "E3", [64, 128], F32))
        E4 = e(nc.sbuf_tensor(ns + "E4", [64, 128], F32))
        qt_b = e(nc.sbuf_tensor(ns + "qt_b", [64, 128], BF16))
        kt_b = e(nc.sbuf_tensor(ns + "kt_b", [64, 128], BF16))
        qh_b = e(nc.sbuf_tensor(ns + "qh_b", [64, 128], BF16))
        kh_b = e(nc.sbuf_tensor(ns + "kh_b", [64, 128], BF16))
        khtok = e(nc.sbuf_tensor(ns + "khtok", [128, 64], BF16))
        vtok = e(nc.sbuf_tensor(ns + "vtok", [128, 128], BF16))
        ssb1 = e(nc.sbuf_tensor(ns + "ssb1", [128, 128], BF16))
        S_f = e(nc.sbuf_tensor(ns + "S_f", [64, 128], F32))
        S_b = e(nc.sbuf_tensor(ns + "S_b", [64, 128], BF16))
        Ldc = e(nc.sbuf_tensor(ns + "Ldc", [64, 1], F32))
        o_sb = e(nc.sbuf_tensor(ns + "o_sb", [128, TC], F32))
        qhat_sb = e(nc.sbuf_tensor(ns + "qhat_sb", [64, 128], BF16))
        summ_sb = e(nc.sbuf_tensor(ns + "summ_sb", [64, 130], F32))
        cm_sb = e(nc.sbuf_tensor(ns + "cm_sb", [128, 128], BF16))
        id_sb = e(nc.sbuf_tensor(ns + "id_sb", [128, 128], BF16))
        ps_a = [e(nc.psum_tensor(ns + f"ps_a{i}", [128, NB], F32)) for i in range(2)]
        ps_sc = e(nc.psum_tensor(ns + "ps_sc", [128, 128], F32))
        ps_o = e(nc.psum_tensor(ns + "ps_o", [128, 128], F32))
        ps_sd = e(nc.psum_tensor(ns + "ps_sd", [64, 128], F32))
        ps_tr = e(nc.psum_tensor(ns + "ps_tr", [128, 256], BF16))
        sems = _fresh(mega["sem_ctx"] if mega else ctx, nc,
                      [ns + s_ for s_ in ("lda", "ldw", "ldt", "mmS", "dvS",
                                          "evS", "plS", "st0")])

        p.emit("gpsimd", lambda e: e.memset(zeros64[:, :], 0.0))
        p.emit("gpsimd", lambda e: e.memset(S_f[:, :], 0.0))
        p.emit("gpsimd", lambda e: e.memset(S_b[:, :], 0.0))
        cst = p.emit("gpsimd", lambda e: e.memset(Ldc[:, :], 0.0))

        for wi, wt in enumerate((Wq1, Wk1, Wb1, Wd1)):
            p.dma("gpsimd", wl1[:, wi * 1024:(wi + 1) * 1024].rearrange(
                "p (a b) -> p a b", a=16),
                wt.rearrange("(a p) b -> p a b", p=128), inc="ldw")
        p.dma("gpsimd", wv1[:, :].rearrange("p (a b) -> p a b", a=16),
              Wv1.rearrange("(a p) b -> p a b", p=128), inc="ldw")
        w_rdy = p.mark("ldw")
        p.dma("sync", bb_sb[:, :], bb1[:, :], inc="ldt")
        p.dma("sync", bd_sb[:, :], bd1[:, :], inc="ldt")
        p.dma("sync", cm_sb[:, :], cmask[:, :], inc="ldt")
        p.dma("sync", id_sb[:, :], identb[:, :], inc="ldt")
        t_rdy = p.mark("ldt")

        a_free = None
        bank_free = [None, None]
        for tbi in range(4):
            alds = []
            for dm in range(16):
                alds.append(p.dma("sync", A[:, dm * NB:(dm + 1) * NB],
                                  x2b[dm * 128:(dm + 1) * 128,
                                      tbi * NB:(tbi + 1) * NB],
                                  waits=[a_free] if dm == 0 else [], inc="lda"))
            a_rdy = alds[-1]
            for zi in range(5):
                bank = zi % 2
                for dm in range(16):
                    w1 = []
                    if dm == 0:
                        w1 = [a_rdy, w_rdy, bank_free[bank]]
                    if zi < 4:
                        p.emit("tensor", lambda e, b=bank, d_=dm, z_=zi: e.matmul(
                            ps_a[b][0:64, :],
                            wl1[:, (z_ * 16 + d_) * 64:(z_ * 16 + d_) * 64 + 64],
                            A[:, d_ * NB:(d_ + 1) * NB],
                            start=(d_ == 0), stop=(d_ == 15)),
                            waits=w1, inc="mmS" if dm == 15 else None)
                    else:
                        p.emit("tensor", lambda e, b=bank, d_=dm: e.matmul(
                            ps_a[b][:, :],
                            wv1[:, d_ * 128:(d_ + 1) * 128],
                            A[:, d_ * NB:(d_ + 1) * NB],
                            start=(d_ == 0), stop=(d_ == 15)),
                            waits=w1, inc="mmS" if dm == 15 else None)
                mmk = p.mark("mmS")
                if zi == 4:
                    a_free = mmk
                if zi == 0:
                    ev = p.emit("vector", lambda e, b=bank, t_=tbi: e.tensor_copy(
                        q_sb[:, t_ * NB:(t_ + 1) * NB], ps_a[b][0:64, :]),
                        waits=[mmk], inc="dvS")
                elif zi == 1:
                    ev = p.emit("vector", lambda e, b=bank, t_=tbi: e.tensor_copy(
                        k_sb[:, t_ * NB:(t_ + 1) * NB], ps_a[b][0:64, :]),
                        waits=[mmk], inc="dvS")
                elif zi == 2:
                    ev = p.emit("vector", lambda e, b=bank, t_=tbi: e.tensor_scalar(
                        out=zb_sb[:, t_ * NB:(t_ + 1) * NB], in0=ps_a[b][0:64, :],
                        scalar1=bb_sb[:, :], scalar2=None, op0=ALU.add),
                        waits=[mmk, t_rdy], inc="dvS")
                elif zi == 3:
                    ev = p.emit("vector", lambda e, b=bank, t_=tbi: e.tensor_scalar(
                        out=zd_sb[:, t_ * NB:(t_ + 1) * NB], in0=ps_a[b][0:64, :],
                        scalar1=bd_sb[:, :], scalar2=None, op0=ALU.add),
                        waits=[mmk, t_rdy], inc="dvS")
                else:
                    ev = p.emit("vector", lambda e, b=bank, t_=tbi: e.tensor_copy(
                        v_sb[:, t_ * NB:(t_ + 1) * NB], ps_a[b][:, :]),
                        waits=[mmk], inc="dvS")
                bank_free[bank] = ev
        proj_done_dv = p.mark("dvS")

        r_ = p.emit("scalar", lambda e: e.activation(
            out=ta64[:, :], in_=zd_sb[:, :], func=AF.Relu),
            waits=[proj_done_dv], inc="evS")
        x_ = p.emit("scalar", lambda e: e.activation(
            out=tb64[:, :], in_=zd_sb[:, :], func=AF.Abs), inc="evS")
        x_ = p.emit("scalar", lambda e: e.activation(
            out=tb64[:, :], in_=tb64[:, :], func=AF.Exp, scale=np.float32(-1.0)),
            inc="evS")
        x_ = p.emit("scalar", lambda e: e.activation(
            out=tb64[:, :], in_=tb64[:, :], func=AF.Ln, bias=np.float32(1.0)),
            inc="evS")
        d_ = p.emit("vector", lambda e: e.tensor_tensor(
            out=ta64[:, :], in0=ta64[:, :], in1=tb64[:, :], op=ALU.add),
            waits=[x_], inc="dvS")
        kd_ = p.emit("vector", lambda e: e.tensor_tensor(
            out=kd_sb[:, :], in0=k_sb[:, :], in1=ta64[:, :], op=ALU.mult),
            inc="dvS")
        m_ = p.emit("vector", lambda e: e.tensor_scalar(
            out=ta64[:, :], in0=zb_sb[:, :], scalar1=float(0.0), scalar2=None,
            op0=ALU.min), inc="dvS")
        y_ = p.emit("scalar", lambda e: e.activation(
            out=tb64[:, :], in_=zb_sb[:, :], func=AF.Abs), waits=[d_], inc="evS")
        y_ = p.emit("scalar", lambda e: e.activation(
            out=tb64[:, :], in_=tb64[:, :], func=AF.Exp, scale=np.float32(-1.0)),
            inc="evS")
        y_ = p.emit("scalar", lambda e: e.activation(
            out=tb64[:, :], in_=tb64[:, :], func=AF.Ln, bias=np.float32(1.0)),
            inc="evS")
        lb_ = p.emit("vector", lambda e: e.tensor_tensor(
            out=lnb_sb[:, :], in0=ta64[:, :], in1=tb64[:, :], op=ALU.subtract),
            waits=[y_], inc="dvS")
        sc_ = p.emit("vector", lambda e: e.tensor_tensor_scan(
            Lf_sb[:, :], lnb_sb[:, :], zeros64[:, :], 0.0, ALU.add, ALU.add),
            inc="dvS")

        CH = 64
        prev_ssb_mm = None
        prev_tr_ev = None
        prev_o_ev = None
        prev_sd_rd = None
        prev_sc_ev = None
        prev_Sb = None
        for m in range(TC // CH):
            c0 = m * CH
            if m == 0:
                lr = p.emit("vector", lambda e: e.tensor_copy(
                    Lrel[:, 0:CH], Lf_sb[:, 0:CH]), waits=[sc_], inc="dvS")
            else:
                lr = p.emit("vector", lambda e, c_=c0: e.tensor_scalar(
                    out=Lrel[:, 0:CH], in0=Lf_sb[:, c_:c_ + CH],
                    scalar1=Lf_sb[:, c_ - 1:c_], scalar2=None,
                    op0=ALU.subtract), inc="dvS")
            n1 = p.emit("vector", lambda e: e.tensor_scalar(
                out=negc[:, :], in0=Lrel[:, CH - 1:CH], scalar1=float(-0.5),
                scalar2=None, op0=ALU.mult), inc="dvS")
            n2 = p.emit("vector", lambda e: e.tensor_scalar(
                out=posc[:, :], in0=Lrel[:, CH - 1:CH], scalar1=float(0.5),
                scalar2=None, op0=ALU.mult), inc="dvS")
            ee = p.emit("scalar", lambda e: e.activation(
                out=Eend[:, :], in_=Lrel[:, CH - 1:CH], func=AF.Exp),
                waits=[n2], inc="evS")
            e1 = p.emit("scalar", lambda e: e.activation(
                out=E1[:, 0:CH], in_=Lrel[:, 0:CH], func=AF.Exp, bias=negc[:, :]),
                inc="evS")
            e2 = p.emit("scalar", lambda e: e.activation(
                out=E2[:, 0:CH], in_=Lrel[:, 0:CH], func=AF.Exp,
                scale=np.float32(-1.0), bias=posc[:, :]), inc="evS")
            e3 = p.emit("scalar", lambda e: e.activation(
                out=E3[:, 0:CH], in_=Lrel[:, 0:CH], func=AF.Exp), inc="evS")
            e4 = p.emit("scalar", lambda e: e.activation(
                out=E4[:, 0:CH], in_=Lrel[:, 0:CH], func=AF.Exp,
                scale=np.float32(-1.0), bias=Lrel[:, CH - 1:CH]), inc="evS")
            qt = p.emit("vector", lambda e, c_=c0: e.tensor_tensor(
                out=qt_b[:, 0:CH], in0=q_sb[:, c_:c_ + CH], in1=E1[:, 0:CH],
                op=ALU.mult), waits=[e1, prev_ssb_mm], inc="dvS")
            kt = p.emit("vector", lambda e, c_=c0: e.tensor_tensor(
                out=kt_b[:, 0:CH], in0=kd_sb[:, c_:c_ + CH], in1=E2[:, 0:CH],
                op=ALU.mult), waits=[e2], inc="dvS")
            qh = p.emit("vector", lambda e, c_=c0: e.tensor_tensor(
                out=qh_b[:, 0:CH], in0=q_sb[:, c_:c_ + CH], in1=E3[:, 0:CH],
                op=ALU.mult), waits=[e3], inc="dvS")
            kh = p.emit("vector", lambda e, c_=c0: e.tensor_tensor(
                out=kh_b[:, 0:CH], in0=kd_sb[:, c_:c_ + CH], in1=E4[:, 0:CH],
                op=ALU.mult), waits=[e4], inc="dvS")
            if m in (0, 1):
                p.emit("vector", lambda e, m_=m: e.tensor_copy(
                    qhat_sb[:, m_ * CH:(m_ + 1) * CH], qh_b[:, 0:CH]),
                    inc="dvS")
            sc_mm = p.emit("tensor", lambda e: e.matmul(
                ps_sc[0:CH, 0:CH], kt_b[:, 0:CH], qt_b[:, 0:CH],
                start=True, stop=True),
                waits=[kt, qt, prev_sc_ev], inc="mmS")
            sse = p.emit("vector", lambda e: e.tensor_tensor(
                out=ssb1[0:CH, 0:CH], in0=ps_sc[0:CH, 0:CH],
                in1=cm_sb[0:CH, 0:CH], op=ALU.mult),
                waits=[sc_mm, t_rdy, prev_ssb_mm], inc="dvS")
            prev_sc_ev = sse
            tr1 = p.emit("tensor", lambda e: e.transpose(
                ps_tr[0:CH, 0:64], kh_b[:, 0:CH], id_sb[0:64, 0:64]),
                waits=[kh, prev_tr_ev], inc="mmS")
            tr2 = p.emit("tensor", lambda e, c_=c0: e.transpose(
                ps_tr[0:CH, 64:192], v_sb[:, c_:c_ + CH], id_sb[:, :]),
                waits=[proj_done_dv], inc="mmS")
            te1 = p.emit("vector", lambda e: e.tensor_copy(khtok[0:CH, :],
                                                           ps_tr[0:CH, 0:64]),
                         waits=[tr1], inc="dvS")
            te2 = p.emit("vector", lambda e: e.tensor_copy(vtok[0:CH, :],
                                                           ps_tr[0:CH, 64:192]),
                         waits=[tr2], inc="dvS")
            prev_tr_ev = te2
            om1 = p.emit("tensor", lambda e: e.matmul(
                ps_o[:, 0:CH], vtok[0:CH, :], ssb1[0:CH, 0:CH],
                start=True, stop=False),
                waits=[te2, sse, prev_o_ev], inc="mmS")
            om2 = p.emit("tensor", lambda e: e.matmul(
                ps_o[:, 0:CH], S_b[:, :], qh_b[:, 0:CH],
                start=False, stop=True),
                waits=[qh, prev_Sb], inc="mmS")
            prev_ssb_mm = om2
            oe = p.emit("scalar", lambda e, c_=c0: e.activation(
                out=o_sb[:, c_:c_ + CH], in_=ps_o[:, 0:CH], func=AF.Copy,
                scale=1.0), waits=[om2], inc="evS")
            prev_o_ev = oe
            sd = p.emit("tensor", lambda e: e.matmul(
                ps_sd[:, :], khtok[0:CH, :], vtok[0:CH, :],
                start=True, stop=True),
                waits=[te1, prev_sd_rd], inc="mmS")
            su = p.emit("vector", lambda e: e.scalar_tensor_tensor(
                out=S_f[:, :], in0=S_f[:, :], scalar=Eend[:, :],
                in1=ps_sd[:, :], op0=ALU.mult, op1=ALU.add),
                waits=[sd, ee, om2], inc="dvS")
            prev_sd_rd = su
            sb2 = p.emit("vector", lambda e: e.tensor_copy(S_b[:, :], S_f[:, :]),
                         inc="dvS")
            prev_Sb = sb2
            lda_ = p.emit("vector", lambda e: e.tensor_tensor(
                out=Ldc[:, :], in0=Ldc[:, :], in1=Lrel[:, CH - 1:CH],
                op=ALU.add), inc="dvS")
            if m == 0:
                p.emit("vector", lambda e: e.tensor_copy(
                    summ_sb[:, 129:130], Lrel[:, CH - 1:CH]), inc="dvS")

        fin1 = p.emit("vector", lambda e: e.tensor_copy(summ_sb[:, 0:128],
                                                        S_f[:, :]), inc="dvS")
        fin2 = p.emit("vector", lambda e: e.tensor_copy(summ_sb[:, 128:129],
                                                        Ldc[:, :]), inc="dvS")
        p.dma("sync", olocal[:, :], o_sb[:, :],
              waits=[p.mark("evS")], inc="st0")
        p.dma("sync", qhato[:, :], qhat_sb[:, :],
              waits=[p.mark("dvS")], inc="st0")
        p.dma("sync", summo[:, :], summ_sb[:, :], waits=[fin2], inc="st0")
        p.emit("sync", lambda e: e.nop(), waits=[p.mark("st0")])
        if mega is None:
            p.realize(nc, sems)
        else:
            mega["sems"].update(sems)
    return nc


def _launch4_inputs(inputs, x2b_full):
    maps = []
    cm = np.ascontiguousarray(_causal128())
    idn = np.ascontiguousarray(np.eye(128, dtype=f32).astype(BF))
    for c in range(NCORE):
        sl = slice(c * TC, (c + 1) * TC)
        m = {"x2b": np.ascontiguousarray(x2b_full[:, sl]),
             "cmask": cm, "identb": idn}
        for nm in ("Wq", "Wk", "Wb", "Wd", "Wv"):
            m[nm + "1"] = np.ascontiguousarray(
                np.asarray(inputs[f"L1_attn_{nm}"]).astype(f32))
        m["bb1"] = np.ascontiguousarray(
            np.asarray(inputs["L1_attn_bb"]).astype(f32).reshape(64, 1))
        m["bd1"] = np.ascontiguousarray(
            np.asarray(inputs["L1_attn_bd"]).astype(f32).reshape(64, 1))
        maps.append(m)
    return maps


def run_launch4(inputs, x2b_full):
    nc = build_launch4()
    maps = _launch4_inputs(inputs, x2b_full)
    res = run_bass_kernel_spmd(nc, maps, core_ids=list(range(NCORE)))
    ol = np.concatenate([res.results[c]["olocal"] for c in range(NCORE)], axis=1)
    qh = np.stack([res.results[c]["qhato"] for c in range(NCORE)])
    sm = np.stack([res.results[c]["summo"] for c in range(NCORE)])
    return ol, qh, sm


def build_launch5(mega=None):
    if mega is None:
        nc, io, ns, pre = bass.Bass("TRN2", target_bir_lowering=False), None, "", None
        p = Plan()
    else:
        nc, p, io, ns = mega["nc"], mega["p"], mega["io"], mega["ns"]
        pre = mega.get("pre")
    p.set_ns(ns)
    olocal = _dram(nc, io, ns, "olocal", [128, TC], F32, "ExternalInput")
    qhat = _dram(nc, io, ns, "qhat", [64, 128], BF16, "ExternalInput")
    summ_all = _dram(nc, io, ns, "summ_all", [512, 130], F32, "ExternalInput")
    own_sum = _dram(nc, io, ns, "own_sum", [64, 130], F32, "ExternalInput")
    ltm = _dram(nc, io, ns, "ltm", [64, 8], F32, "ExternalInput")
    x2 = _dram(nc, io, ns, "x2", [D, TC], F32, "ExternalInput")
    Wo1 = _dram(nc, io, ns, "Wo1", [128, D], F32, "ExternalInput")
    w1fc_t = _dram(nc, io, ns, "w1fc_t", [40 * 16 * 128, 128], F32, "ExternalInput")
    w1proj_t = _dram(nc, io, ns, "w1proj_t", [16 * 40 * 128, 128], F32, "ExternalInput")
    outx = _dram(nc, io, ns, "outx", [D, TC], F32, "ExternalOutput")
    x3buf = _dram(nc, io, ns, "x3buf", [D, TC], F32, "Internal")
    t1fc = _dram(nc, io, ns, "t1fc", [40 * 16 * 128, 128], BF16, "Internal")
    t1proj = _dram(nc, io, ns, "t1proj", [16 * 40 * 128, 128], BF16, "Internal")

    NB = 512
    CH = 64
    with ExitStack() as ctx:
        e = ctx.enter_context
        A = e(nc.sbuf_tensor(ns + "A", [128, 16 * NB], BF16))
        Bh = e(nc.sbuf_tensor(ns + "Bh", [128, 40 * NB], BF16))
        s_sb = [e(nc.sbuf_tensor(ns + f"s_sb{i}", [128, NB], F32)) for i in range(16)]
        xr = [e(nc.sbuf_tensor(ns + f"xr{i}", [128, NB], F32)) for i in range(2)]
        if pre is None:
            wstr = [e(nc.sbuf_tensor(ns + f"wstr{i}", [128, 2048], F32)) for i in range(2)]
            aa_t = e(nc.sbuf_tensor(ns + "aa_t", [128, 2048], F32))
            msk_t = e(nc.sbuf_tensor(ns + "msk_t", [128, 2048], BF16))
            sgn_t = e(nc.sbuf_tensor(ns + "sgn_t", [128, 2048], BF16))
            tstr = [e(nc.sbuf_tensor(ns + f"tstr{i}", [128, 2048], BF16)) for i in range(2)]
        else:
            tstr = pre["tstr"]
        pstr = e(nc.sbuf_tensor(ns + "pstr", [128, 5120], BF16))
        rt_t = e(nc.sbuf_tensor(ns + "rt_t", [128, NB], F32))
        sbf_t = e(nc.sbuf_tensor(ns + "sbf_t", [128, NB], BF16))
        xnb_t = e(nc.sbuf_tensor(ns + "xnb_t", [128, NB], BF16))
        sq_t = e(nc.sbuf_tensor(ns + "sq_t", [128, NB], BF16))
        tmp1 = e(nc.sbuf_tensor(ns + "tmp1", [128, NB], F32))
        onescol = e(nc.sbuf_tensor(ns + "onescol", [128, 1], BF16))
        onesrow = e(nc.sbuf_tensor(ns + "onesrow", [1, 128], BF16))
        if pre is None:
            onescol_f = e(nc.sbuf_tensor(ns + "onescol_f", [128, 1], F32))
            onesrow_f = e(nc.sbuf_tensor(ns + "onesrow_f", [1, 128], F32))
            wacc = e(nc.sbuf_tensor(ns + "wacc", [128, 1], F32))
            wacc2 = e(nc.sbuf_tensor(ns + "wacc2", [128, 1], F32))
            red1 = e(nc.sbuf_tensor(ns + "red1", [128, 1], F32))
            sc1 = e(nc.sbuf_tensor(ns + "sc1", [1, 4], F32))
            thrb = e(nc.sbuf_tensor(ns + "thrb", [128, 1], F32))
            gamb = e(nc.sbuf_tensor(ns + "gamb", [128, 1], F32))
            thrb2 = e(nc.sbuf_tensor(ns + "thrb2", [128, 1], F32))
            gamb2 = e(nc.sbuf_tensor(ns + "gamb2", [128, 1], F32))
        else:
            gamb, thrb = pre["gamb"], pre["thrb"]
            gamb2, thrb2 = pre["gamb2"], pre["thrb2"]
        mu = e(nc.sbuf_tensor(ns + "mu", [1, NB], F32))
        msq = e(nc.sbuf_tensor(ns + "msq", [1, NB], F32))
        varr = e(nc.sbuf_tensor(ns + "varr", [1, NB], F32))
        rstd = e(nc.sbuf_tensor(ns + "rstd", [1, NB], F32))
        A_bf = e(nc.sbuf_tensor(ns + "A_bf", [1, NB], BF16))
        B_bf = e(nc.sbuf_tensor(ns + "B_bf", [1, NB], BF16))
        o_sb = e(nc.sbuf_tensor(ns + "o_sb", [128, TC], F32))
        o_b = e(nc.sbuf_tensor(ns + "o_b", [128, TC], BF16))
        wo1_sb = e(nc.sbuf_tensor(ns + "wo1_sb", [128, D], BF16))
        gs = e(nc.sbuf_tensor(ns + "gs", [64, 8 * 130], F32))
        own_sb = e(nc.sbuf_tensor(ns + "own_sb", [64, 130], F32))
        ltm_sb = e(nc.sbuf_tensor(ns + "ltm_sb", [64, 8], F32))
        qh_sb = e(nc.sbuf_tensor(ns + "qh_sb", [64, 128], BF16))
        qh1s = e(nc.sbuf_tensor(ns + "qh1s", [64, CH], BF16))
        S_in = e(nc.sbuf_tensor(ns + "S_in", [64, 128], F32))
        S_in_b = e(nc.sbuf_tensor(ns + "S_in_b", [64, 128], BF16))
        Tacc = e(nc.sbuf_tensor(ns + "Tacc", [64, 1], F32))
        eT = e(nc.sbuf_tensor(ns + "eT", [64, 1], F32))
        fvec = e(nc.sbuf_tensor(ns + "fvec", [64, 1], F32))
        ldm = e(nc.sbuf_tensor(ns + "ldm", [64, 1], F32))
        eP = e(nc.sbuf_tensor(ns + "eP", [64, 1], F32))
        ps_a = [e(nc.psum_tensor(ns + f"ps_a{i}", [128, NB], F32)) for i in range(2)]
        ps_s1 = e(nc.psum_tensor(ns + "ps_s1", [1, NB], F32))
        ps_s2 = e(nc.psum_tensor(ns + "ps_s2", [1, NB], F32))
        ps_b1 = e(nc.psum_tensor(ns + "ps_b1", [128, NB], F32))
        ps_b2 = e(nc.psum_tensor(ns + "ps_b2", [128, NB], F32))
        ps_c = e(nc.psum_tensor(ns + "ps_c", [128, CH], F32))
        if pre is None:
            ps_sc = e(nc.psum_tensor(ns + "ps_sc", [1, 8], F32))
        sems = _fresh(mega["sem_ctx"] if mega else ctx, nc,
                      [ns + s_ for s_ in ("lda", "ldw", "ldx0", "ldx1b", "ldt",
                                          "ldstr0", "ldstr1", "ldpr0", "ldpr1",
                                          "ldpr2", "ldpr3", "wx3",
                                          "mmS", "dvS", "evS", "plS", "st0",
                                          "stw0", "stw1")])

        p.emit("gpsimd", lambda e: e.memset(onescol[:, :], 1.0))
        p.emit("gpsimd", lambda e: e.memset(onesrow[:, :], 1.0))
        if pre is None:
            p.emit("gpsimd", lambda e: e.memset(onescol_f[:, :], 1.0))
            p.emit("gpsimd", lambda e: e.memset(onesrow_f[:, :], 1.0))
        p.emit("gpsimd", lambda e: e.memset(Tacc[:, :], 0.0))
        ones_ready = p.emit("gpsimd", lambda e: e.memset(S_in[:, :], 0.0))

        old = p.dma("sync", o_sb[:, :], olocal[:, :], inc="ldt")
        p.dma("sync", qh_sb[:, :], qhat[:, :], inc="ldt")
        for b in range(8):
            p.dma("sync", gs[:, b * 130:(b + 1) * 130],
                  summ_all[b * 64:(b + 1) * 64, :], inc="ldt")
        p.dma("sync", own_sb[:, :], own_sum[:, :], inc="ldt")
        p.dma("sync", ltm_sb[:, :], ltm[:, :], inc="ldt")
        wo1ld = p.dma("gpsimd", wo1_sb[:, :], Wo1[:, :], inc="ldw")
        t_rdy = p.mark("ldt")

        if pre is None:
            macc = {}
            for wi, (wt, acc) in enumerate(((w1fc_t, wacc), (w1proj_t, wacc2))):
                for i in range(40):
                    srcv = wt[i * 2048:(i + 1) * 2048, :].rearrange(
                        "(a p) b -> p a b", p=128)
                    ld = p.dma("sync",
                               wstr[wi][:, :].rearrange("p (a b) -> p a b", a=16),
                               srcv, waits=[macc.get((wi, i - 1))],
                               inc=f"ldstr{wi}")
                    p.emit("vector", lambda e, w_=wi: e.tensor_reduce(
                        out=red1[:, :], in_=wstr[w_][:, :], axis=AX.X,
                        op=ALU.add, apply_absolute_value=True), waits=[ld])
                    ac = p.emit("vector", lambda e, a_=acc, first=(i == 0): (
                        e.tensor_copy(a_[:, :], red1[:, :]) if first else
                        e.tensor_tensor(out=a_[:, :], in0=a_[:, :], in1=red1[:, :],
                                        op=ALU.add)))
                    macc[(wi, i)] = ac

            def scalar_reduce(acc, gbuf, tbuf, denom, col, ps_big):
                t = p.emit("tensor", lambda e, a_=acc, c_=col: e.matmul(
                    ps_sc[0:1, c_:c_ + 1], onescol_f[:, :], a_[:, :],
                    start=True, stop=True),
                    waits=[macc[(0, 39)], macc[(1, 39)], ones_ready], inc="mmS")
                g1 = p.emit("vector", lambda e, c_=col: e.tensor_scalar(
                    out=sc1[0:1, c_:c_ + 1], in0=ps_sc[0:1, c_:c_ + 1],
                    scalar1=float(1.0 / denom), scalar2=float(1e-5),
                    op0=ALU.mult, op1=ALU.max), waits=[t])
                t2_ = p.emit("vector", lambda e, c_=col: e.tensor_scalar(
                    out=sc1[0:1, c_ + 1:c_ + 2], in0=sc1[0:1, c_:c_ + 1],
                    scalar1=float(0.7), scalar2=None, op0=ALU.mult))
                bg = p.emit("tensor", lambda e, c_=col, ps=ps_big: e.matmul(
                    ps[:, 0:2], onesrow_f[:, :], sc1[0:1, c_:c_ + 2],
                    start=True, stop=True), waits=[t2_], inc="mmS")
                c1 = p.emit("scalar", lambda e, g_=gbuf, ps=ps_big: e.activation(
                    out=g_[:, :], in_=ps[:, 0:1], func=AF.Copy, scale=1.0),
                    waits=[bg], inc="evS")
                c2 = p.emit("scalar", lambda e, t_=tbuf, ps=ps_big: e.activation(
                    out=t_[:, :], in_=ps[:, 1:2], func=AF.Copy, scale=1.0),
                    inc="evS")
                return (g1, c2)
            thr_ready = scalar_reduce(wacc, gamb, thrb, 5120.0 * 2048.0, 0, ps_b1)
            thr2_ready = scalar_reduce(wacc2, gamb2, thrb2, 5120.0 * 2048.0, 2, ps_b2)

            tern_mark = {}
            for wi, (wt, dst, tb_buf, nch) in enumerate(
                    ((w1fc_t, t1fc, thrb, 40), (w1proj_t, t1proj, thrb2, 40))):
                thr_m = thr_ready[1] if wi == 0 else thr2_ready[1]
                for j in range(nch):
                    buf = j % 2
                    srcv = wt[j * 2048:(j + 1) * 2048, :].rearrange(
                        "(a p) b -> p a b", p=128)
                    ld = p.dma("sync",
                               wstr[buf][:, :].rearrange("p (a b) -> p a b", a=16),
                               srcv,
                               waits=[tern_mark.get(("sgn", wi, j - 2)),
                                      macc[(wi, 39)] if j < 2 else None,
                                      macc[(1, 39)] if (wi == 0 and j < 2) else None],
                               inc=f"ldstr{buf}")
                    a_ = p.emit("scalar", lambda e, b=buf: e.activation(
                        out=aa_t[:, :], in_=wstr[b][:, :], func=AF.Abs),
                        waits=[ld, thr_m], inc="evS")
                    m_ = p.emit("vector", lambda e, t_=tb_buf: e.tensor_scalar(
                        out=msk_t[:, :], in0=aa_t[:, :], scalar1=t_[:, :],
                        scalar2=None, op0=ALU.is_gt), waits=[a_], inc="dvS")
                    g_ = p.emit("scalar", lambda e, b=buf: e.activation(
                        out=sgn_t[:, :], in_=wstr[b][:, :], func=AF.Sign),
                        waits=[m_], inc="evS")
                    tern_mark[("sgn", wi, j)] = g_
                    t_ = p.emit("gpsimd", lambda e, b=buf: e.tensor_tensor(
                        out=tstr[b][:, :], in0=msk_t[:, :], in1=sgn_t[:, :],
                        op=ALU.mult),
                        waits=[g_, tern_mark.get(("st", wi, j - 2)),
                               tern_mark.get(("st", wi - 1, 38 + j))
                               if j < 2 else None],
                        inc="plS")
                    stm = p.dma("sync", dst[j * 2048:(j + 1) * 2048, :].rearrange(
                        "(a p) b -> p a b", p=128),
                        tstr[buf][:, :].rearrange("p (a b) -> p a b", a=16),
                        waits=[t_], inc=f"stw{buf}")
                    tern_mark[("st", wi, j)] = stm
            tern_ready = p.mark("stw0")
            tern_ready2 = p.mark("stw1")
        else:
            thr_ready, thr2_ready = pre["thr_ready"], pre["thr2_ready"]
            tern_ready, tern_ready2 = pre["tern_ready"], pre["tern_ready2"]

        sin_m = None
        for b in range(7, -1, -1):
            x1_ = p.emit("scalar", lambda e: e.activation(
                out=eT[:, :], in_=Tacc[:, :], func=AF.Exp),
                waits=[t_rdy, ones_ready, sin_m], inc="evS")
            x2_ = p.emit("vector", lambda e, b_=b: e.tensor_scalar(
                out=fvec[:, :], in0=eT[:, :], scalar1=ltm_sb[:, b_:b_ + 1],
                scalar2=None, op0=ALU.mult), waits=[x1_], inc="dvS")
            x3_ = p.emit("vector", lambda e, b_=b: e.scalar_tensor_tensor(
                out=S_in[:, :], in0=gs[:, b_ * 130:b_ * 130 + 128],
                scalar=fvec[:, :], in1=S_in[:, :], op0=ALU.mult, op1=ALU.add),
                inc="dvS")
            x4_ = p.emit("vector", lambda e, b_=b: e.tensor_scalar(
                out=ldm[:, :], in0=gs[:, b_ * 130 + 128:b_ * 130 + 129],
                scalar1=ltm_sb[:, b_:b_ + 1], scalar2=None, op0=ALU.mult),
                inc="dvS")
            sin_m = p.emit("vector", lambda e: e.tensor_tensor(
                out=Tacc[:, :], in0=Tacc[:, :], in1=ldm[:, :], op=ALU.add),
                inc="dvS")
        sb_ = p.emit("vector", lambda e: e.tensor_copy(S_in_b[:, :], S_in[:, :]),
                     inc="dvS")
        ep_ = p.emit("scalar", lambda e: e.activation(
            out=eP[:, :], in_=own_sb[:, 129:130], func=AF.Exp),
            waits=[t_rdy], inc="evS")
        q1s = p.emit("vector", lambda e: e.tensor_scalar(
            out=qh1s[:, :], in0=qh_sb[:, CH:2 * CH], scalar1=eP[:, :],
            scalar2=None, op0=ALU.mult), waits=[ep_], inc="dvS")
        cm1 = p.emit("tensor", lambda e: e.matmul(
            ps_c[:, 0:CH], S_in_b[:, :], qh_sb[:, 0:CH],
            start=True, stop=True), waits=[sb_, t_rdy], inc="mmS")
        co1 = p.emit("vector", lambda e: e.tensor_tensor(
            out=o_sb[:, 0:CH], in0=o_sb[:, 0:CH], in1=ps_c[:, 0:CH],
            op=ALU.add), waits=[cm1], inc="dvS")
        cm2 = p.emit("tensor", lambda e: e.matmul(
            ps_c[:, 0:CH], S_in_b[:, :], qh1s[:, :],
            start=True, stop=True), waits=[q1s, co1], inc="mmS")
        co2 = p.emit("vector", lambda e: e.tensor_tensor(
            out=o_sb[:, CH:2 * CH], in0=o_sb[:, CH:2 * CH], in1=ps_c[:, 0:CH],
            op=ALU.add), waits=[cm2], inc="dvS")
        ob_ = p.emit("vector", lambda e: e.tensor_copy(o_b[:, :], o_sb[:, :]),
                     inc="dvS")

        st_done = None
        bc_done = None
        bank_free = [None, None]
        last_xr = [None, None]
        tmp1_dma = None
        prev_s1m = prev_s2m = None
        pstr_mark = None
        strip_mark = {}
        for tbi in range(4):
            for do in range(16):
                bank = do % 2
                p.emit("tensor", lambda e, b=bank, d_=do, t_=tbi: e.matmul(
                    ps_a[b][:, :], wo1_sb[:, d_ * 128:(d_ + 1) * 128],
                    o_b[:, t_ * NB:(t_ + 1) * NB], start=True, stop=True),
                    waits=[ob_, wo1ld, bank_free[bank]], inc="mmS")
                mmk = p.mark("mmS")
                xld = p.dma("sync", xr[bank][:, :],
                            x2[do * 128:(do + 1) * 128, tbi * NB:(tbi + 1) * NB],
                            waits=[last_xr[bank]],
                            inc=("ldx0" if bank == 0 else "ldx1b"))
                sv = p.emit("vector", lambda e, b=bank, d_=do: e.tensor_tensor(
                    out=s_sb[d_][:, :], in0=ps_a[b][:, :], in1=xr[b][:, :],
                    op=ALU.add), waits=[mmk, xld], inc="dvS")
                last_xr[bank] = sv
                bank_free[bank] = sv
                sb2 = p.emit("vector", lambda e, d_=do: e.tensor_copy(
                    sbf_t[:, :], s_sb[d_][:, :]), waits=[prev_s1m], inc="dvS")
                sq_ = p.emit("scalar", lambda e: e.activation(
                    out=sq_t[:, :], in_=sbf_t[:, :], func=AF.Square),
                    waits=[sb2, prev_s2m], inc="evS")
                w_st1 = [sb2, ones_ready]
                if do == 0:
                    w_st1 += [st_done]
                prev_s1m = p.emit("tensor", lambda e, d_=do: e.matmul(
                    ps_s1[:, :], onescol[:, :], sbf_t[:, :],
                    start=(d_ == 0), stop=(d_ == 15)), waits=w_st1, inc="mmS")
                w_st2 = [sq_]
                if do == 0:
                    w_st2 += [st_done]
                prev_s2m = p.emit("tensor", lambda e, d_=do: e.matmul(
                    ps_s2[:, :], onescol[:, :], sq_t[:, :],
                    start=(d_ == 0), stop=(d_ == 15)), waits=w_st2, inc="mmS")
            stats_done = p.mark("mmS")

            def ln_block(nfeat, eps, wait0):
                m1 = p.emit("vector", lambda e: e.tensor_scalar(
                    out=mu[:, :], in0=ps_s1[:, :], scalar1=float(1.0 / nfeat),
                    scalar2=None, op0=ALU.mult), waits=[wait0], inc="dvS")
                m2 = p.emit("scalar", lambda e: e.activation(
                    out=msq[:, :], in_=mu[:, :], func=AF.Square),
                    waits=[m1], inc="evS")
                v1 = p.emit("vector", lambda e: e.tensor_scalar(
                    out=varr[:, :], in0=ps_s2[:, :], scalar1=float(1.0 / nfeat),
                    scalar2=None, op0=ALU.mult), inc="dvS")
                v2 = p.emit("vector", lambda e: e.tensor_tensor(
                    out=varr[:, :], in0=varr[:, :], in1=msq[:, :],
                    op=ALU.subtract), waits=[m2], inc="dvS")
                l1 = p.emit("scalar", lambda e, e_=eps: e.activation(
                    out=rstd[:, :], in_=varr[:, :], func=AF.Ln,
                    bias=np.float32(e_)), waits=[v2], inc="evS")
                l2 = p.emit("scalar", lambda e: e.activation(
                    out=rstd[:, :], in_=rstd[:, :], func=AF.Exp,
                    scale=np.float32(-0.5)), inc="evS")
                a1 = p.emit("vector", lambda e: e.tensor_copy(A_bf[:, :],
                                                              rstd[:, :]),
                            waits=[l2], inc="dvS")
                b1 = p.emit("vector", lambda e: e.scalar_tensor_tensor(
                    out=B_bf[:, :], in0=mu[:, :], scalar=float(-1.0),
                    in1=rstd[:, :], op0=ALU.mult, op1=ALU.mult), inc="dvS")
                p.emit("tensor", lambda e: e.matmul(
                    ps_b1[:, :], onesrow[:, :], A_bf[:, :], start=True,
                    stop=True), waits=[a1, bc_done], inc="mmS")
                cA = p.mark("mmS")
                p.emit("tensor", lambda e: e.matmul(
                    ps_b2[:, :], onesrow[:, :], B_bf[:, :], start=True,
                    stop=True), waits=[b1], inc="mmS")
                cB = p.mark("mmS")
                return v1, cA, cB

            v1, cA, cB = ln_block(2048.0, 1e-5, stats_done)
            st_done = v1
            u2m = None
            for do in range(16):
                u1 = p.emit("vector", lambda e, d_=do: e.tensor_tensor(
                    out=tmp1[:, :], in0=s_sb[d_][:, :], in1=ps_b1[:, :],
                    op=ALU.mult), waits=[cA, tmp1_dma], inc="dvS")
                u2 = p.emit("vector", lambda e: e.tensor_tensor(
                    out=tmp1[:, :], in0=tmp1[:, :], in1=ps_b2[:, :],
                    op=ALU.add), waits=[cB], inc="dvS")
                u2m = u2
                x3d = p.dma("sync", x3buf[do * 128:(do + 1) * 128,
                                          tbi * NB:(tbi + 1) * NB],
                            tmp1[:, :], waits=[u2], inc="wx3")
                tmp1_dma = x3d
                p.emit("vector", lambda e, d_=do: e.tensor_copy(
                    A[:, d_ * NB:(d_ + 1) * NB], tmp1[:, :]),
                    waits=[strip_mark.get(("fcmm", tbi - 1, 39))], inc="dvS")
            bc_done = u2m
            x3_dma = p.mark("wx3")
            a_rdy = p.mark("dvS")
            prev_s1m = prev_s2m = None
            for ot in range(40):
                buf = ot % 2
                ld = p.dma("sync",
                           tstr[buf][:, :].rearrange("p (a b) -> p a b", a=16),
                           t1fc[ot * 2048:(ot + 1) * 2048, :].rearrange(
                               "(a p) b -> p a b", p=128),
                           waits=[strip_mark.get(("fcmm", tbi, ot - 2)),
                                  strip_mark.get(("fcmm", tbi - 1,
                                                  38 + (ot % 2)))
                                  if (tbi > 0 and ot < 2) else None,
                                  tern_ready if (tbi == 0 and ot < 2) else None,
                                  tern_ready2 if (tbi == 0 and ot < 2)
                                  else None],
                           inc=f"ldstr{buf}")
                for dm in range(16):
                    w1 = []
                    if dm == 0:
                        w1 = [ld, a_rdy, bank_free[buf]]
                    p.emit("tensor", lambda e, b=buf, d_=dm: e.matmul(
                        ps_a[b][:, :], tstr[b][:, d_ * 128:(d_ + 1) * 128],
                        A[:, d_ * NB:(d_ + 1) * NB],
                        start=(d_ == 0), stop=(d_ == 15)),
                        waits=w1, inc="mmS" if dm == 15 else None)
                mmk = p.mark("mmS")
                strip_mark[("fcmm", tbi, ot)] = mmk
                re_ = p.emit("scalar", lambda e, b=buf: e.activation(
                    out=rt_t[:, :], in_=ps_a[b][:, :], func=AF.Relu,
                    scale=gamb[:, :]), waits=[mmk], inc="evS")
                bank_free[buf] = re_
                h1 = p.emit("scalar", lambda e, o_=ot: e.activation(
                    out=Bh[:, o_ * NB:(o_ + 1) * NB], in_=rt_t[:, :],
                    func=AF.Square), inc="evS")
            h1_all = p.mark("evS")
            prev_s1m = prev_s2m = None
            for do in range(16):
                bank = do % 2
                qlds = []
                for qq in range(4):
                    qlds.append(p.dma(
                        "sync",
                        pstr[:, qq * 1280:(qq + 1) * 1280].rearrange(
                            "p (a b) -> p a b", a=10),
                        t1proj[do * 5120 + qq * 1280:
                               do * 5120 + (qq + 1) * 1280, :].rearrange(
                            "(a p) b -> p a b", p=128),
                        waits=[pstr_mark.get(qq) if isinstance(pstr_mark, dict)
                               else pstr_mark,
                               tern_ready if (tbi == 0 and do == 0)
                               else None,
                               tern_ready2 if (tbi == 0 and do == 0)
                               else None],
                        inc=f"ldpr{qq}"))
                if not isinstance(pstr_mark, dict):
                    pstr_mark = {}
                for ht in range(40):
                    w1 = []
                    if ht % 10 == 0:
                        w1 = [qlds[ht // 10]]
                    if ht == 0:
                        w1 += [bank_free[bank], h1_all]
                    p.emit("tensor", lambda e, b=bank, h_=ht: e.matmul(
                        ps_a[b][:, :], pstr[:, h_ * 128:(h_ + 1) * 128],
                        Bh[:, h_ * NB:(h_ + 1) * NB],
                        start=(h_ == 0), stop=(h_ == 39)),
                        waits=w1, inc="mmS")
                    if ht % 10 == 9:
                        pstr_mark[ht // 10] = p.mark("mmS")
                mmk = p.mark("mmS")
                x3ld = p.dma("sync", xr[bank][:, :],
                             x3buf[do * 128:(do + 1) * 128,
                                   tbi * NB:(tbi + 1) * NB],
                             waits=[last_xr[bank],
                                    x3_dma if do < 2 else None],
                             inc=("ldx0" if bank == 0 else "ldx1b"))
                sv = p.emit("vector", lambda e, b=bank, d_=do:
                            e.scalar_tensor_tensor(
                                out=s_sb[d_][:, :], in0=ps_a[b][:, :],
                                scalar=gamb2[:, :], in1=xr[b][:, :],
                                op0=ALU.mult, op1=ALU.add),
                            waits=[mmk, x3ld], inc="dvS")
                last_xr[bank] = sv
                bank_free[bank] = sv
                sb2 = p.emit("vector", lambda e, d_=do: e.tensor_copy(
                    sbf_t[:, :], s_sb[d_][:, :]), waits=[prev_s1m], inc="dvS")
                sq_ = p.emit("scalar", lambda e: e.activation(
                    out=sq_t[:, :], in_=sbf_t[:, :], func=AF.Square),
                    waits=[sb2, prev_s2m], inc="evS")
                w_st1 = [sb2]
                if do == 0:
                    w_st1 += [st_done]
                prev_s1m = p.emit("tensor", lambda e, d_=do: e.matmul(
                    ps_s1[:, :], onescol[:, :], sbf_t[:, :],
                    start=(d_ == 0), stop=(d_ == 15)), waits=w_st1, inc="mmS")
                w_st2 = [sq_]
                if do == 0:
                    w_st2 += [st_done]
                prev_s2m = p.emit("tensor", lambda e, d_=do: e.matmul(
                    ps_s2[:, :], onescol[:, :], sq_t[:, :],
                    start=(d_ == 0), stop=(d_ == 15)), waits=w_st2, inc="mmS")
            stats_done = p.mark("mmS")
            v1, cA, cB = ln_block(2048.0, 1e-5, stats_done)
            st_done = v1
            u2m = None
            for do in range(16):
                u1 = p.emit("vector", lambda e, d_=do: e.tensor_tensor(
                    out=tmp1[:, :], in0=s_sb[d_][:, :], in1=ps_b1[:, :],
                    op=ALU.mult), waits=[cA, tmp1_dma], inc="dvS")
                u2 = p.emit("vector", lambda e: e.tensor_tensor(
                    out=tmp1[:, :], in0=tmp1[:, :], in1=ps_b2[:, :],
                    op=ALU.add), waits=[cB], inc="dvS")
                u2m = u2
                od = p.dma("sync", outx[do * 128:(do + 1) * 128,
                                        tbi * NB:(tbi + 1) * NB],
                           tmp1[:, :], waits=[u2], inc="st0")
                tmp1_dma = od
            bc_done = u2m
        p.emit("sync", lambda e: e.nop(), waits=[p.mark("st0")])
        if mega is None:
            p.realize(nc, sems)
        else:
            mega["sems"].update(sems)
    return nc


def _launch5_inputs(inputs, x2_full, olocal_full, qhat_all, summ_all):
    w1fcT = np.ascontiguousarray(np.asarray(inputs["L1_ffn_w_fc"]).T.astype(f32))
    w1prT = np.ascontiguousarray(np.asarray(inputs["L1_ffn_w_proj"]).T.astype(f32))
    w1fc_t = _tile_lhsT(w1fcT, 2048, 5120)
    w1proj_t = _tile_lhsT(w1prT, 5120, 2048)
    Wo1 = np.ascontiguousarray(np.asarray(inputs["L1_attn_Wo"]).astype(f32))
    summ_stack = np.ascontiguousarray(summ_all.reshape(8 * 64, 130))
    maps = []
    for c in range(NCORE):
        sl = slice(c * TC, (c + 1) * TC)
        ltm = np.zeros((64, 8), dtype=f32)
        ltm[:, :c] = 1.0
        maps.append({
            "olocal": np.ascontiguousarray(olocal_full[:, sl]),
            "qhat": np.ascontiguousarray(qhat_all[c]),
            "summ_all": summ_stack,
            "own_sum": np.ascontiguousarray(summ_all[c]),
            "ltm": ltm,
            "x2": np.ascontiguousarray(x2_full[:, sl]),
            "Wo1": Wo1, "w1fc_t": w1fc_t, "w1proj_t": w1proj_t,
        })
    return maps


def run_launch5(inputs, x2_full, olocal_full, qhat_all, summ_all):
    nc = build_launch5()
    maps = _launch5_inputs(inputs, x2_full, olocal_full, qhat_all, summ_all)
    res = run_bass_kernel_spmd(nc, maps, core_ids=list(range(NCORE)))
    return np.concatenate([res.results[c]["outx"] for c in range(NCORE)], axis=1)


def build_phaseA(mega=None):
    if mega is None:
        nc, io, ns = bass.Bass("TRN2", target_bir_lowering=False), None, ""
        p = Plan()
    else:
        nc, p, io, ns = mega["nc"], mega["p"], mega["io"], mega["ns"]
    p.set_ns(ns)
    xTf = _dram(nc, io, ns, "xTf", [D, S], F32, "ExternalInput")
    WS, BS = {}, {}
    for w in "qkvg":
        WS[w] = _dram(nc, io, ns, f"W{w}S", [D, 256], F32, "ExternalInput")
        BS[w] = _dram(nc, io, ns, f"b{w}S", [128, 2], F32, "ExternalInput")
    tabs = {t: _dram(nc, io, ns, t, [128, S], F32, "ExternalInput")
            for t in ("cosq", "sinq", "cosk", "sink")}
    identb = _dram(nc, io, ns, "identA", [128, 128], BF16, "ExternalInput")
    qrh = _dram(nc, io, ns, "qrh", [256, S], BF16, "ExternalOutput")
    krh = _dram(nc, io, ns, "krh", [256, S], BF16, "ExternalOutput")
    vth = _dram(nc, io, ns, "vth", [256, S], BF16, "ExternalOutput")
    gh = _dram(nc, io, ns, "gh", [256, S], BF16, "ExternalOutput")
    outmap = {"q": qrh, "k": krh, "g": gh}

    NB = 512
    with ExitStack() as ctx:
        e = ctx.enter_context
        xb = e(nc.sbuf_tensor(ns + "xb", [128, 2 * 16 * NB], BF16))
        wb = e(nc.sbuf_tensor(ns + "wb", [128, 4 * 16 * 256], BF16))
        bb = e(nc.sbuf_tensor(ns + "bb", [128, 8], F32))
        tb_ = {t: [e(nc.sbuf_tensor(f"{ns}tb_{t}{i}", [128, NB], F32))
                   for i in range(2)] for t in tabs}
        id_sb = e(nc.sbuf_tensor(ns + "id_sb", [128, 128], BF16))
        qa = e(nc.sbuf_tensor(ns + "qa", [128, NB], F32))
        qs = e(nc.sbuf_tensor(ns + "qs", [128, NB], F32))
        t1 = e(nc.sbuf_tensor(ns + "t1", [128, NB], F32))
        t2 = e(nc.sbuf_tensor(ns + "t2", [128, NB], F32))
        vt_s = e(nc.sbuf_tensor(ns + "vt_s", [128, NB], BF16))
        ob = [e(nc.sbuf_tensor(f"{ns}ob{i}", [128, NB], BF16)) for i in range(4)]
        ps = [e(nc.psum_tensor(f"{ns}ps{i}", [128, NB], F32)) for i in range(2)]
        ps_tr = e(nc.psum_tensor(ns + "ps_tr", [128, NB], BF16))
        sems = _fresh(mega["sem_ctx"] if mega else ctx, nc,
                      [ns + s_ for s_ in ("ldx0", "ldx1", "ldw", "ldt", "ldt0",
                                          "ldt1", "mmS", "dvS", "evS", "plS",
                                          "st0", "st1", "st2", "st3")])

        wld = None
        for wi, w in enumerate("qkvg"):
            wld = p.dma("gpsimd",
                        wb[:, wi * 4096:(wi + 1) * 4096].rearrange(
                            "p (a b) -> p a b", a=16),
                        WS[w].rearrange("(a p) b -> p a b", p=128), inc="ldw")
            p.dma("sync", bb[:, wi * 2:(wi + 1) * 2], BS[w][:, :], inc="ldt")
        p.dma("sync", id_sb[:, :], identb[:, :], inc="ldt")
        consts_rdy = p.mark("ldt")

        pairswap = []
        for i in range(0, 32, 2):
            pairswap += [i + 1, i]

        last_bank_free = [None, None]
        last_ob_dma = [None, None, None, None]
        last_tab_use = [None, None]
        x_consumed = [None, None]
        obi = 0
        for tbi in range(32):
            par = tbi % 2
            xoff = par * 16 * NB
            xlds = []
            for dm in range(16):
                xlds.append(p.dma("gpsimd",
                                  xb[:, xoff + dm * NB:xoff + (dm + 1) * NB],
                                  xTf[dm * 128:(dm + 1) * 128,
                                      tbi * NB:(tbi + 1) * NB],
                                  waits=[x_consumed[par]] if dm == 0 else [],
                                  inc=f"ldx{par}"))
            x_rdy = xlds[-1]
            tld = None
            for t in tabs:
                tld = p.dma("sync", tb_[t][par][:, :],
                            tabs[t][:, tbi * NB:(tbi + 1) * NB],
                            waits=[last_tab_use[par]] if t == "cosq" else [],
                            inc=f"ldt{par}")
            tabs_rdy = tld
            for gi, (w, h) in enumerate(
                    [(w, h) for w in "qkvg" for h in range(2)]):
                wi = "qkvg".index(w)
                bank = gi % 2
                for dm in range(16):
                    w1 = []
                    if dm == 0:
                        w1 = [x_rdy, wld, last_bank_free[bank]]
                    p.emit("tensor",
                           lambda e, b=bank, d_=dm, w_=wi, h_=h, xo=xoff:
                           e.matmul(
                               ps[b][:, :],
                               wb[:, (w_ * 16 + d_) * 256 + h_ * 128:
                                  (w_ * 16 + d_) * 256 + h_ * 128 + 128],
                               xb[:, xo + d_ * NB:xo + (d_ + 1) * NB],
                               start=(d_ == 0), stop=(d_ == 15)),
                           waits=w1, inc="mmS" if dm == 15 else None)
                mm_done = p.mark("mmS")
                if gi == 7:
                    x_consumed[par] = mm_done
                if w in ("q", "k"):
                    ct, st_ = ("cosq", "sinq") if w == "q" else ("cosk", "sink")
                    bf_ = p.emit("vector",
                                 lambda e, b=bank, w_=wi, h_=h: e.tensor_scalar(
                                     out=qa[:, :], in0=ps[b][:, :],
                                     scalar1=bb[:, w_ * 2 + h_:w_ * 2 + h_ + 1],
                                     scalar2=None, op0=ALU.add),
                                 waits=[mm_done, tabs_rdy, consts_rdy],
                                 inc="dvS")
                    last_bank_free[bank] = bf_
                    p.emit("vector", lambda e: e.stream_shuffle(
                        qs[:, :], qa[:, :], pairswap))
                    p.emit("vector", lambda e, c=ct, pr=par: e.tensor_tensor(
                        out=t1[:, :], in0=qa[:, :], in1=tb_[c][pr][:, :],
                        op=ALU.mult))
                    p.emit("vector", lambda e, s_=st_, pr=par: e.tensor_tensor(
                        out=t2[:, :], in0=qs[:, :], in1=tb_[s_][pr][:, :],
                        op=ALU.mult))
                    rdy = p.emit("vector",
                                 lambda e, o=obi: e.tensor_tensor(
                                     out=ob[o][:, :], in0=t1[:, :],
                                     in1=t2[:, :], op=ALU.add),
                                 waits=[last_ob_dma[obi]], inc="dvS")
                    last_tab_use[par] = rdy
                elif w == "g":
                    rdy = p.emit("scalar",
                                 lambda e, b=bank, o=obi, h_=h: e.activation(
                                     out=ob[o][:, :], in_=ps[b][:, :],
                                     func=AF.Silu,
                                     bias=bb[:, 6 + h_:7 + h_]),
                                 waits=[mm_done, consts_rdy,
                                        last_ob_dma[obi]], inc="evS")
                    last_bank_free[bank] = rdy
                else:
                    vv = p.emit("vector",
                                lambda e, b=bank, h_=h: e.tensor_scalar(
                                    out=vt_s[:, :], in0=ps[b][:, :],
                                    scalar1=bb[:, 4 + h_:5 + h_], scalar2=None,
                                    op0=ALU.add),
                                waits=[mm_done, consts_rdy], inc="dvS")
                    last_bank_free[bank] = vv
                    trm = None
                    for i4 in range(4):
                        trm = p.emit("tensor",
                                     lambda e, i_=i4: e.transpose(
                                         ps_tr[:, i_ * 128:(i_ + 1) * 128],
                                         vt_s[:, i_ * 128:(i_ + 1) * 128],
                                         id_sb[:, :]),
                                     waits=[vv] if i4 == 0 else [],
                                     inc="mmS")
                    rdy = p.emit("vector",
                                 lambda e, o=obi: e.tensor_copy(
                                     ob[o][:, :], ps_tr[:, :]),
                                 waits=[trm, last_ob_dma[obi]], inc="dvS")
                dst = vth if w == "v" else outmap[w]
                st2_ = p.dma("sync", dst[h * 128:(h + 1) * 128,
                                         tbi * NB:(tbi + 1) * NB],
                             ob[obi][:, :], waits=[rdy], inc=f"st{obi}")
                last_ob_dma[obi] = st2_
                obi = (obi + 1) % 4
        p.emit("sync", lambda e: e.nop(),
               waits=[p.mark(f"st{i}") for i in range(4) if p.has(f"st{i}")])
        if mega is None:
            p.realize(nc, sems)
        else:
            mega["sems"].update(sems)
    return nc


def _phaseA_inputs(inputs):
    global _ROPE_CACHE
    if _ROPE_CACHE is None:
        _ROPE_CACHE = _rope_tables()
    cosq, sinq, cosk, sink = _ROPE_CACHE
    x = np.asarray(inputs["L0_attn_x"])[0]
    xTfull = np.ascontiguousarray(x.T)
    idn = np.ascontiguousarray(np.eye(128, dtype=f32).astype(BF))
    maps = []
    for c in range(NCORE):
        cols = slice(256 * c, 256 * c + 256)
        m = {"xTf": xTfull, "cosq": cosq, "sinq": sinq, "cosk": cosk,
             "sink": sink, "identA": idn}
        for w, nm in (("q", "Wq"), ("k", "Wk"), ("v", "Wv"), ("g", "Wg")):
            m[f"W{w}S"] = np.ascontiguousarray(
                np.asarray(inputs[f"L0_attn_{nm}"]).astype(f32)[:, cols])
            m[f"b{w}S"] = np.ascontiguousarray(
                np.asarray(inputs[f"L0_attn_b{w}"]).astype(f32)[cols]
                .reshape(2, 128).T)
        maps.append(m)
    return maps




def emit_meanpair(p, nc, bufs, wt_fc, wt_proj, nch):
    wstr = bufs["wstr"]
    red1 = bufs["red1"]
    wacc, wacc2 = bufs["wacc"], bufs["wacc2"]
    prev_pair_done = bufs.get("prev_pair_done")
    macc = {}
    chain = [bufs.get("prev_pair_done"), bufs.get("prev_pair_done")]
    for wi, (wt, acc) in enumerate(((wt_fc, wacc), (wt_proj, wacc2))):
        for i in range(nch):
            buf = i % 2
            srcv = wt[i * 2048:(i + 1) * 2048, :].rearrange(
                "(a p) b -> p a b", p=128)
            ld = p.dma("gpsimd", wstr[buf][:, :].rearrange(
                "p (a b) -> p a b", a=16), srcv,
                waits=[chain[buf]], inc=f"ldstr{buf}")
            rd = p.emit("vector", lambda e, b_=buf: e.tensor_reduce(
                out=red1[:, :], in_=wstr[b_][:, :], axis=AX.X,
                op=ALU.add, apply_absolute_value=True), waits=[ld])
            chain[buf] = rd
            ac = p.emit("vector", lambda e, a_=acc, first=(i == 0): (
                e.tensor_copy(a_[:, :], red1[:, :]) if first else
                e.tensor_tensor(out=a_[:, :], in0=a_[:, :], in1=red1[:, :],
                                op=ALU.add)))
            macc[(wi, i)] = ac
    return macc


def emit_thrpair(p, nc, bufs, denom_fc, denom_proj, mode, macc,
                 chain=None):
    nch = max(k[1] for k in macc) + 1
    onescol_f, onesrow_f = bufs["onescol_f"], bufs["onesrow_f"]
    red1, sc1 = bufs["red1"], bufs["sc1"]
    gamb, thrb, gamb2, thrb2 = bufs["gamb"], bufs["thrb"], bufs["gamb2"], bufs["thrb2"]
    ps_small, ps_big = bufs["ps_small"], bufs["ps_big"]
    wacc, wacc2 = bufs["wacc"], bufs["wacc2"]
    ones_ready = bufs["ones_ready"]

    def scalar_reduce(acc, gbuf, tbuf, denom, col, chain):
        t = p.emit("tensor", lambda e, a_=acc, c_=col: e.matmul(
            ps_small[0:1, c_:c_ + 1], onescol_f[:, :], a_[:, :],
            start=True, stop=True),
            waits=[macc[(0, nch - 1)], macc[(1, nch - 1)], ones_ready,
                   chain],
            inc="mmS")
        if mode == "bitlinear":
            g1 = p.emit("vector", lambda e, c_=col: e.tensor_scalar(
                out=sc1[0:1, c_:c_ + 1], in0=ps_small[0:1, c_:c_ + 1],
                scalar1=float(1.0 / denom), scalar2=None, op0=ALU.mult),
                waits=[t])
            t2_ = p.emit("vector", lambda e, c_=col: e.tensor_scalar(
                out=sc1[0:1, c_ + 1:c_ + 2], in0=sc1[0:1, c_:c_ + 1],
                scalar1=float(1e-6), scalar2=float(0.5), op0=ALU.add,
                op1=ALU.mult))
        else:
            g1 = p.emit("vector", lambda e, c_=col: e.tensor_scalar(
                out=sc1[0:1, c_:c_ + 1], in0=ps_small[0:1, c_:c_ + 1],
                scalar1=float(1.0 / denom), scalar2=float(1e-5),
                op0=ALU.mult, op1=ALU.max), waits=[t])
            t2_ = p.emit("vector", lambda e, c_=col: e.tensor_scalar(
                out=sc1[0:1, c_ + 1:c_ + 2], in0=sc1[0:1, c_:c_ + 1],
                scalar1=float(0.7), scalar2=None, op0=ALU.mult))
        bg = p.emit("tensor", lambda e, c_=col: e.matmul(
            ps_big[:, c_:c_ + 2], onesrow_f[:, :], sc1[0:1, c_:c_ + 2],
            start=True, stop=True), waits=[t2_], inc="mmS")
        c1 = p.emit("scalar", lambda e, g_=gbuf, c_=col: e.activation(
            out=g_[:, :], in_=ps_big[:, c_:c_ + 1], func=AF.Copy, scale=1.0),
            waits=[bg], inc="evS")
        c2 = p.emit("scalar", lambda e, t_=tbuf, c_=col: e.activation(
            out=t_[:, :], in_=ps_big[:, c_ + 1:c_ + 2], func=AF.Copy,
            scale=1.0), inc="evS")
        return (g1, c2)
    thr_ready = scalar_reduce(wacc, gamb, thrb, denom_fc, 0, chain)
    thr2_ready = scalar_reduce(wacc2, gamb2, thrb2, denom_proj, 2,
                               thr_ready[1])
    return (thr_ready, thr2_ready)


def emit_ternonly(p, nc, bufs, wt_fc, wt_proj, tfc, tproj, nch, macc, thrs,
                  extra_wait=None, prev=None):
    wstr, tstr = bufs["wstr"], bufs["tstr"]
    aa_t, msk_t, sgn_t = bufs["aa"], bufs["msk"], bufs["sgn"]
    gamb, thrb, gamb2, thrb2 = (bufs["gamb"], bufs["thrb"], bufs["gamb2"],
                                bufs["thrb2"])
    thr_ready, thr2_ready = thrs
    tern_mark = {}
    if prev is not None:
        tern_mark[("sgn", 0, -2)] = prev["last_sgn0"]
        tern_mark[("sgn", 0, -1)] = prev["last_sgn1"]
        tern_mark[("st", -1, nch - 2)] = prev["last_st0"]
        tern_mark[("st", -1, nch - 1)] = prev["last_st1"]
    for wi, (wt, dst, tb_buf) in enumerate(((wt_fc, tfc, thrb),
                                            (wt_proj, tproj, thrb2))):
        thr_m = thr_ready[1] if wi == 0 else thr2_ready[1]
        for j in range(nch):
            buf = j % 2
            srcv = wt[j * 2048:(j + 1) * 2048, :].rearrange(
                "(a p) b -> p a b", p=128)
            ld = p.dma("gpsimd", wstr[buf][:, :].rearrange(
                "p (a b) -> p a b", a=16), srcv,
                waits=[tern_mark.get(("sgn", wi, j - 2)),
                       macc[(wi, nch - 1)] if j < 2 else None,
                       macc[(1, nch - 1)] if (wi == 0 and j < 2) else None,
                       extra_wait if j < 2 else None],
                inc=f"ldstr{buf}")
            a_ = p.emit("scalar", lambda e, b=buf: e.activation(
                out=aa_t[:, :], in_=wstr[b][:, :], func=AF.Abs),
                waits=[ld, thr_m], inc="evS")
            m_ = p.emit("vector", lambda e, t_=tb_buf: e.tensor_scalar(
                out=msk_t[:, :], in0=aa_t[:, :], scalar1=t_[:, :],
                scalar2=None, op0=ALU.is_gt), waits=[a_], inc="dvS")
            g_ = p.emit("scalar", lambda e, b=buf: e.activation(
                out=sgn_t[:, :], in_=wstr[b][:, :], func=AF.Sign),
                waits=[m_], inc="evS")
            tern_mark[("sgn", wi, j)] = g_
            t_ = p.emit("gpsimd", lambda e, b=buf: e.tensor_tensor(
                out=tstr[b][:, :], in0=msk_t[:, :], in1=sgn_t[:, :],
                op=ALU.mult),
                waits=[g_, tern_mark.get(("st", wi, j - 2)),
                       tern_mark.get(("st", wi - 1, nch - 2 + j))
                       if j < 2 else None],
                inc="plS")
            stm = p.dma("gpsimd", dst[j * 2048:(j + 1) * 2048, :].rearrange(
                "(a p) b -> p a b", p=128),
                tstr[buf][:, :].rearrange("p (a b) -> p a b", a=16),
                waits=[t_], inc=f"stw{buf}")
            tern_mark[("st", wi, j)] = stm
    return {
        "thr_ready": thr_ready, "thr2_ready": thr2_ready,
        "tern_ready": p.mark("stw0"), "tern_ready2": p.mark("stw1"),
        "gamb": gamb, "thrb": thrb, "gamb2": gamb2, "thrb2": thrb2,
        "pair_done": tern_mark[("st", 1, nch - 1)], "tstr": tstr,
        "last_sgn0": tern_mark[("sgn", 1, nch - 2)],
        "last_sgn1": tern_mark[("sgn", 1, nch - 1)],
        "last_st0": tern_mark[("st", 1, nch - 2 if nch % 2 == 0 else nch - 1)],
        "last_st1": tern_mark[("st", 1, nch - 1 if nch % 2 == 0 else nch - 2)],
    }


U32 = mybir.dt.uint32


def build_mega():
    import os
    tern_mode = os.environ.get("MEGA_TERN", "serial")
    nc = bass.Bass("TRN2", target_bir_lowering=False)
    p = Plan()
    sems = {}
    qrh_t = nc.dram_tensor("qrh", [256, S], BF16)
    krh_t = nc.dram_tensor("krh", [256, S], BF16)
    vth_t = nc.dram_tensor("vth", [256, S], BF16)
    gh_t = nc.dram_tensor("gh", [256, S], BF16)
    yh_t = nc.dram_tensor("yh", [256, S], BF16)
    yall_t = nc.dram_tensor("y_all", [16384, TC], BF16, addr_space="Shared")
    ymine_t = nc.dram_tensor("y_mine", [D, TC], BF16)
    x2_t = nc.dram_tensor("x2buf", [D, TC], F32)
    x2b_t = nc.dram_tensor("x2bbuf", [D, TC], BF16)
    ol_t = nc.dram_tensor("olocalb", [128, TC], F32)
    qh_t = nc.dram_tensor("qhatb", [64, 128], BF16)
    sm_t = nc.dram_tensor("summb", [64, 130], F32)
    small_t = nc.dram_tensor("summ_allb", [512, 130], F32, addr_space="Shared")
    yidx = nc.dram_tensor("yidx", [128, 16], U32, kind="ExternalInput").ap()

    wfc_t = nc.dram_tensor("C_wfc_t", [64 * 16 * 128, 128], F32,
                           kind="ExternalInput").ap()
    wproj_t = nc.dram_tensor("C_wproj_t", [16 * 64 * 128, 128], F32,
                             kind="ExternalInput").ap()
    w1fc_t = nc.dram_tensor("E_w1fc_t", [40 * 16 * 128, 128], F32,
                            kind="ExternalInput").ap()
    w1proj_t = nc.dram_tensor("E_w1proj_t", [16 * 40 * 128, 128], F32,
                              kind="ExternalInput").ap()
    tfc_t = nc.dram_tensor("tfcb", [64 * 16 * 128, 128], BF16)
    tproj_t = nc.dram_tensor("tprojb", [16 * 64 * 128, 128], BF16)
    t1fc_t = nc.dram_tensor("t1fcb", [40 * 16 * 128, 128], BF16)
    t1proj_t = nc.dram_tensor("t1projb", [16 * 40 * 128, 128], BF16)

    with ExitStack() as outer:
        mega = {"nc": nc, "p": p, "sems": sems, "sem_ctx": outer}

        if tern_mode != "off":
            e = outer.enter_context
            tb_ = {
                "wstr": [e(nc.sbuf_tensor(f"T_wstr{i}", [128, 2048], F32))
                         for i in range(2)],
                "tstr": [e(nc.sbuf_tensor(f"T_tstr{i}", [128, 2048], BF16))
                         for i in range(2)],
                "aa": e(nc.sbuf_tensor("T_aa", [128, 2048], F32)),
                "msk": e(nc.sbuf_tensor("T_msk", [128, 2048], BF16)),
                "sgn": e(nc.sbuf_tensor("T_sgn", [128, 2048], BF16)),
                "onescol_f": e(nc.sbuf_tensor("T_onescol_f", [128, 1], F32)),
                "onesrow_f": e(nc.sbuf_tensor("T_onesrow_f", [1, 128], F32)),
                "red1": e(nc.sbuf_tensor("T_red1", [128, 1], F32)),
                "sc1": e(nc.sbuf_tensor("T_sc1", [1, 8], F32)),
            }
            psT = e(nc.psum_tensor("T_ps", [128, 16], F32))
            tb_["ps_small"] = psT[0:1, 8:16]
            tb_["ps_big"] = psT[:, 0:8]
            gtiles = {}
            for nm in ("C_gamb", "C_thrb", "C_gamb2", "C_thrb2",
                       "E_gamb", "E_thrb", "E_gamb2", "E_thrb2",
                       "C_wacc", "C_wacc2", "E_wacc", "E_wacc2"):
                gtiles[nm] = e(nc.sbuf_tensor("T_" + nm, [128, 1], F32))
            sems.update(_fresh(outer, nc, ["T_" + s_ for s_ in (
                "ldstr0", "ldstr1", "stw0", "stw1", "mmS", "dvS", "evS", "plS")]))
            sems.update(_fresh(outer, nc, ["U_" + s_ for s_ in (
                "ldstr0", "ldstr1", "stw0", "stw1", "dvS", "evS", "plS")]))
            bufsC = dict(tb_, gamb=gtiles["C_gamb"], thrb=gtiles["C_thrb"],
                         gamb2=gtiles["C_gamb2"], thrb2=gtiles["C_thrb2"],
                         wacc=gtiles["C_wacc"], wacc2=gtiles["C_wacc2"])
            bufsE = dict(tb_, gamb=gtiles["E_gamb"], thrb=gtiles["E_thrb"],
                         gamb2=gtiles["E_gamb2"], thrb2=gtiles["E_thrb2"],
                         wacc=gtiles["E_wacc"], wacc2=gtiles["E_wacc2"])

        x1ysb = outer.enter_context(nc.sbuf_tensor("X1_ysb", [128, TC], BF16))
        x1idx_sb = outer.enter_context(nc.sbuf_tensor("X1_idx", [128, 16], U32))
        snapA = p.snapshot()
        mega["ns"] = "A_"
        mega["io"] = {"qrh": qrh_t.ap(), "krh": krh_t.ap(),
                      "vth": vth_t.ap(), "gh": gh_t.ap()}
        build_phaseA(mega=mega)
        p.set_ns("G_")
        p.barrier()
        if tern_mode != "off":
            p.set_ns("T_")
            ones2 = p.emit("gpsimd",
                           lambda e: e.memset(tb_["onescol_f"][:, :], 1.0))
            ones2 = p.emit("gpsimd",
                           lambda e: e.memset(tb_["onesrow_f"][:, :], 1.0))
            bufsC["ones_ready"] = ones2
            bufsE["ones_ready"] = ones2
            maccC = emit_meanpair(p, nc, bufsC, wfc_t, wproj_t, 64)
            bufsE["prev_pair_done"] = maccC[(1, 63)]
            maccE = emit_meanpair(p, nc, bufsE, w1fc_t, w1proj_t, 40)
            thrsC = emit_thrpair(p, nc, bufsC, 2048.0 * 8192.0,
                                 8192.0 * 2048.0, "bitlinear", maccC)
            thrsE = emit_thrpair(p, nc, bufsE, 5120.0 * 2048.0,
                                 5120.0 * 2048.0, "maxthr", maccE,
                                 chain=thrsC[1][1])
        snapB = p.snapshot()
        mega["ns"] = "B_"
        mega["io"] = {"qrh": qrh_t.ap(), "krh": krh_t.ap(),
                      "vTh": vth_t.ap(), "gh": gh_t.ap(), "yh": yh_t.ap()}
        build_launch2(mega=mega)
        yh_done = [p.mark("st0"), p.mark("st1")]
        p.set_ns("X1_")
        sems.update(_fresh(outer, nc, ["X1_cc", "X1_ld", "X1_st"]))
        cc = p.emit("gpsimd", lambda e: e.collective_compute(
            "AllGather", ALU.bypass, replica_groups=[list(range(NCORE))],
            ins=[yh_t.ap().opt()], outs=[yall_t.ap().opt()]),
            waits=yh_done, inc="cc", inc_k=1, _dma=True)
        if tern_mode != "off":
            pT2 = Plan()
            pT2.set_ns("U_")
            preC = emit_ternonly(pT2, nc, bufsC, wfc_t, wproj_t, tfc_t.ap(),
                                 tproj_t.ap(), 64, maccC, thrsC,
                                 extra_wait=maccE[(1, 39)])
            preE = emit_ternonly(pT2, nc, bufsE, w1fc_t, w1proj_t, t1fc_t.ap(),
                                 t1proj_t.ap(), 40, maccE, thrsE,
                                 extra_wait=preC["last_sgn1"], prev=preC)
            if tern_mode == "merge":
                p.merge_from(pT2, since=snapB, defer_self={"gpsimd": 4})
            else:
                p.merge_from(pT2, since=p.snapshot())
        p.set_ns("X1_")
        ild = p.dma("sync", x1idx_sb[:, :], yidx[:, :], inc="ld")
        prev_st = None
        for g in range(16):
            gm = p.emit("gpsimd", lambda e, g_=g: e.indirect_dma_start(
                out=x1ysb[:, :], out_offset=None, in_=yall_t.ap(),
                in_offset=bass.IndirectOffsetOnAxis(
                    ap=x1idx_sb[:, g_:g_ + 1], axis=0)),
                waits=[cc, ild, prev_st], inc="ld", inc_k=16, _dma=True)
            prev_st = p.dma("sync",
                            ymine_t.ap()[g * 128:(g + 1) * 128, :],
                            x1ysb[:, :], waits=[gm], inc="st")
        p.set_ns("G_")
        p.barrier()

        mega["ns"] = "C_"
        mega["io"] = {"y_sl": ymine_t.ap(), "x2o": x2_t.ap(),
                      "x2bo": x2b_t.ap(), "wfc_t": wfc_t, "wproj_t": wproj_t,
                      "tfc": tfc_t.ap(), "tproj": tproj_t.ap()}
        if tern_mode != "off":
            mega["pre"] = preC
        build_launch3(mega=mega)
        mega.pop("pre", None)
        p.set_ns("G_")
        p.barrier()

        mega["ns"] = "D_"
        mega["io"] = {"x2b": x2b_t.ap(), "olocal": ol_t.ap(),
                      "qhato": qh_t.ap(), "summo": sm_t.ap()}
        build_launch4(mega=mega)
        p.set_ns("G_")
        p.barrier()

        p.set_ns("X2_")
        sems.update(_fresh(outer, nc, ["X2_cc"]))
        p.emit("gpsimd", lambda e: e.collective_compute(
            "AllGather", ALU.bypass, replica_groups=[list(range(NCORE))],
            ins=[sm_t.ap().opt()], outs=[small_t.ap().opt()]),
            inc="cc", inc_k=1, _dma=True)
        p.set_ns("G_")
        p.barrier()

        mega["ns"] = "E_"
        mega["io"] = {"olocal": ol_t.ap(), "qhat": qh_t.ap(),
                      "summ_all": small_t.ap(), "own_sum": sm_t.ap(),
                      "x2": x2_t.ap(), "w1fc_t": w1fc_t, "w1proj_t": w1proj_t,
                      "t1fc": t1fc_t.ap(), "t1proj": t1proj_t.ap()}
        if tern_mode != "off":
            mega["pre"] = preE
        build_launch5(mega=mega)
        mega.pop("pre", None)

        p.realize(nc, sems)
    return nc


def _mega_inputs(inputs):
    mapsA = _phaseA_inputs(inputs)
    masks = np.ascontiguousarray(_diag_masks())
    x = np.asarray(inputs["L0_attn_x"])[0]
    xTfull = np.ascontiguousarray(x.T)
    Wo = np.ascontiguousarray(np.asarray(inputs["L0_attn_Wo"]).astype(f32))
    bo = _pack_bias(np.asarray(inputs["L0_attn_bo"]))
    wfcT = np.ascontiguousarray(np.asarray(inputs["L0_ffn_w_fc"]).T.astype(f32))
    wprT = np.ascontiguousarray(np.asarray(inputs["L0_ffn_w_proj"]).T.astype(f32))
    wfc_t = _tile_lhsT(wfcT, 2048, 8192)
    wproj_t = _tile_lhsT(wprT, 8192, 2048)
    w1fcT = np.ascontiguousarray(np.asarray(inputs["L1_ffn_w_fc"]).T.astype(f32))
    w1prT = np.ascontiguousarray(np.asarray(inputs["L1_ffn_w_proj"]).T.astype(f32))
    w1fc_t = _tile_lhsT(w1fcT, 2048, 5120)
    w1proj_t = _tile_lhsT(w1prT, 5120, 2048)
    Wo1 = np.ascontiguousarray(np.asarray(inputs["L1_attn_Wo"]).astype(f32))
    cm = np.ascontiguousarray(_causal128())
    idn = np.ascontiguousarray(np.eye(128, dtype=f32).astype(BF))
    maps = []
    for c in range(NCORE):
        sl = slice(c * TC, (c + 1) * TC)
        m = {}
        for k, v in mapsA[c].items():
            m["A_" + k] = v
        qd = np.empty((256, 512), dtype=f32)
        kd = np.empty((256, 512), dtype=f32)
        pscl = np.empty((256, 131), dtype=f32)
        for hi in range(2):
            a, b, cc_ = _decay_tiles(2 * c + hi)
            qd[hi * 128:(hi + 1) * 128] = a
            kd[hi * 128:(hi + 1) * 128] = b
            pscl[hi * 128:(hi + 1) * 128] = cc_
        m.update({"B_qdec": qd, "B_kdec": kd, "B_pairsc": pscl,
                  "B_masks": masks})
        dg = np.arange(2048)
        vrow = (dg // 256) * 2048 + (dg % 256) * 8 + c
        m["yidx"] = np.ascontiguousarray(
            vrow.reshape(16, 128).T.astype(np.uint32))
        m.update({"C_Wo": Wo, "C_bo": bo,
                  "C_xT": np.ascontiguousarray(xTfull[:, sl]),
                  "C_wfc_t": wfc_t, "C_wproj_t": wproj_t})
        dd = {}
        for nm in ("Wq", "Wk", "Wb", "Wd", "Wv"):
            dd["D_" + nm + "1"] = np.ascontiguousarray(
                np.asarray(inputs[f"L1_attn_{nm}"]).astype(f32))
        dd["D_bb1"] = np.ascontiguousarray(
            np.asarray(inputs["L1_attn_bb"]).astype(f32).reshape(64, 1))
        dd["D_bd1"] = np.ascontiguousarray(
            np.asarray(inputs["L1_attn_bd"]).astype(f32).reshape(64, 1))
        dd["D_cmask"] = cm
        dd["D_identb"] = idn
        m.update(dd)
        ltm = np.zeros((64, 8), dtype=f32)
        ltm[:, :c] = 1.0
        m.update({"E_ltm": ltm, "E_Wo1": Wo1, "E_w1fc_t": w1fc_t,
                  "E_w1proj_t": w1proj_t})
        maps.append(m)
    return maps


def kernel_mega(**inputs):
    cores = list(range(NCORE))
    res = run_bass_kernel_spmd(_nc("mega", build_mega), _mega_inputs(inputs),
                               core_ids=cores)
    outx = np.concatenate([res.results[c]["E_outx"] for c in cores], axis=1)
    return np.ascontiguousarray(outx.T).reshape(1, S, D)
